# version 08
# baseline (speedup 1.0000x reference)
import contextlib
import numpy as np
import ml_dtypes
import concourse.bass as bass
import concourse.mybir as mybir
from concourse.bass_utils import run_bass_kernel_spmd
from concourse.alu_op_type import AluOpType as OP

F32 = mybir.dt.float32
BF16 = mybir.dt.bfloat16
AF = mybir.ActivationFunctionType
AX = mybir.AxisListType

B, S, D, H, DH, NE = 32, 2048, 1024, 8, 64, 8
CORES = 8
SPC = B // CORES
NCH, NG, NPAIR, FCH, ICH, DCH = 16, 4, 4, 16, 8, 8
EPS_LN = 1e-5
EPS_RMS = float(np.finfo(np.float32).eps)
GELU_A = 1.702
LIMIT = 7.0
bf16 = ml_dtypes.bfloat16

DBG_STOP = None
DBG_DUMP = True
DBG_NSEQ = None


class _CutDone(Exception):
    pass


ARENA_OFF = {}


def _host_prep(inp):
    f = np.float32
    wq = np.asarray(inp["L0_attn_wq"], f)
    wk = np.asarray(inp["L0_attn_wk"], f)
    wv = np.asarray(inp["L0_attn_wv"], f)
    wo = np.asarray(inp["L0_attn_wo"], f)
    wg = np.asarray(inp["L0_attn_w_gates"], f)
    rmsw = np.asarray(inp["L0_attn_rms_w"], f)
    nrmw = np.asarray(inp["L0_ffn_norm_w"], f)
    gw = np.asarray(inp["L0_ffn_gate_w"], f)
    gb = np.asarray(inp["L0_ffn_gate_b"], f)
    w1 = np.asarray(inp["L0_ffn_mlp1_w"], f)
    b1 = np.asarray(inp["L0_ffn_mlp1_b"], f)
    w2 = np.asarray(inp["L0_ffn_mlp2_w"], f)
    b2 = np.asarray(inp["L0_ffn_mlp2_b"], f)
    tok = np.asarray(inp["L0_attn_tokens"], f)

    def pairstack(w):
        a = np.stack([np.concatenate([w[2 * p], w[2 * p + 1]], axis=1) for p in range(NPAIR)])
        a = a.reshape(NPAIR, DCH, 128, 128).transpose(0, 2, 1, 3)
        return np.ascontiguousarray(a.reshape(NPAIR, 128, DCH * 128))

    wq_h, wk_h, wv_h = pairstack(wq), pairstack(wk), pairstack(wv)
    wo_h = np.ascontiguousarray(
        np.stack([wo[2 * p:2 * p + 2].reshape(128, 1024) for p in range(NPAIR)]))
    wg_h = np.ascontiguousarray(wg.reshape(DCH, 128, H).transpose(1, 0, 2).reshape(128, DCH * H))
    rms_h = np.ascontiguousarray(rmsw.reshape(DCH, 128).T)
    nrm_h = np.ascontiguousarray(nrmw.reshape(DCH, 128).T)
    gw_h = np.ascontiguousarray(gw.T.reshape(DCH, 128, NE).transpose(1, 0, 2).reshape(128, DCH * NE))
    gb_h = np.ascontiguousarray(gb.reshape(NE, 1))

    perm = np.empty(2048, np.int64)
    for fc in range(FCH):
        perm[fc * 128:(fc + 1) * 128] = 256 * (fc // 2) + 2 * np.arange(128) + (fc % 2)
    w1p = w1[:, perm, :]
    a = w1p.transpose(0, 2, 1).reshape(NE, DCH, 128, FCH, 128).transpose(0, 2, 3, 1, 4)
    w1_h = np.ascontiguousarray(a.reshape(NE, 128, FCH * DCH * 128))
    b1_h = np.ascontiguousarray(
        b1[:, perm].reshape(NE, FCH, 128).transpose(2, 0, 1).reshape(128, NE * FCH))
    a = w2.transpose(0, 2, 1).reshape(NE, ICH, 128, 1024).transpose(0, 2, 1, 3)
    w2_h = np.ascontiguousarray(a.reshape(NE, 128, ICH * 1024))
    b2_h = np.ascontiguousarray(b2)

    identb = np.eye(128, dtype=bf16)
    identf = np.eye(128, dtype=np.float32)
    p_ = np.arange(128)[:, None]
    q_ = np.arange(512)[None, :]
    maskt = np.concatenate([(128 * c + p_ <= q_) for c in range(4)], axis=1).astype(bf16)
    psel = np.zeros((8, 512), bf16)
    for p in range(NPAIR):
        psel[2 * p, p * 128:p * 128 + 64] = 1
        psel[2 * p + 1, p * 128 + 64:p * 128 + 128] = 1
    oneh = np.zeros((8, 1024), bf16)
    for e in range(NE):
        oneh[e, e * 128:(e + 1) * 128] = 1
    onescol = np.ones((128, 1), bf16)
    onesrow = np.ones((1, 128), bf16)

    shared = dict(wq=wq_h, wk=wk_h, wv=wv_h, wo=wo_h, wg=wg_h, rmsw=rms_h,
                  nrmw=nrm_h, gw=gw_h, gb=gb_h, w1=w1_h, b1=b1_h, w2=w2_h, b2=b2_h,
                  identb=identb, identf=identf, maskt=maskt, psel=psel, oneh=oneh,
                  onescol=onescol, onesrow=onesrow)
    maps = []
    for c in range(CORES):
        m = dict(shared)
        m["xin"] = np.ascontiguousarray(tok[c * SPC:(c + 1) * SPC])
        maps.append(m)
    return maps


class Sch:
    def __init__(self):
        self.sem, self.cnt, self.lastw = {}, {}, {}

    def add(self, n, h):
        self.sem[n] = h
        self.cnt[n] = 0

    def inc(self, instr, s, n=1):
        instr.then_inc(self.sem[s], n)
        self.cnt[s] += n
        return self.cnt[s]

    def tok(self, s):
        return self.cnt[s]

    def wait(self, eng, en, s, val):
        if not val or val <= 0:
            return
        k = (en, s)
        if self.lastw.get(k, 0) >= val:
            return
        eng.wait_ge(self.sem[s], val)
        self.lastw[k] = val

    def drain_inc(self, eng, s):
        eng.drain()
        i = eng.nop()
        i.then_inc(self.sem[s], 1)
        self.cnt[s] += 1
        return self.cnt[s]


class Res:
    def __init__(self):
        self.w = None
        self.r = {}

    def set_w(self, sem, tok, slack=0):
        self.w = (sem, tok, slack)
        self.r = {}

    def add_r(self, sem, tok):
        if self.r.get(sem, 0) < tok:
            self.r[sem] = tok


def build():
    nc = bass.Bass("TRN2", target_bir_lowering=False)
    dt = nc.dram_tensor
    xin = dt("xin", [SPC, S, D], F32, kind="ExternalInput").ap()
    xout = dt("xout", [SPC, S, D], F32, kind="ExternalOutput").ap()
    d_wq = dt("wq", [NPAIR, 128, 1024], F32, kind="ExternalInput").ap()
    d_wk = dt("wk", [NPAIR, 128, 1024], F32, kind="ExternalInput").ap()
    d_wv = dt("wv", [NPAIR, 128, 1024], F32, kind="ExternalInput").ap()
    d_wo = dt("wo", [NPAIR, 128, 1024], F32, kind="ExternalInput").ap()
    d_wg = dt("wg", [128, 64], F32, kind="ExternalInput").ap()
    d_rms = dt("rmsw", [128, 8], F32, kind="ExternalInput").ap()
    d_nrm = dt("nrmw", [128, 8], F32, kind="ExternalInput").ap()
    d_gw = dt("gw", [128, 64], F32, kind="ExternalInput").ap()
    d_gb = dt("gb", [8, 1], F32, kind="ExternalInput").ap()
    d_w1 = dt("w1", [NE, 128, 16384], F32, kind="ExternalInput").ap()
    d_b1 = dt("b1", [128, 128], F32, kind="ExternalInput").ap()
    d_w2 = dt("w2", [NE, 128, 8192], F32, kind="ExternalInput").ap()
    d_b2 = dt("b2", [8, 1024], F32, kind="ExternalInput").ap()
    d_idb = dt("identb", [128, 128], BF16, kind="ExternalInput").ap()
    d_idf = dt("identf", [128, 128], F32, kind="ExternalInput").ap()
    d_mask = dt("maskt", [128, 2048], BF16, kind="ExternalInput").ap()
    d_psel = dt("psel", [8, 512], BF16, kind="ExternalInput").ap()
    d_oneh = dt("oneh", [8, 1024], BF16, kind="ExternalInput").ap()
    d_onec = dt("onescol", [128, 1], BF16, kind="ExternalInput").ap()
    d_oner = dt("onesrow", [1, 128], BF16, kind="ExternalInput").ap()
    d_zscr = dt("zscr", [4096], F32, kind="Internal").ap()
    d_dbg = dt("dbg", [128, 28800], F32, kind="ExternalOutput").ap() if DBG_STOP else None

    with contextlib.ExitStack() as es:
        sbt = lambda n, sh, d_: es.enter_context(nc.sbuf_tensor(n, sh, d_))
        pst = lambda n, sh, d_: es.enter_context(nc.psum_tensor(n, sh, d_))

        x_sb = sbt("x_sb", [128, NCH * 1024], F32)
        identb = sbt("s_identb", [128, 128], BF16)
        identf = sbt("s_identf", [128, 128], F32)
        maskt = sbt("s_maskt", [128, 2048], BF16)
        psel = sbt("s_psel", [8, 512], BF16)
        oneh = sbt("s_oneh", [8, 1024], BF16)
        onescol = sbt("s_onescol", [128, 1], BF16)
        onesrow = sbt("s_onesrow", [1, 128], BF16)
        rms_sb = sbt("rms_sb", [128, 8], F32)
        nrm_sb = sbt("nrm_sb", [128, 8], F32)
        wg_sb = sbt("wg_sb", [128, 64], BF16)
        gw_sb = sbt("gw_sb", [128, 64], BF16)
        gb_sb = sbt("gb_sb", [8, 1], F32)
        b1_sb = sbt("b1_sb", [128, 128], F32)
        b2_sb = sbt("b2_sb", [8, 1024], BF16)
        b2f = sbt("b2f", [8, 1024], F32)
        sums = sbt("sums", [128, 16], F32)
        sqs = sbt("sqs", [128, 16], F32)
        nm = sbt("nm", [128, 16], F32)
        msq = sbt("msq", [128, 16], F32)
        nv = sbt("nv", [128, 16], F32)
        vpe = sbt("vpe", [128, 16], F32)
        lnv = sbt("lnv", [128, 16], F32)
        rstd = sbt("rstd", [128, 16], F32)
        rsq = sbt("rsq", [128, 16], F32)
        vv = sbt("vv", [128, 16], F32)
        mt2 = sbt("mt2", [128, 16], F32)
        lnw = sbt("lnw", [128, 16], F32)
        alphav = sbt("alphav", [128, 16], F32)
        cc1 = sbt("cc1", [128, 16], F32)
        cc2 = sbt("cc2", [128, 16], F32)
        mcol = sbt("mcol", [1, 16], F32)
        msc = sbt("msc", [1, 4], F32)
        nmb1 = sbt("nmb1", [1, 1], BF16)
        negM = sbt("negM", [128, 1], F32)
        lgtm = sbt("lgtm", [128, 128], F32)
        rm1 = sbt("rm1", [128, 16], F32)
        rm2 = sbt("rm2", [128, 16], F32)
        req1 = sbt("req1", [128, 128], F32)
        rl2 = sbt("rl2", [128, 128], F32)
        req2 = sbt("req2", [128, 128], F32)
        rd12 = sbt("rd12", [128, 16], F32)
        rw1 = sbt("rw1", [128, 16], F32)
        rw2 = sbt("rw2", [128, 16], F32)
        rwa = sbt("rwa", [128, 128], F32)
        rwf = sbt("rwf", [128, 128], F32)
        sqscr = sbt("sqscr", [128, 512], F32)
        dscr = sbt("dscr", [128, 64], F32)

        AR_COLS = 61440
        arena = sbt("arena", [128, AR_COLS], BF16)

        class Arena:
            def __init__(self):
                self.off = 0

            def take(self, cols):
                cols = (cols + 31) & ~31
                assert self.off + cols <= AR_COLS, f"arena overflow {self.off}+{cols}"
                o = self.off
                self.off += cols
                return o

            def b16(self, cols, parts=128, name=None):
                o = self.take(cols)
                if name:
                    ARENA_OFF[name] = (o, cols, 'b16', parts)
                return arena[0:parts, o:o + cols]

            def f32(self, cols, parts=128, name=None):
                o = self.take(2 * cols)
                if name:
                    ARENA_OFF[name] = (o, 2 * cols, 'f32', parts)
                return arena[0:parts, o:o + 2 * cols].bitcast(F32)

        A = Arena()
        tbf = A.b16(4096, name='tbf')
        tT = A.b16(16384, name='tT')
        mark = A.off
        qp = A.b16(2048, name='qp')
        kp = A.b16(2048, name='kp')
        vp = A.b16(2080, name='vp')
        q2 = A.b16(2048, name='q2')
        wqkv = A.b16(1024, name='wqkv')
        woall = A.b16(4096, name='woall')
        gTb = A.b16(2048, parts=8, name='gTb')
        outT = A.b16(8192, name='outT')
        Ebufs = [A.b16(512, name=f'E{_i}') for _i in range(6)]
        zst = A.b16(4096, parts=1, name='zst')
        zbuf = A.b16(2048, parts=8, name='zbuf')
        crec = A.f32(512, parts=8, name='crec')
        cgb = A.b16(2048, parts=8, name='cgb')
        attn_end = A.off
        A.off = mark
        w1b = A.b16(16384, name='w1b')
        w2b = A.b16(8192, name='w2b')
        actb = [A.b16(4096, name='act0'), tbf]
        wbr = [A.b16(512, name='wbr0'), A.b16(512, name='wbr1')]
        g1p = [A.f32(512, name='g1a'), A.f32(512, name='g1b')]
        l1p = [A.f32(512, name='l1a'), A.f32(512, name='l1b')]
        lcb2 = [A.f32(512, name='lcba'), A.f32(512, name='lcbb')]
        p1 = A.f32(512, name='p1')
        a0 = A.f32(512, name='a0')
        lgt = A.f32(512, parts=8, name='lgt')
        wTb = A.b16(2048, parts=8, name='wTb')
        ffn_end = A.off
        wqkvB = arena[0:128, ARENA_OFF['lgt'][0]:ARENA_OFF['lgt'][0] + 1024]
        wbuf = [wqkv, wqkvB]
        wbufn = ["wqkv", "wqkvB"]
        assert max(attn_end, ffn_end) <= AR_COLS, (attn_end, ffn_end)

        pb = [pst(f"pb{i}", [128, 512], F32) for i in range(7)]
        pbt = pst("pbt", [128, 1024], BF16)

        sch = Sch()
        for n in ["Cs", "Cg", "Cx", "Cq", "Cz1", "Cz2", "Cw1", "Cw2",
                  "Pe", "Ac", "Dv", "Pl", "So"]:
            sch.add(n, es.enter_context(nc.semaphore(n)))
        block = es.enter_context(nc.Block())
        ENG = {"PE": block.tensor, "VE": block.vector, "SE": block.scalar,
               "GE": block.gpsimd, "SY": block.sync}
        SEM = {"PE": "Pe", "VE": "Dv", "SE": "Ac", "GE": "Pl"}

        R = {}
        for name in (["wqkvB", "lgt2"]
                     + ["x", "tbf", "tT", "qp", "kp", "vp", "q2", "wo", "gT", "outT",
                      "zst", "zbuf", "crec", "cgb", "w1", "w2", "act", "g1", "lc",
                      "p1", "a0", "lgt", "wTb", "lgtm", "rwf", "stats", "mh",
                      "negM", "b2", "rtr", "wqkv", "sqscr"]
                     + [f"pb{i}" for i in range(7)] + ["pbt"]
                     + [f"wbr{i}" for i in range(2)]):
            R[name] = Res()

        def sec(en, fn, reads=(), writes=(), war_extra=()):
            sem = SEM.get(en)

            def body(eng):
                for rn in reads:
                    rr = R[rn]
                    if rr.w and rr.w[0] != sem:
                        sch.wait(eng, en, rr.w[0], rr.w[1] + rr.w[2])
                for wn in writes:
                    rr = R[wn]
                    for s_, t_ in rr.r.items():
                        if s_ != sem:
                            sch.wait(eng, en, s_, t_)
                    if rr.w and rr.w[0] != sem:
                        sch.wait(eng, en, rr.w[0], rr.w[1])
                for s_, t_ in war_extra:
                    sch.wait(eng, en, s_, t_)
                out = fn(eng)
                if out is not None and sem is not None:
                    sch.inc(out, sem)
            ENG[en](body)
            tok = sch.tok(sem) if sem else None
            for rn in reads:
                if sem:
                    R[rn].add_r(sem, tok)
            for wn in writes:
                if sem:
                    R[wn].set_w(sem, tok, 1 if en in ("VE", "SE", "GE") else 0)
            return tok

        def seal(en):
            sem = SEM[en]

            def body(eng):
                sch.drain_inc(eng, sem)
            ENG[en](body)
            t = sch.tok(sem)
            for rr in R.values():
                if rr.w and rr.w[0] == sem and rr.w[2] and rr.w[1] < t:
                    rr.w = (sem, t, 0)
            return t

        def _(s):
            for dst, src in ((identf, d_idf), (rms_sb, d_rms), (nrm_sb, d_nrm),
                             (gb_sb, d_gb), (b1_sb, d_b1), (b2f, d_b2)):
                i = s.dma_start(out=dst[:], in_=src[:, :])
                sch.inc(i, "Cs", 16)
        ENG["SY"](_)
        lx0 = sch.tok("Cs")

        def _(g):
            for dst, src in ((identb, d_idb), (maskt, d_mask), (psel, d_psel),
                             (oneh, d_oneh), (onescol, d_onec), (onesrow, d_oner),
                             (wg_sb, d_wg), (gw_sb, d_gw)):
                i = g.dma_start(out=dst[:], in_=src[:, :])
                sch.inc(i, "Cg", 16)
        ENG["GE"](_)
        lw0 = sch.tok("Cg")

        def _(v):
            v.memset(dscr[:], 0.0)
            return v.tensor_scalar(b2_sb[:], b2f[:], GELU_A, None, OP.mult)
        sec("VE", _, writes=("b2",), war_extra=(("Cs", lx0),))
        for _en, _eng in (("PE", "tensor"), ("VE", "vector"), ("SE", "scalar"), ("GE", "gpsimd")):
            def _w(e, _en=_en):
                sch.wait(e, _en_cur[0], "Cs", lx0)
                sch.wait(e, _en_cur[0], "Cg", lw0)
            _en_cur = [_en]
            ENG[_en](_w)
        seal("VE")

        state = {"store": 0, "w1_last": 0, "w2_last": 0}

        def cut(label):
            if DBG_STOP == label:
                raise _CutDone()

        nseq = DBG_NSEQ if DBG_NSEQ is not None else SPC
        for s_i in range(nseq):
          try:
            if s_i == 0:
                def _(s, s_i=s_i):
                    sch.wait(s, "SY", "So", state["store"])
                    for s_, t_ in R["x"].r.items():
                        sch.wait(s, "SY", s_, t_)
                    i = s.dma_start(out=x_sb[:].rearrange("p (c d) -> p c d", c=NCH),
                                    in_=xin[s_i].rearrange("(c p) d -> p c d", p=128))
                    sch.inc(i, "Cx", 16)
                ENG["SY"](_)
                R["x"].set_w("Cx", sch.tok("Cx"))

            cut("load")
            def emit_ln(is_ln2, precomputed=False):
                if not precomputed:
                    def _(sc):
                        i = None
                        for c in range(NCH):
                            xc = x_sb[:, c * 1024:(c + 1) * 1024]
                            sc.activation(sqscr[:], xc[:, 0:512], AF.Square,
                                          accum_out=sqs[:, c:c + 1])
                            i = sc.activation(sqscr[:], xc[:, 512:1024], AF.Square,
                                              accum_out=msq[:, c:c + 1])
                        return i
                    sec("SE", _, reads=("x",), writes=("sqscr", "stats"))
                    sq_t = seal("SE")

                    def _(v):
                        i = None
                        for c in range(NCH):
                            i = v.reduce_sum(sums[:, c:c + 1],
                                             x_sb[:, c * 1024:(c + 1) * 1024], axis=AX.X)
                        return i
                    sec("VE", _, reads=("x",), writes=("stats",))
                    seal("VE")
                else:
                    sq_t = seal("SE")

                def _(v):
                    v.tensor_tensor(vv[:], sqs[:], msq[:], OP.add)
                    v.tensor_scalar(nm[:], sums[:], -1.0 / D, None, OP.mult)
                    v.drain()
                    v.tensor_scalar(msq[:], vv[:], 1.0 / D, None, OP.mult)
                    v.tensor_tensor(vpe[:], nm[:], nm[:], OP.mult)
                    v.drain()
                    v.tensor_tensor(nv[:], vpe[:], msq[:], OP.subtract)
                    v.drain()
                    return v.tensor_scalar(vpe[:], nv[:], -1.0, EPS_LN, OP.mult, OP.add)
                sec("VE", _, reads=("stats",), writes=("stats",), war_extra=(("Ac", sq_t),))
                seal("VE")

                def _(sc):
                    sc.activation(lnv[:], vpe[:], AF.Ln)
                    sc.drain()
                    sc.activation(rstd[:], lnv[:], AF.Exp, scale=-0.5)
                    return sc.activation(rsq[:], lnv[:], AF.Exp, scale=-1.0)
                sec("SE", _, reads=("stats",), writes=("stats",))
                seal("SE")

                def _(v):
                    v.tensor_scalar(vv[:], nv[:], -1.0, None, OP.mult)
                    v.drain()
                    v.tensor_tensor(mt2[:], vv[:], rsq[:], OP.mult)
                    v.drain()
                    return v.tensor_scalar(mt2[:], mt2[:], EPS_RMS, None, OP.add)
                sec("VE", _, reads=("stats",), writes=("stats",))
                seal("VE")

                def _(sc):
                    sc.activation(lnw[:], mt2[:], AF.Ln)
                    sc.drain()
                    return sc.activation(vv[:], lnw[:], AF.Exp, scale=-0.5)
                sec("SE", _, reads=("stats",), writes=("stats",))
                seal("SE")

                def _(v):
                    i = v.tensor_tensor(alphav[:], rstd[:], vv[:], OP.mult)
                    if is_ln2:
                        v.tensor_scalar(cc1[:], rstd[:], 1.0, None, OP.add)
                        i = v.tensor_tensor(cc2[:], nm[:], rstd[:], OP.mult)
                    return i
                sec("VE", _, reads=("stats",), writes=("stats",))
                seal("VE")

            def emit_apply_T(wvec_sb, rewrite, war_pe=0):
                for g in range(NG):
                    def _(v, g=g):
                        i = None
                        for k in range(4):
                            c = g * 4 + k
                            i = v.tensor_scalar(tbf[:, k * 1024:(k + 1) * 1024],
                                                x_sb[:, c * 1024:(c + 1) * 1024],
                                                nm[:, c:c + 1], alphav[:, c:c + 1],
                                                OP.add, OP.mult)
                        return i
                    sec("VE", _, reads=("stats", "x"), writes=("tbf",),
                        war_extra=((("Pe", war_pe),) if g == 0 and war_pe else ()))
                    seal("VE")

                    for dcp in range(4):
                        def _(t, g=g, dcp=dcp):
                            i = None
                            for half, dc in ((0, 2 * dcp), (512, 2 * dcp + 1)):
                                for k in range(4):
                                    i = t.transpose(
                                        pbt[:, half + k * 128:half + (k + 1) * 128],
                                        tbf[:, k * 1024 + dc * 128:k * 1024 + (dc + 1) * 128],
                                        identb[:])
                            return i
                        sec("PE", _, reads=("tbf",), writes=("pbt",))

                        def _(v, g=g, dcp=dcp):
                            i = None
                            for half, dc in ((0, 2 * dcp), (512, 2 * dcp + 1)):
                                i = v.tensor_scalar(
                                    tT[:, dc * 2048 + g * 512:dc * 2048 + (g + 1) * 512],
                                    pbt[:, half:half + 512], wvec_sb[:, dc:dc + 1],
                                    None, OP.mult)
                            return i
                        sec("VE", _, reads=("pbt",), writes=("tT",))

                    if rewrite:
                        def _(v, g=g):
                            i = None
                            for k in range(4):
                                c = g * 4 + k
                                i = v.tensor_scalar(x_sb[:, c * 1024:(c + 1) * 1024],
                                                    x_sb[:, c * 1024:(c + 1) * 1024],
                                                    cc1[:, c:c + 1], cc2[:, c:c + 1],
                                                    OP.mult, OP.add)
                            return i
                        sec("VE", _, reads=("stats",), writes=("x",))
                seal("VE")

            emit_ln(is_ln2=False, precomputed=(s_i > 0))
            cut("ln1")
            emit_apply_T(rms_sb, rewrite=False, war_pe=state["w2_last"])
            cut("applyT")

            g_pe0 = sch.tok("Pe")

            def _(t, g_pe0=g_pe0):
                rrt = R["tT"].w
                sch.wait(t, "PE", rrt[0], rrt[1] + rrt[2])
                for s_, t_ in R["pb6"].r.items():
                    sch.wait(t, "PE", s_, t_)
                for g in range(NG):
                    if g > 0:
                        sch.wait(t, "PE", "Ac", sch.tok("Ac") + g)
                    for dc in range(DCH):
                        i = t.matmul(pb[6][0:8, 0:512], wg_sb[:, dc * 8:(dc + 1) * 8],
                                     tT[:, dc * 2048 + g * 512:dc * 2048 + (g + 1) * 512],
                                     start=(dc == 0), stop=(dc == DCH - 1))
                    sch.inc(i, "Pe")
            ENG["PE"](_)

            def _(sc, g_pe0=g_pe0):
                for g in range(NG):
                    sch.wait(sc, "SE", "Pe", g_pe0 + g + 1)
                    i = sc.activation(gTb[:, g * 512:(g + 1) * 512],
                                      pb[6][0:8, 0:512], AF.Sigmoid)
                    sch.inc(i, "Ac")
                i = sc.copy(dscr[:, 16:32], dscr[:, 0:16])
                sch.inc(i, "Ac")
            ENG["SE"](_)
            R["gT"].set_w("Ac", sch.tok("Ac"), 0)
            R["pb6"].set_w("Pe", sch.tok("Pe"), 0)
            R["pb6"].add_r("Ac", sch.tok("Ac"))
            cut("gates")

            def _(g_, s_i=s_i):
                for s_, t_ in R["wo"].r.items():
                    sch.wait(g_, "GE", s_, t_)
                sch.wait(g_, "GE", "Pe", state["w1_last"])
                i = g_.dma_start(out=woall[:].rearrange("r (p c) -> r p c", p=NPAIR),
                                 in_=d_wo.rearrange("p r c -> r p c"))
                sch.inc(i, "Cq", 16)
                R["wo"].set_w("Cq", sch.cnt["Cq"])
            ENG["GE"](_)

            state["prev_av"] = 0
            for pair in range(NPAIR):
                state["attn_ep"] = sch.tok("Dv")
                def wdma(src, bufi, pair=pair):
                    def _(g_):
                        for s_, t_ in R[wbufn[bufi]].r.items():
                            sch.wait(g_, "GE", s_, t_)
                        if bufi == 1:
                            rl = R["lgt"]
                            if rl.w:
                                sch.wait(g_, "GE", rl.w[0], rl.w[1] + rl.w[2])
                            for s_, t_ in rl.r.items():
                                sch.wait(g_, "GE", s_, t_)
                        sch.wait(g_, "GE", "Pe", state["w1_last"])
                        i = g_.dma_start(out=wbuf[bufi][:], in_=src[pair])
                        sch.inc(i, "Cq", 16)
                        R[wbufn[bufi]].set_w("Cq", sch.cnt["Cq"])
                    ENG["GE"](_)

                if pair == 0:
                    wdma(d_wq, 0)
                    wdma(d_wk, 1)

                for which in (0, 1):
                    dstq = qp if which == 0 else kp
                    bres = "qp" if which == 0 else "kp"
                    wbi = (pair + which) % 2
                    qk_pe0 = sch.tok("Pe")
                    qk_ac0 = sch.tok("Ac")

                    def _(t, qk_pe0=qk_pe0, qk_ac0=qk_ac0, wbi=wbi):
                        rr = R[wbufn[wbi]].w
                        sch.wait(t, "PE", rr[0], rr[1])
                        rrt = R["tT"].w
                        sch.wait(t, "PE", rrt[0], rrt[1] + rrt[2])
                        for bn in ("pb5", "pb6"):
                            for s_, t_ in R[bn].r.items():
                                sch.wait(t, "PE", s_, t_)
                        for g in range(NG):
                            bk = pb[5 + (g % 2)]
                            if g >= 2:
                                sch.wait(t, "PE", "Ac", qk_ac0 + g - 1)
                            for dc in range(DCH):
                                i = t.matmul(bk[:, 0:512], wbuf[wbi][:, dc * 128:(dc + 1) * 128],
                                             tT[:, dc * 2048 + g * 512:dc * 2048 + (g + 1) * 512],
                                             start=(dc == 0), stop=(dc == DCH - 1))
                            sch.inc(i, "Pe")
                    ENG["PE"](_)

                    def _(sc, dstq=dstq, qk_pe0=qk_pe0):
                        for g in range(NG):
                            sch.wait(sc, "SE", "Pe", qk_pe0 + g + 1)
                            i = sc.copy(dstq[:, g * 512:(g + 1) * 512], pb[5 + (g % 2)][:, 0:512])
                            sch.inc(i, "Ac")
                    ENG["SE"](_)
                    R[wbufn[wbi]].add_r("Pe", sch.tok("Pe"))
                    if wbi == 1:
                        R["lgt"].add_r("Pe", sch.tok("Pe"))
                    R[bres].set_w("Ac", sch.tok("Ac"), 1)
                    R["pb5"].set_w("Pe", qk_pe0 + 3, 0)
                    R["pb6"].set_w("Pe", qk_pe0 + 4, 0)
                    R["pb5"].add_r("Ac", qk_ac0 + 3)
                    R["pb6"].add_r("Ac", qk_ac0 + 4)
                wdma(d_wv, pair % 2)

                vp_pe0 = sch.tok("Pe")
                vp_dv0 = sch.tok("Dv")

                def _(t, vp_pe0=vp_pe0, vp_dv0=vp_dv0, wbi=pair % 2):
                    rr = R[wbufn[wbi]].w
                    sch.wait(t, "PE", rr[0], rr[1])
                    for bn in ("pb4", "pb5"):
                        for s_, t_ in R[bn].r.items():
                            sch.wait(t, "PE", s_, t_)
                    for c in range(NCH):
                        bk = pb[4 + c % 2]
                        if c >= 2:
                            sch.wait(t, "PE", "Dv", vp_dv0 + c - 1)
                        for dc in range(DCH):
                            i = t.matmul(bk[:, 0:128],
                                         tT[:, dc * 2048 + c * 128:dc * 2048 + (c + 1) * 128],
                                         wbuf[pair % 2][:, dc * 128:(dc + 1) * 128],
                                         start=(dc == 0), stop=(dc == DCH - 1))
                        sch.inc(i, "Pe")
                ENG["PE"](_)

                def _(v, vp_pe0=vp_pe0):
                    for c in range(NCH):
                        bk = pb[4 + c % 2]
                        sch.wait(v, "VE", "Pe", vp_pe0 + c + 1)
                        v.tensor_copy(vp[:, c * 130:c * 130 + 64], bk[:, 0:64])
                        i = v.tensor_copy(vp[:, c * 130 + 65:c * 130 + 129], bk[:, 64:128])
                        sch.inc(i, "Dv")
                    v3 = vp[:].rearrange("p (cc j) -> p cc j", cc=NCH)
                    v.memset(v3[:, :, 64:65], 1.0)
                    v.memset(v3[:, :, 129:130], 1.0)
                    i = v.tensor_copy(dscr[:, 48:64].bitcast(F32) if False else dscr[:, 32:48], dscr[:, 0:16])
                    sch.inc(i, "Dv")
                ENG["VE"](_)
                R[wbufn[pair % 2]].add_r("Pe", sch.tok("Pe"))
                if pair % 2 == 1:
                    R["lgt"].add_r("Pe", sch.tok("Pe"))
                R["vp"].set_w("Dv", sch.tok("Dv"), 0)
                R["pb4"].set_w("Pe", vp_pe0 + 15, 0)
                R["pb5"].set_w("Pe", vp_pe0 + 16, 0)
                R["pb4"].add_r("Dv", vp_dv0 + 15)
                R["pb5"].add_r("Dv", vp_dv0 + 16)
                if pair < NPAIR - 1:
                    wdma(d_wq, (pair + 1) % 2, pair=pair + 1)
                    wdma(d_wk, pair % 2, pair=pair + 1)

                def _(sc):
                    return sc.activation(q2[:, 0:2048], qp[:], AF.Square)
                sec("SE", _, reads=("qp",), writes=("q2",))
                seal("SE")

                def _(t):
                    for g4 in range(4):
                        i = t.matmul(pb[g4][0:1, 0:512], onescol[:],
                                     q2[:, g4 * 512:(g4 + 1) * 512], start=True, stop=True)
                        sch.inc(i, "Pe")
                    return None
                sec("PE", _, reads=("q2",), writes=("pb0", "pb1", "pb2", "pb3"))
                nq_t = sch.tok("Pe")
                for g4 in range(4):
                    R[f"pb{g4}"].set_w("Pe", nq_t, 0)

                def _(v):
                    sch.wait(v, "VE", "Pe", nq_t)
                    i = None
                    for g4 in range(4):
                        i = v.reduce_max(mcol[0:1, g4:g4 + 1], pb[g4][0:1, 0:512], axis=AX.X)
                    return i
                sec("VE", _, reads=("pb0", "pb1", "pb2", "pb3"), writes=("mh",))
                seal("VE")

                def _(sc):
                    return sc.activation(q2[:, 0:2048], kp[:], AF.Square)
                sec("SE", _, reads=("kp", "q2"), writes=("q2",))
                seal("SE")

                def _(t):
                    for g4 in range(4):
                        i = t.matmul(pb[g4][0:1, 0:512], onescol[:],
                                     q2[:, g4 * 512:(g4 + 1) * 512], start=True, stop=True)
                        sch.inc(i, "Pe")
                    return None
                sec("PE", _, reads=("q2",), writes=("pb0", "pb1", "pb2", "pb3"))
                nk_t = sch.tok("Pe")
                for g4 in range(4):
                    R[f"pb{g4}"].set_w("Pe", nk_t, 0)

                def _(v):
                    sch.wait(v, "VE", "Pe", nk_t)
                    for g4 in range(4):
                        v.reduce_max(mcol[0:1, 8 + g4:9 + g4], pb[g4][0:1, 0:512], axis=AX.X)
                    sch.drain_inc(v, "Dv")
                    v.reduce_max(msc[0:1, 0:1], mcol[0:1, 0:4], axis=AX.X)
                    v.reduce_max(msc[0:1, 1:2], mcol[0:1, 8:12], axis=AX.X)
                    sch.drain_inc(v, "Dv")
                    v.drain()
                    return v.tensor_tensor(msc[0:1, 2:3], msc[0:1, 0:1], msc[0:1, 1:2], OP.mult)
                sec("VE", _, reads=("pb0", "pb1", "pb2", "pb3"), writes=("mh",))
                seal("VE")

                def _(sc):
                    return sc.activation(msc[0:1, 3:4], msc[0:1, 2:3], AF.Sqrt)
                sec("SE", _, reads=("mh",), writes=("mh",))
                seal("SE")

                def _(v):
                    return v.tensor_scalar(nmb1[0:1, 0:1], msc[0:1, 3:4], -0.125, None, OP.mult)
                sec("VE", _, reads=("mh",), writes=("mh",))
                seal("VE")

                def _(t):
                    return t.matmul(pb[6][:, 0:1], onesrow[0:1, :], nmb1[0:1, 0:1],
                                    start=True, stop=True)
                sec("PE", _, reads=("mh",), writes=("pb6",))

                def _(v):
                    return v.tensor_copy(negM[:], pb[6][:, 0:1])
                sec("VE", _, reads=("pb6",), writes=("negM",))
                negM_t = seal("VE")
                cut(f"mhat{pair}")

                for g in range(NG):
                    J = 4 * g + 4
                    d0 = 4 * g

                    def qw(j):
                        return max(0, (j - d0) * 128) if j >= d0 else 0

                    peops = []
                    for j in range(J):
                        peops.append(("s", j))
                        if j >= 3:
                            peops.append(("av", j - 3))
                    peops += [("av", J - 3), ("av", J - 2), ("av", J - 1)]
                    c_pe0 = sch.tok("Pe")
                    c_ac0 = sch.tok("Ac")
                    c_dv0 = sch.tok("Dv")
                    p_pe = {op: c_pe0 + k + 1 for k, op in enumerate(peops)}
                    p_ex = {j: c_ac0 + 2 * j + 1 for j in range(J)}
                    p_exd = {j: c_ac0 + 2 * j + 2 for j in range(J)}
                    p_mkd = {d0 + c: c_dv0 + 2 * (c + 1) for c in range(4)}

                    def _(t, g=g, J=J, d0=d0):
                        sch.wait(t, "PE", "Dv", state["attn_ep"])
                        sch.wait(t, "PE", "Dv", negM_t)
                        for kind, j in peops:
                            if kind == "s":
                                w = 512 - qw(j)
                                if j >= 2:
                                    sch.wait(t, "PE", "Ac", p_ex(j - 2) if False else p_ex[j - 2])
                                q0 = g * 512 + qw(j)
                                t.matmul(pb[j % 2][:, 0:w], kp[0:64, j * 128:(j + 1) * 128],
                                         qp[0:64, q0:(g + 1) * 512], start=True, stop=True)
                                i = t.matmul(pb[2 + j % 2][:, 0:w],
                                             kp[64:128, j * 128:(j + 1) * 128],
                                             qp[64:128, q0:(g + 1) * 512], start=True, stop=True)
                            else:
                                w = 512 - qw(j)
                                if j >= d0:
                                    sch.wait(t, "PE", "Dv", p_mkd[j])
                                else:
                                    sch.wait(t, "PE", "Ac", p_exd[j])
                                o0 = qw(j)
                                t.matmul(pb[4][0:65, o0:512], vp[:, j * 130:j * 130 + 65],
                                         Ebufs[j % 3][:, 0:w], start=(j == 0), stop=(j == J - 1))
                                i = t.matmul(pb[5][0:65, o0:512],
                                             vp[:, j * 130 + 65:j * 130 + 130],
                                             Ebufs[3 + j % 3][:, 0:w], start=(j == 0), stop=(j == J - 1))
                            sch.inc(i, "Pe")
                    ENG["PE"](_)
                    assert sch.tok("Pe") == c_pe0 + len(peops)

                    def _(sc, g=g, J=J):
                        sch.wait(sc, "SE", "Dv", negM_t)
                        for j in range(J):
                            w = 512 - qw(j)
                            sch.wait(sc, "SE", "Pe", p_pe[("s", j)])
                            if j >= 3:
                                sch.wait(sc, "SE", "Pe", p_pe[("av", j - 3)])
                            elif state["prev_av"]:
                                sch.wait(sc, "SE", "Pe", state["prev_av"])
                            sc.activation(Ebufs[j % 3][:, 0:w], pb[j % 2][:, 0:w], AF.Exp,
                                          bias=negM[:, 0:1], scale=0.125)
                            i = sc.activation(Ebufs[3 + j % 3][:, 0:w], pb[2 + j % 2][:, 0:w],
                                              AF.Exp, bias=negM[:, 0:1], scale=0.125)
                            sch.inc(i, "Ac")
                            i = sc.copy(dscr[:, 16:32], dscr[:, 0:16])
                            sch.inc(i, "Ac")
                    ENG["SE"](_)
                    assert sch.tok("Ac") == c_ac0 + 2 * J

                    def _(v, g=g, J=J, d0=d0, pair=pair):
                        for j in range(d0, J):
                            c = j - d0
                            w = 512 - 128 * c
                            sch.wait(v, "VE", "Ac", p_exd[j])
                            v.tensor_tensor(Ebufs[j % 3][:, 0:w], Ebufs[j % 3][:, 0:w],
                                            maskt[:, 0:w], OP.mult)
                            i = v.tensor_tensor(Ebufs[3 + j % 3][:, 0:w],
                                                Ebufs[3 + j % 3][:, 0:w],
                                                maskt[:, 0:w], OP.mult)
                            sch.inc(i, "Dv")
                            i = v.tensor_copy(dscr[:, 32:48], dscr[:, 0:16])
                            sch.inc(i, "Dv")
                        sch.wait(v, "VE", "Pe", p_pe[("av", J - 1)])
                        col = pair * 2048 + g * 512
                        zc = g * 1024
                        v.tensor_copy(outT[0:64, col:col + 512], pb[4][0:64, 0:512])
                        v.tensor_copy(outT[64:128, col:col + 512], pb[5][0:64, 0:512])
                        v.tensor_copy(zst[0:1, zc:zc + 512], pb[4][64:65, 0:512])
                        i = v.tensor_copy(zst[0:1, zc + 512:zc + 1024], pb[5][64:65, 0:512])
                        sch.inc(i, "Dv")
                        i = v.tensor_copy(dscr[:, 32:48], dscr[:, 0:16])
                        sch.inc(i, "Dv")
                    ENG["VE"](_)
                    assert sch.tok("Dv") == c_dv0 + 2 * 4 + 2
                    state["attn_ep"] = sch.tok("Dv")
                    state["prev_av"] = p_pe[("av", J - 1)]
                    cut(f"core{pair}_{g}")
                R["outT"].set_w("Dv", sch.tok("Dv"), 0)
                R["zst"].set_w("Dv", sch.tok("Dv"), 0)
                R["pb4"].set_w("Pe", state["prev_av"], 0)
                R["pb5"].set_w("Pe", state["prev_av"], 0)
                R["pb4"].add_r("Dv", state["attn_ep"])
                R["pb5"].add_r("Dv", state["attn_ep"])
                for jj in range(4):
                    R[f"pb{jj}"].set_w("Pe", state["prev_av"], 0)
                    R[f"pb{jj}"].add_r("Dv", state["attn_ep"])
                cut(f"preZ{pair}")
                R["negM"].add_r("Ac", sch.tok("Ac"))
                cut(f"preZ{pair}")

                def _(g_, pair=pair):
                    rr = R["zst"]
                    sch.wait(g_, "GE", rr.w[0], rr.w[1] + rr.w[2])
                    for s_, t_ in R["zbuf"].r.items():
                        sch.wait(g_, "GE", s_, t_)
                    sch.wait(g_, "GE", "Cz2", sch.cnt["Cz2"])
                    i = g_.dma_start(out=d_zscr.rearrange("(o w) -> o w", o=1), in_=zst[0:1, :])
                    sch.inc(i, "Cz1", 16)
                    g_.wait_ge(sch.sem["Cz1"], sch.cnt["Cz1"])
                    i = g_.dma_start(
                        out=zbuf[2 * pair:2 * pair + 2, :].rearrange("h (g w) -> h g w", g=4),
                        in_=d_zscr.rearrange("(g h w) -> h g w", g=4, h=2))
                    sch.inc(i, "Cz2", 16)
                    R["zst"].add_r("Cz1", sch.cnt["Cz1"])
                ENG["GE"](_)
                R["zbuf"].set_w("Cz2", sch.tok("Cz2"))
                cut(f"pair{pair}")

            for gz in range(NG):
                def _(v, gz=gz):
                    return v.reciprocal(crec[:], zbuf[0:8, gz * 512:(gz + 1) * 512])
                sec("VE", _, reads=("zbuf",), writes=("crec",))
                seal("VE")

                def _(v, gz=gz):
                    return v.tensor_tensor(cgb[:, gz * 512:(gz + 1) * 512], crec[:],
                                           gTb[:, gz * 512:(gz + 1) * 512], OP.mult)
                sec("VE", _, reads=("crec", "gT"), writes=("cgb",))
            cgb_t2 = seal("VE")

            state["scale_done"] = cgb_t2
            state["pred_done"] = cgb_t2
            for g in range(NG):
                s_pe0 = sch.tok("Pe")
                s_dv0 = sch.tok("Dv")

                def _(t, g=g, s_pe0=s_pe0, s_dv0=s_dv0):
                    sch.wait(t, "PE", "Dv", cgb_t2)
                    sch.wait(t, "PE", "Dv", state["attn_ep"])
                    for pair in range(NPAIR):
                        if pair >= 2:
                            sch.wait(t, "PE", "Dv", s_dv0 + pair - 1)
                        else:
                            sch.wait(t, "PE", "Dv", state["scale_done"])
                        i = t.matmul(pb[2 + pair % 2][:, 0:512],
                                     psel[:, pair * 128:(pair + 1) * 128],
                                     cgb[0:8, g * 512:(g + 1) * 512], start=True, stop=True)
                        sch.inc(i, "Pe")
                    for k in range(8):
                        ns, dh = divmod(k, 2)
                        sch.wait(t, "PE", "Dv", s_dv0 + 5)
                        if k >= 2:
                            sch.wait(t, "PE", "Dv", s_dv0 + 5 + k - 1)
                        else:
                            sch.wait(t, "PE", "Dv", state["pred_done"])
                        for pair in range(NPAIR):
                            i = t.matmul(pb[k % 2][:, 0:512],
                                         outT[:, pair * 2048 + g * 512 + ns * 128:
                                              pair * 2048 + g * 512 + (ns + 1) * 128],
                                         woall[:, pair * 1024 + dh * 512:
                                               pair * 1024 + (dh + 1) * 512],
                                         start=(pair == 0), stop=(pair == NPAIR - 1))
                        sch.inc(i, "Pe")
                ENG["PE"](_)

                def _(v, g=g, s_pe0=s_pe0):
                    for pair in range(NPAIR):
                        sch.wait(v, "VE", "Pe", s_pe0 + pair + 1)
                        col = pair * 2048 + g * 512
                        i = v.tensor_tensor(outT[:, col:col + 512], outT[:, col:col + 512],
                                            pb[2 + pair % 2][:, 0:512], OP.mult)
                        sch.inc(i, "Dv")
                    i = v.tensor_copy(dscr[:, 32:48], dscr[:, 0:16])
                    sch.inc(i, "Dv")
                    for k in range(8):
                        ns, dh = divmod(k, 2)
                        c = g * 4 + ns
                        sch.wait(v, "VE", "Pe", s_pe0 + 4 + k + 1)
                        xsl = x_sb[:, c * 1024 + dh * 512:c * 1024 + (dh + 1) * 512]
                        i = v.tensor_tensor(xsl, xsl, pb[k % 2][:, 0:512], OP.add)
                        sch.inc(i, "Dv")
                ENG["VE"](_)
                state["scale_done"] = s_dv0 + 5
                state["pred_done"] = s_dv0 + 13

                def _(sc, g=g, s_dv0=s_dv0):
                    i = None
                    for ns in range(4):
                        c = g * 4 + ns
                        sch.wait(sc, "SE", "Dv", s_dv0 + 7 + 2 * ns)
                        xc = x_sb[:, c * 1024:(c + 1) * 1024]
                        sc.activation(sqscr[:], xc[:, 0:512], AF.Square,
                                      accum_out=sqs[:, c:c + 1])
                        i = sc.activation(sqscr[:], xc[:, 512:1024], AF.Square,
                                          accum_out=msq[:, c:c + 1])
                    sch.inc(i, "Ac")
                ENG["SE"](_)
                R["x"].add_r("Ac", sch.tok("Ac"))

                def _(v, g=g):
                    i = None
                    for ns in range(4):
                        c = g * 4 + ns
                        i = v.reduce_sum(sums[:, c:c + 1],
                                         x_sb[:, c * 1024:(c + 1) * 1024], axis=AX.X)
                    sch.inc(i, "Dv")
                ENG["VE"](_)
                R["x"].add_r("Dv", sch.tok("Dv"))
            def _(v):
                sch.drain_inc(v, "Dv")
            ENG["VE"](_)
            R["x"].set_w("Dv", sch.tok("Dv"), 0)
            state["pred_dv"] = sch.tok("Dv")
            cut("pred")
            emit_ln(is_ln2=True, precomputed=True)
            emit_apply_T(nrm_sb, rewrite=True)
            hT_t = sch.tok("Dv")
            cut("ln2")

            for g in range(NG):
                def _(t, g=g):
                    i = None
                    for dc in range(DCH):
                        i = t.matmul(pb[6][0:8, 0:512], gw_sb[:, dc * 8:(dc + 1) * 8],
                                     tT[:, dc * 2048 + g * 512:dc * 2048 + (g + 1) * 512],
                                     start=(dc == 0), stop=(dc == DCH - 1))
                    return i
                sec("PE", _, reads=("tT",), writes=("pb6",))

                def _(v, g=g):
                    return v.tensor_scalar(lgt[:], pb[6][0:8, 0:512], gb_sb[0:8, 0:1],
                                           None, OP.add)
                sec("VE", _, reads=("pb6",), writes=("lgt",))
                seal("VE")

                def _(t, g=g):
                    i = None
                    for k in range(4):
                        i = t.transpose(pb[5][:, k * 8:(k + 1) * 8],
                                        lgt[0:8, k * 128:(k + 1) * 128], identf[0:8, 0:8])
                    return i
                sec("PE", _, reads=("lgt",), writes=("pb5",))

                def _(v, g=g):
                    return v.tensor_copy(lgtm[:, g * 32:g * 32 + 32], pb[5][:, 0:32])
                sec("VE", _, reads=("pb5",), writes=("lgtm",))
            seal("VE")

            def _(v):
                L3 = lgtm[:].rearrange("p (s e) -> p s e", e=8)
                v.tensor_reduce(rm1[:], L3, axis=AX.X, op=OP.max)
                sch.drain_inc(v, "Dv")
                v.tensor_tensor(req1[:].rearrange("p (s e) -> p s e", e=8), L3,
                                rm1[:].rearrange("p s -> p s ()").broadcast_to([128, 16, 8]),
                                OP.is_equal)
                v.drain()
                v.scalar_tensor_tensor(rl2[:], req1[:], -1e30, lgtm[:], OP.mult, OP.add)
                sch.drain_inc(v, "Dv")
                v.tensor_reduce(rm2[:], rl2[:].rearrange("p (s e) -> p s e", e=8),
                                axis=AX.X, op=OP.max)
                sch.drain_inc(v, "Dv")
                v.tensor_tensor(req2[:].rearrange("p (s e) -> p s e", e=8),
                                rl2[:].rearrange("p (s e) -> p s e", e=8),
                                rm2[:].rearrange("p s -> p s ()").broadcast_to([128, 16, 8]),
                                OP.is_equal)
                v.drain()
                return v.tensor_tensor(rd12[:], rm1[:], rm2[:], OP.subtract)
            sec("VE", _, reads=("lgtm",), writes=("rtr",))
            seal("VE")

            def _(sc):
                return sc.activation(rw1[:], rd12[:], AF.Sigmoid)
            sec("SE", _, reads=("rtr",), writes=("rtr",))
            seal("SE")

            def _(v):
                v.tensor_scalar(rw2[:], rw1[:], -1.0, 1.0, OP.mult, OP.add)
                v.drain()
                v.tensor_tensor(rwa[:].rearrange("p (s e) -> p s e", e=8),
                                req1[:].rearrange("p (s e) -> p s e", e=8),
                                rw1[:].rearrange("p s -> p s ()").broadcast_to([128, 16, 8]),
                                OP.mult)
                v.tensor_tensor(rwf[:].rearrange("p (s e) -> p s e", e=8),
                                req2[:].rearrange("p (s e) -> p s e", e=8),
                                rw2[:].rearrange("p s -> p s ()").broadcast_to([128, 16, 8]),
                                OP.mult)
                v.drain()
                return v.tensor_tensor(rwf[:], rwf[:], rwa[:], OP.add)
            sec("VE", _, reads=("rtr",), writes=("rwf",))
            seal("VE")

            for g in range(NG):
                def _(t, g=g):
                    i = None
                    for k in range(4):
                        sub = g * 4 + k
                        i = t.transpose(pb[5][0:8, k * 128:(k + 1) * 128],
                                        rwf[:, sub * 8:(sub + 1) * 8], identf[:])
                    return i
                sec("PE", _, reads=("rwf",), writes=("pb5",))

                def _(v, g=g):
                    return v.tensor_scalar(wTb[:, g * 512:(g + 1) * 512], pb[5][0:8, 0:512],
                                           1.0 / GELU_A, None, OP.mult)
                sec("VE", _, reads=("pb5",), writes=("wTb",))
            seal("VE")
            cut("router")

            wtb_t2 = sch.tok("Dv")
            NT = NE * NG
            state["t1war"] = [("Dv", wtb_t2)] * 4
            state["m2war"] = [wtb_t2, wtb_t2]
            state["pb6war"] = ("Dv", wtb_t2)
            state["wbrR"] = [0, 0]
            state["actR"] = [0, 0]
            prev = {}
            w1tok, w2tok = {}, {}

            def _(s):
                sch.wait(s, "GE", "Dv", state["pred_dv"])
                sch.wait(s, "GE", "Pe", state["w1_last"])
                i = s.dma_start(out=w1b[:, 0:8192], in_=d_w1[0][:, 0:8192])
                sch.inc(i, "Cw1", 16)
                i = s.dma_start(out=w1b[:, 8192:16384], in_=d_w1[0][:, 8192:16384])
                sch.inc(i, "Cw1", 16)
                sch.wait(s, "GE", "Pe", state["w2_last"])
                i = s.dma_start(out=w2b[:], in_=d_w2[0])
                sch.inc(i, "Cw2", 16)
            ENG["GE"](_)
            w1tok[0] = (sch.tok("Cw1") - 16, sch.tok("Cw1"))
            w2tok[0] = sch.tok("Cw2")

            for n in range(NT + 1):
                cur = n if n < NT else None
                prv = n - 1 if n >= 1 else None
                par = n % 2
                if cur is not None:
                    e_c, t_c = divmod(cur, NG)

                f_pe0 = sch.tok("Pe")
                f_ac0 = sch.tok("Ac")
                f_dv0 = sch.tok("Dv")
                f_pl0 = sch.tok("Pl")
                xacc_base = f_dv0 + (32 if cur is not None else 0)

                def p_eg(c, b=f_ac0):
                    return b + (2 if c == 0 else 3 * c + 1)

                def p_el(c, b=f_ac0):
                    return b + (3 if c == 0 else 3 * c + 2)

                def p_si(c, b=f_ac0):
                    return b + (3 * c + 6 if c <= 6 else 26)

                def p_lcb(c, b=f_dv0):
                    return b + (2 if c == 0 else 4 * c)

                def p_a0(c, b=f_dv0):
                    return b + (4 * (c + 1) + 1 if c <= 6 else 31)

                def p_a0d(c, b=f_dv0):
                    return b + (4 * (c + 1) + 2 if c <= 6 else 32)

                m2_new = []

                def _(t, cur=cur, prv=prv, xacc_base=xacc_base, f_pe0=f_pe0,
                      p_eg=p_eg, p_el=p_el, f_pl0=f_pl0, m2_new=m2_new):
                    if cur is not None:
                        e, t_i = divmod(cur, NG)
                        sch.wait(t, "PE", "Dv", wtb_t2)
                        sch.wait(t, "PE", state["pb6war"][0], state["pb6war"][1])
                        i = t.matmul(pb[6][:, 0:512], oneh[:, e * 128:(e + 1) * 128],
                                     wTb[0:8, t_i * 512:(t_i + 1) * 512],
                                     start=True, stop=True)
                        sch.inc(i, "Pe")
                        sch.wait(t, "PE", "Cw1", w1tok[e][0])
                        rrt = R["tT"].w
                        sch.wait(t, "PE", rrt[0], rrt[1] + rrt[2])
                        for c in range(ICH):
                            if c == 4:
                                sch.wait(t, "PE", "Cw1", w1tok[e][1])
                            for which in (0, 1):
                                fc = 2 * c + which
                                bi = (c % 2) if which == 0 else 2 + c % 2
                                bk = pb[bi]
                                if c >= 2:
                                    sch.wait(t, "PE", "Ac",
                                             p_eg(c - 2) if which == 0 else p_el(c - 2))
                                else:
                                    s_, t_ = state["t1war"][bi]
                                    sch.wait(t, "PE", s_, t_)
                                for dc in range(DCH):
                                    i = t.matmul(
                                        bk[:, 0:512],
                                        w1b[:, fc * 1024 + dc * 128:fc * 1024 + (dc + 1) * 128],
                                        tT[:, dc * 2048 + t_i * 512:dc * 2048 + (t_i + 1) * 512],
                                        start=(dc == 0), stop=(dc == DCH - 1))
                                sch.inc(i, "Pe")
                    if prv is not None:
                        ep, tp = divmod(prv, NG)
                        sch.wait(t, "PE", "Cw2", w2tok[ep])
                        sch.wait(t, "PE", "Pl", f_pl0)
                        for k in range(8):
                            ns, dh = divmod(k, 2)
                            c2 = tp * 4 + ns
                            bk = pb[4 + k % 2]
                            if k >= 2:
                                sch.wait(t, "PE", "Dv", xacc_base + (k - 2) + 1)
                            else:
                                sch.wait(t, "PE", "Dv", state["m2war"][k % 2])
                            first = True
                            if ep == 0:
                                t.matmul(bk[:, 0:512], wTb[0:8, c2 * 128:(c2 + 1) * 128],
                                         b2_sb[0:8, dh * 512:(dh + 1) * 512],
                                         start=True, stop=False)
                                first = False
                            for ic in range(ICH):
                                i = t.matmul(
                                    bk[:, 0:512],
                                    actb[prv % 2][:, ic * 512 + ns * 128:ic * 512 + (ns + 1) * 128],
                                    w2b[:, ic * 1024 + dh * 512:ic * 1024 + (dh + 1) * 512],
                                    start=(first and ic == 0), stop=(ic == ICH - 1))
                                first = False
                            sch.inc(i, "Pe")
                            m2_new.append(sch.tok("Pe"))
                ENG["PE"](_)
                exp_pe = (1 + 16 if cur is not None else 0) + (8 if prv is not None else 0)
                assert sch.tok("Pe") == f_pe0 + exp_pe

                if cur is not None:
                    def _(sc, cur=cur, par=par, f_pe0=f_pe0, f_ac0=f_ac0,
                          p_lcb=p_lcb, p_a0=p_a0, prev=prev):
                        e, t_i = divmod(cur, NG)
                        sch.wait(sc, "SE", "Pe", f_pe0 + 1)
                        sch.wait(sc, "SE", "Pl", state["wbrR"][par])
                        i = sc.copy(wbr[par][:], pb[6][:, 0:512])
                        sch.inc(i, "Ac")

                        def eg(c):
                            bgc = b1_sb[:, e * 16 + 2 * c:e * 16 + 2 * c + 1]
                            sch.wait(sc, "SE", "Pe", f_pe0 + 2 + 2 * c)
                            if c >= 2:
                                sch.wait(sc, "SE", "Dv", p_lcb(c - 2))
                            elif prev:
                                sch.wait(sc, "SE", "Dv", prev["lcb"][c])
                            i = sc.activation(g1p[c % 2][:], pb[c % 2][:, 0:512],
                                              AF.Identity, bias=bgc)
                            sch.inc(i, "Ac")

                        def el(c):
                            blc = b1_sb[:, e * 16 + 2 * c + 1:e * 16 + 2 * c + 2]
                            sch.wait(sc, "SE", "Pe", f_pe0 + 3 + 2 * c)
                            if c >= 2:
                                sch.wait(sc, "SE", "Dv", p_lcb(c - 2))
                            elif prev:
                                sch.wait(sc, "SE", "Dv", prev["lcb"][c])
                            i = sc.activation(l1p[c % 2][:], pb[2 + c % 2][:, 0:512],
                                              AF.Identity, bias=blc)
                            sch.inc(i, "Ac")

                        def si(c):
                            sch.wait(sc, "SE", "Dv", p_lcb(c))
                            if c >= 1:
                                sch.wait(sc, "SE", "Dv", p_a0(c - 1))
                            elif prev:
                                sch.wait(sc, "SE", "Dv", prev["a07"])
                            i = sc.activation(p1[:], g1p[c % 2][:], AF.Silu, scale=GELU_A)
                            sch.inc(i, "Ac")

                        eg(0); el(0)
                        for c in range(1, ICH):
                            eg(c); el(c); si(c - 1)
                        i = sc.copy(dscr[:, 16:32], dscr[:, 0:16])
                        sch.inc(i, "Ac")
                        si(ICH - 1)
                        i = sc.copy(dscr[:, 16:32], dscr[:, 0:16])
                        sch.inc(i, "Ac")
                    ENG["SE"](_)
                    assert sch.tok("Ac") == f_ac0 + 27

                def _(v, cur=cur, prv=prv, f_pl0=f_pl0, m2_new=m2_new,
                      p_eg=p_eg, p_el=p_el, p_si=p_si):
                    if cur is not None:
                        def minglcb(c):
                            sch.wait(v, "VE", "Ac", p_eg(c) + 1)
                            i = v.tensor_scalar(g1p[c % 2][:], g1p[c % 2][:],
                                                LIMIT, None, OP.min)
                            sch.inc(i, "Dv")
                            sch.wait(v, "VE", "Ac", p_el(c) + 1)
                            i = v.tensor_scalar(lcb2[c % 2][:], l1p[c % 2][:],
                                                LIMIT, -LIMIT, OP.min, OP.max)
                            sch.inc(i, "Dv")

                        def a0op(c):
                            sch.wait(v, "VE", "Ac", p_si(c) + 1)
                            sch.wait(v, "VE", "Pl", f_pl0 + c)
                            i = v.scalar_tensor_tensor(a0[:], lcb2[c % 2][:], 1.0,
                                                       p1[:], OP.add, OP.mult)
                            sch.inc(i, "Dv")
                            i = v.tensor_copy(dscr[:, 32:48], dscr[:, 0:16])
                            sch.inc(i, "Dv")

                        minglcb(0)
                        for c in range(1, ICH):
                            minglcb(c)
                            a0op(c - 1)
                        a0op(ICH - 1)
                    if prv is not None:
                        tp = prv % NG
                        for k in range(8):
                            ns, dh = divmod(k, 2)
                            c2 = tp * 4 + ns
                            sch.wait(v, "VE", "Pe", m2_new[k])
                            xsl = x_sb[:, c2 * 1024 + dh * 512:c2 * 1024 + (dh + 1) * 512]
                            i = v.tensor_tensor(xsl, xsl, pb[4 + k % 2][:, 0:512], OP.add)
                            sch.inc(i, "Dv")
                ENG["VE"](_)
                exp_dv = (32 if cur is not None else 0) + (8 if prv is not None else 0)
                assert sch.tok("Dv") == f_dv0 + exp_dv

                if prv is not None and prv >= NT - NG:
                    tq = prv - (NT - NG)

                    def _(v):
                        sch.drain_inc(v, "Dv")
                    ENG["VE"](_)
                    seal_t = sch.tok("Dv")

                    def _(s, tq=tq, s_i=s_i, seal_t=seal_t):
                        sch.wait(s, "SY", "Dv", seal_t)
                        i = s.dma_start(
                            out=xout[s_i][4 * tq * 128:(4 * tq + 4) * 128]
                                .rearrange("(c p) d -> p c d", p=128),
                            in_=x_sb[:, 4 * tq * 1024:(4 * tq + 4) * 1024]
                                .rearrange("p (c d) -> p c d", c=4))
                        sch.inc(i, "So", 16)
                        if s_i < nseq - 1:
                            sch.wait(s, "SY", "So", sch.cnt["So"])
                            i = s.dma_start(
                                out=x_sb[:, 4 * tq * 1024:(4 * tq + 4) * 1024]
                                    .rearrange("p (c d) -> p c d", c=4),
                                in_=xin[s_i + 1][4 * tq * 128:(4 * tq + 4) * 128]
                                    .rearrange("(c p) d -> p c d", p=128))
                            sch.inc(i, "Cx", 16)
                    ENG["SY"](_)
                    if s_i < nseq - 1:
                        cx_t = sch.tok("Cx")
                        R["x"].set_w("Cx", cx_t)

                        def _(sc, tq=tq, cx_t=cx_t):
                            sch.wait(sc, "SE", "Cx", cx_t)
                            i = None
                            for ns in range(4):
                                c = tq * 4 + ns
                                xc = x_sb[:, c * 1024:(c + 1) * 1024]
                                sc.activation(sqscr[:], xc[:, 0:512], AF.Square,
                                              accum_out=sqs[:, c:c + 1])
                                i = sc.activation(sqscr[:], xc[:, 512:1024], AF.Square,
                                                  accum_out=msq[:, c:c + 1])
                            sch.inc(i, "Ac")
                        ENG["SE"](_)
                        R["x"].add_r("Ac", sch.tok("Ac"))

                        def _(v, tq=tq, cx_t=cx_t):
                            sch.wait(v, "VE", "Cx", cx_t)
                            i = None
                            for ns in range(4):
                                c = tq * 4 + ns
                                i = v.reduce_sum(sums[:, c:c + 1],
                                                 x_sb[:, c * 1024:(c + 1) * 1024],
                                                 axis=AX.X)
                            sch.inc(i, "Dv")
                        ENG["VE"](_)
                        R["x"].add_r("Dv", sch.tok("Dv"))

                if cur is not None:
                    def _(p, cur=cur, par=par, f_ac0=f_ac0, p_a0d=p_a0d):
                        sch.wait(p, "GE", "Ac", f_ac0 + 2)
                        sch.wait(p, "GE", "Pe", state["actR"][par])
                        for c in range(ICH):
                            sch.wait(p, "GE", "Dv", p_a0d(c))
                            i = p.tensor_tensor(actb[par][:, c * 512:(c + 1) * 512],
                                                a0[:], wbr[par][:], OP.mult)
                            sch.inc(i, "Pl")
                        i = p.tensor_copy(dscr[:, 48:64], dscr[:, 0:16])
                        sch.inc(i, "Pl")
                    ENG["GE"](_)
                    assert sch.tok("Pl") == f_pl0 + 9

                if cur is not None and t_c == NG - 1 and e_c < NE - 1:
                    def _(s, e_c=e_c, f_pe0=f_pe0):
                        sch.wait(s, "GE", "Pe", f_pe0 + 9)
                        i = s.dma_start(out=w1b[:, 0:8192], in_=d_w1[e_c + 1][:, 0:8192])
                        sch.inc(i, "Cw1", 16)
                        sch.wait(s, "GE", "Pe", f_pe0 + 17)
                        i = s.dma_start(out=w1b[:, 8192:16384],
                                        in_=d_w1[e_c + 1][:, 8192:16384])
                        sch.inc(i, "Cw1", 16)
                    ENG["GE"](_)
                    w1tok[e_c + 1] = (sch.tok("Cw1") - 16, sch.tok("Cw1"))
                if cur is not None and t_c == 0 and e_c >= 1:
                    def _(s, e_c=e_c, f_pe0=f_pe0):
                        sch.wait(s, "GE", "Pe", f_pe0 + 25)
                        i = s.dma_start(out=w2b[:], in_=d_w2[e_c])
                        sch.inc(i, "Cw2", 16)
                    ENG["GE"](_)
                    w2tok[e_c] = sch.tok("Cw2")
                if cur is not None:
                    state["t1war"] = [("Ac", p_eg(6)), ("Ac", p_eg(7)),
                                      ("Ac", p_el(6)), ("Ac", p_el(7))]
                    state["pb6war"] = ("Ac", f_ac0 + 1)
                    state["wbrR"][par] = f_pl0 + 9
                    prev = {"lcb": (p_lcb(6), p_lcb(7)), "a07": p_a0(7)}
                    if cur == NT - 1:
                        state["w1_last"] = f_pe0 + 17
                if prv is not None:
                    state["m2war"] = [xacc_base + 7, xacc_base + 8]
                    state["actR"][prv % 2] = m2_new[-1]
                    if prv == NT - 1:
                        state["w2_last"] = m2_new[-1]
                cut(f"stage_{n}")
            def _(v):
                sch.drain_inc(v, "Dv")
            ENG["VE"](_)
            ffn_t = sch.tok("Dv")
            cut("full")
          except _CutDone:
            pass
          if True:
            if DBG_STOP:
                def _(v):
                    i = None
                    for ix, tt in enumerate((sums, sqs, msq, nm, nv, vpe, lnv, rstd,
                                             rsq, vv, mt2, lnw, alphav, cc1, cc2)):
                        i = v.tensor_copy(q2[:, 0:2048].bitcast(F32)[:, ix * 16:(ix + 1) * 16], tt[:])
                    q2f = q2[:, 0:2048].bitcast(F32)
                    v.tensor_copy(q2f[:, 256:384], lgtm[:])
                    v.tensor_copy(q2f[:, 384:512], rwf[:])
                    v.tensor_copy(q2f[:, 512:528], rm1[:])
                    v.tensor_copy(q2f[:, 528:544], rm2[:])
                    v.tensor_copy(q2f[:, 544:560], rw1[:])
                    i = v.tensor_copy(q2f[:, 560:688], req1[:])
                    return i
                sec("VE", _, reads=("stats",), writes=("q2",))
            ffn_t = seal("VE")
            se_t = seal("SE")
            pl_t = seal("GE")

            def _(s):
                sch.wait(s, "SY", "Dv", ffn_t)
                sch.wait(s, "SY", "Ac", se_t)
                sch.wait(s, "SY", "Pl", pl_t)
                sch.wait(s, "SY", "Pe", sch.tok("Pe"))
                for cn in ("Cq", "Cz1", "Cz2", "Cw1", "Cw2"):
                    sch.wait(s, "SY", cn, sch.tok(cn))
                i = s.dma_start(out=d_dbg[:, :],
                                in_=arena[:, 0:57600].bitcast(F32)[:, 0:28800])
                sch.inc(i, "So", 16)
            if DBG_STOP and DBG_DUMP:
                ENG["SY"](_)

          if True:
            state["store"] = sch.tok("So")

        def _(g_):
            g_.wait_ge(sch.sem["So"], sch.cnt["So"])
        ENG["GE"](_)

    return nc


_NC_CACHE = None


def kernel(**inputs):
    global _NC_CACHE
    maps = _host_prep(inputs)
    if _NC_CACHE is None:
        _NC_CACHE = build()
    res = run_bass_kernel_spmd(_NC_CACHE, maps, core_ids=list(range(CORES)))
    out = np.empty((B, S, D), np.float32)
    for c in range(CORES):
        out[c * SPC:(c + 1) * SPC] = res.results[c]["xout"]
    return out


if __name__ == "__main__":
    import time
    t0 = time.time()
    nc = build()
    n = sum(1 for _ in nc.all_instructions())
    print("build s:", round(time.time() - t0, 1), "instrs:", n)


# version 10
# speedup vs baseline: 1.0641x; 1.0641x over previous
import contextlib
import numpy as np
import ml_dtypes
import concourse.bass as bass
import concourse.mybir as mybir
from concourse.bass_utils import run_bass_kernel_spmd
from concourse.alu_op_type import AluOpType as OP

F32 = mybir.dt.float32
BF16 = mybir.dt.bfloat16
AF = mybir.ActivationFunctionType
AX = mybir.AxisListType

B, S, D, H, DH, NE = 32, 2048, 1024, 8, 64, 8
CORES = 8
SPC = B // CORES
NCH, NG, NPAIR, FCH, ICH, DCH = 16, 4, 4, 16, 8, 8
EPS_LN = 1e-5
EPS_RMS = float(np.finfo(np.float32).eps)
GELU_A = 1.702
LIMIT = 7.0
bf16 = ml_dtypes.bfloat16

DBG_STOP = None
DBG_DUMP = True
DBG_NSEQ = None


class _CutDone(Exception):
    pass


ARENA_OFF = {}


def _host_prep(inp):
    f = np.float32
    wq = np.asarray(inp["L0_attn_wq"], f)
    wk = np.asarray(inp["L0_attn_wk"], f)
    wv = np.asarray(inp["L0_attn_wv"], f)
    wo = np.asarray(inp["L0_attn_wo"], f)
    wg = np.asarray(inp["L0_attn_w_gates"], f)
    rmsw = np.asarray(inp["L0_attn_rms_w"], f)
    nrmw = np.asarray(inp["L0_ffn_norm_w"], f)
    gw = np.asarray(inp["L0_ffn_gate_w"], f)
    gb = np.asarray(inp["L0_ffn_gate_b"], f)
    w1 = np.asarray(inp["L0_ffn_mlp1_w"], f)
    b1 = np.asarray(inp["L0_ffn_mlp1_b"], f)
    w2 = np.asarray(inp["L0_ffn_mlp2_w"], f)
    b2 = np.asarray(inp["L0_ffn_mlp2_b"], f)
    tok = np.asarray(inp["L0_attn_tokens"], f)

    def pairstack(w):
        a = np.stack([np.concatenate([w[2 * p], w[2 * p + 1]], axis=1) for p in range(NPAIR)])
        a = a.reshape(NPAIR, DCH, 128, 128).transpose(0, 2, 1, 3)
        return np.ascontiguousarray(a.reshape(NPAIR, 128, DCH * 128))

    wq_h, wk_h, wv_h = pairstack(wq), pairstack(wk), pairstack(wv)
    wo_h = np.ascontiguousarray(
        np.stack([wo[2 * p:2 * p + 2].reshape(128, 1024) for p in range(NPAIR)]))
    wg_h = np.ascontiguousarray(wg.reshape(DCH, 128, H).transpose(1, 0, 2).reshape(128, DCH * H))
    rms_h = np.ascontiguousarray(rmsw.reshape(DCH, 128).T)
    nrm_h = np.ascontiguousarray(nrmw.reshape(DCH, 128).T)
    gw_h = np.ascontiguousarray(gw.T.reshape(DCH, 128, NE).transpose(1, 0, 2).reshape(128, DCH * NE))
    gb_h = np.ascontiguousarray(gb.reshape(NE, 1))

    perm = np.empty(2048, np.int64)
    for fc in range(FCH):
        perm[fc * 128:(fc + 1) * 128] = 256 * (fc // 2) + 2 * np.arange(128) + (fc % 2)
    w1p = w1[:, perm, :]
    a = w1p.transpose(0, 2, 1).reshape(NE, DCH, 128, FCH, 128).transpose(0, 2, 3, 1, 4)
    w1_h = np.ascontiguousarray(a.reshape(NE, 128, FCH * DCH * 128))
    b1_h = np.ascontiguousarray(
        b1[:, perm].reshape(NE, FCH, 128).transpose(2, 0, 1).reshape(128, NE * FCH))
    a = w2.transpose(0, 2, 1).reshape(NE, ICH, 128, 1024).transpose(0, 2, 1, 3)
    w2_h = np.ascontiguousarray(a.reshape(NE, 128, ICH * 1024))
    b2_h = np.ascontiguousarray(b2)

    identb = np.eye(128, dtype=bf16)
    identf = np.eye(128, dtype=np.float32)
    p_ = np.arange(128)[:, None]
    q_ = np.arange(512)[None, :]
    maskt = np.concatenate([(128 * c + p_ <= q_) for c in range(4)], axis=1).astype(bf16)
    psel = np.zeros((8, 512), bf16)
    for p in range(NPAIR):
        psel[2 * p, p * 128:p * 128 + 64] = 1
        psel[2 * p + 1, p * 128 + 64:p * 128 + 128] = 1
    oneh = np.zeros((8, 1024), bf16)
    for e in range(NE):
        oneh[e, e * 128:(e + 1) * 128] = 1
    onescol = np.ones((128, 1), bf16)
    onesrow = np.ones((1, 128), bf16)

    shared = dict(wq=wq_h, wk=wk_h, wv=wv_h, wo=wo_h, wg=wg_h, rmsw=rms_h,
                  nrmw=nrm_h, gw=gw_h, gb=gb_h, w1=w1_h, b1=b1_h, w2=w2_h, b2=b2_h,
                  identb=identb, identf=identf, maskt=maskt, psel=psel, oneh=oneh,
                  onescol=onescol, onesrow=onesrow)
    maps = []
    for c in range(CORES):
        m = dict(shared)
        m["xin"] = np.ascontiguousarray(tok[c * SPC:(c + 1) * SPC])
        maps.append(m)
    return maps


class Sch:
    def __init__(self):
        self.sem, self.cnt, self.lastw = {}, {}, {}

    def add(self, n, h):
        self.sem[n] = h
        self.cnt[n] = 0

    def inc(self, instr, s, n=1):
        instr.then_inc(self.sem[s], n)
        self.cnt[s] += n
        return self.cnt[s]

    def tok(self, s):
        return self.cnt[s]

    def wait(self, eng, en, s, val):
        if not val or val <= 0:
            return
        k = (en, s)
        if self.lastw.get(k, 0) >= val:
            return
        eng.wait_ge(self.sem[s], val)
        self.lastw[k] = val

    def drain_inc(self, eng, s):
        eng.drain()
        i = eng.nop()
        i.then_inc(self.sem[s], 1)
        self.cnt[s] += 1
        return self.cnt[s]


class Res:
    def __init__(self):
        self.w = None
        self.r = {}

    def set_w(self, sem, tok, slack=0):
        self.w = (sem, tok, slack)
        self.r = {}

    def add_r(self, sem, tok):
        if self.r.get(sem, 0) < tok:
            self.r[sem] = tok


def build():
    nc = bass.Bass("TRN2", target_bir_lowering=False)
    dt = nc.dram_tensor
    xin = dt("xin", [SPC, S, D], F32, kind="ExternalInput").ap()
    xout = dt("xout", [SPC, S, D], F32, kind="ExternalOutput").ap()
    d_wq = dt("wq", [NPAIR, 128, 1024], F32, kind="ExternalInput").ap()
    d_wk = dt("wk", [NPAIR, 128, 1024], F32, kind="ExternalInput").ap()
    d_wv = dt("wv", [NPAIR, 128, 1024], F32, kind="ExternalInput").ap()
    d_wo = dt("wo", [NPAIR, 128, 1024], F32, kind="ExternalInput").ap()
    d_wg = dt("wg", [128, 64], F32, kind="ExternalInput").ap()
    d_rms = dt("rmsw", [128, 8], F32, kind="ExternalInput").ap()
    d_nrm = dt("nrmw", [128, 8], F32, kind="ExternalInput").ap()
    d_gw = dt("gw", [128, 64], F32, kind="ExternalInput").ap()
    d_gb = dt("gb", [8, 1], F32, kind="ExternalInput").ap()
    d_w1 = dt("w1", [NE, 128, 16384], F32, kind="ExternalInput").ap()
    d_b1 = dt("b1", [128, 128], F32, kind="ExternalInput").ap()
    d_w2 = dt("w2", [NE, 128, 8192], F32, kind="ExternalInput").ap()
    d_b2 = dt("b2", [8, 1024], F32, kind="ExternalInput").ap()
    d_idb = dt("identb", [128, 128], BF16, kind="ExternalInput").ap()
    d_idf = dt("identf", [128, 128], F32, kind="ExternalInput").ap()
    d_mask = dt("maskt", [128, 2048], BF16, kind="ExternalInput").ap()
    d_psel = dt("psel", [8, 512], BF16, kind="ExternalInput").ap()
    d_oneh = dt("oneh", [8, 1024], BF16, kind="ExternalInput").ap()
    d_onec = dt("onescol", [128, 1], BF16, kind="ExternalInput").ap()
    d_oner = dt("onesrow", [1, 128], BF16, kind="ExternalInput").ap()
    d_zscr = dt("zscr", [4096], F32, kind="Internal").ap()
    d_dbg = dt("dbg", [128, 28800], F32, kind="ExternalOutput").ap() if DBG_STOP else None

    with contextlib.ExitStack() as es:
        sbt = lambda n, sh, d_: es.enter_context(nc.sbuf_tensor(n, sh, d_))
        pst = lambda n, sh, d_: es.enter_context(nc.psum_tensor(n, sh, d_))

        x_sb = sbt("x_sb", [128, NCH * 1024], F32)
        identb = sbt("s_identb", [128, 128], BF16)
        identf = sbt("s_identf", [128, 128], F32)
        maskt = sbt("s_maskt", [128, 2048], BF16)
        psel = sbt("s_psel", [8, 512], BF16)
        oneh = sbt("s_oneh", [8, 1024], BF16)
        onescol = sbt("s_onescol", [128, 1], BF16)
        onesrow = sbt("s_onesrow", [1, 128], BF16)
        rms_sb = sbt("rms_sb", [128, 8], F32)
        nrm_sb = sbt("nrm_sb", [128, 8], F32)
        wg_sb = sbt("wg_sb", [128, 64], BF16)
        gw_sb = sbt("gw_sb", [128, 64], BF16)
        gb_sb = sbt("gb_sb", [8, 1], F32)
        b1_sb = sbt("b1_sb", [128, 128], F32)
        b2_sb = sbt("b2_sb", [8, 1024], BF16)
        b2f = sbt("b2f", [8, 1024], F32)
        sums = sbt("sums", [128, 16], F32)
        sqs = sbt("sqs", [128, 16], F32)
        nm = sbt("nm", [128, 16], F32)
        msq = sbt("msq", [128, 16], F32)
        nv = sbt("nv", [128, 16], F32)
        vpe = sbt("vpe", [128, 16], F32)
        lnv = sbt("lnv", [128, 16], F32)
        rstd = sbt("rstd", [128, 16], F32)
        rsq = sbt("rsq", [128, 16], F32)
        vv = sbt("vv", [128, 16], F32)
        mt2 = sbt("mt2", [128, 16], F32)
        lnw = sbt("lnw", [128, 16], F32)
        alphav = sbt("alphav", [128, 16], F32)
        cc1 = sbt("cc1", [128, 16], F32)
        cc2 = sbt("cc2", [128, 16], F32)
        mcol = sbt("mcol", [1, 16], F32)
        msc = sbt("msc", [1, 4], F32)
        nmb1 = sbt("nmb1", [1, 1], BF16)
        negM = sbt("negM", [128, 1], F32)
        lgtm = sbt("lgtm", [128, 128], F32)
        rm1 = sbt("rm1", [128, 16], F32)
        rm2 = sbt("rm2", [128, 16], F32)
        req1 = sbt("req1", [128, 128], F32)
        rl2 = sbt("rl2", [128, 128], F32)
        req2 = sbt("req2", [128, 128], F32)
        rd12 = sbt("rd12", [128, 16], F32)
        rw1 = sbt("rw1", [128, 16], F32)
        rw2 = sbt("rw2", [128, 16], F32)
        rwa = sbt("rwa", [128, 128], F32)
        rwf = sbt("rwf", [128, 128], F32)
        sqscr = sbt("sqscr", [128, 512], F32)
        dscr = sbt("dscr", [128, 64], F32)

        AR_COLS = 61440
        arena = sbt("arena", [128, AR_COLS], BF16)

        class Arena:
            def __init__(self):
                self.off = 0

            def take(self, cols):
                cols = (cols + 31) & ~31
                assert self.off + cols <= AR_COLS, f"arena overflow {self.off}+{cols}"
                o = self.off
                self.off += cols
                return o

            def b16(self, cols, parts=128, name=None):
                o = self.take(cols)
                if name:
                    ARENA_OFF[name] = (o, cols, 'b16', parts)
                return arena[0:parts, o:o + cols]

            def f32(self, cols, parts=128, name=None):
                o = self.take(2 * cols)
                if name:
                    ARENA_OFF[name] = (o, 2 * cols, 'f32', parts)
                return arena[0:parts, o:o + 2 * cols].bitcast(F32)

        A = Arena()
        tbf = A.b16(4096, name='tbf')
        tT = A.b16(16384, name='tT')
        mark = A.off
        qp = A.b16(2048, name='qp')
        kp = A.b16(2048, name='kp')
        vp = A.b16(2080, name='vp')
        q2 = A.b16(2048, name='q2')
        wqkv = A.b16(1024, name='wqkv')
        woall = A.b16(4096, name='woall')
        gTb = A.b16(2048, parts=8, name='gTb')
        outT = A.b16(8192, name='outT')
        Ebufs = [A.b16(512, name=f'E{_i}') for _i in range(6)]
        zst = A.b16(4096, parts=1, name='zst')
        zbuf = A.b16(2048, parts=8, name='zbuf')
        crec8 = A.f32(2048, parts=8, name='crec')
        cgb = A.b16(2048, parts=8, name='cgb')
        attn_end = A.off
        A.off = mark
        w1b = A.b16(16384, name='w1b')
        w2b = A.b16(8192, name='w2b')
        actb = [A.b16(4096, name='act0'), tbf]
        wbr = [A.b16(512, name='wbr0'), A.b16(512, name='wbr1')]
        g1p = [A.f32(512, name='g1a'), A.f32(512, name='g1b')]
        l1p = [A.f32(512, name='l1a'), A.f32(512, name='l1b')]
        lcb2 = [A.f32(512, name='lcba'), A.f32(512, name='lcbb')]
        p1 = A.f32(512, name='p1')
        a0 = A.f32(512, name='a0')
        lgt = A.f32(512, parts=8, name='lgt')
        wTb = A.b16(2048, parts=8, name='wTb')
        ffn_end = A.off
        wqkvB = arena[0:128, ARENA_OFF['lgt'][0]:ARENA_OFF['lgt'][0] + 1024]
        wbuf = [wqkv, wqkvB]
        wbufn = ["wqkv", "wqkvB"]
        assert max(attn_end, ffn_end) <= AR_COLS, (attn_end, ffn_end)

        pb = [pst(f"pb{i}", [128, 512], F32) for i in range(7)]
        pbt = pst("pbt", [128, 1024], BF16)

        sch = Sch()
        for n in ["Cs", "Cg", "Cx", "Cq", "Cz1", "Cz2", "Cw1", "Cw2",
                  "Pe", "Ac", "Dv", "Pl", "So"]:
            sch.add(n, es.enter_context(nc.semaphore(n)))
        block = es.enter_context(nc.Block())
        ENG = {"PE": block.tensor, "VE": block.vector, "SE": block.scalar,
               "GE": block.gpsimd, "SY": block.sync}
        SEM = {"PE": "Pe", "VE": "Dv", "SE": "Ac", "GE": "Pl"}

        R = {}
        for name in (["wqkvB", "lgt2"]
                     + ["x", "tbf", "tT", "qp", "kp", "vp", "q2", "wo", "gT", "outT",
                      "zst", "zbuf", "crec", "cgb", "w1", "w2", "act", "g1", "lc",
                      "p1", "a0", "lgt", "wTb", "lgtm", "rwf", "stats", "mh",
                      "negM", "b2", "rtr", "wqkv", "sqscr"]
                     + [f"pb{i}" for i in range(7)] + ["pbt"]
                     + [f"wbr{i}" for i in range(2)]):
            R[name] = Res()

        def sec(en, fn, reads=(), writes=(), war_extra=()):
            sem = SEM.get(en)

            def body(eng):
                for rn in reads:
                    rr = R[rn]
                    if rr.w and rr.w[0] != sem:
                        sch.wait(eng, en, rr.w[0], rr.w[1] + rr.w[2])
                for wn in writes:
                    rr = R[wn]
                    for s_, t_ in rr.r.items():
                        if s_ != sem:
                            sch.wait(eng, en, s_, t_)
                    if rr.w and rr.w[0] != sem:
                        sch.wait(eng, en, rr.w[0], rr.w[1])
                for s_, t_ in war_extra:
                    sch.wait(eng, en, s_, t_)
                out = fn(eng)
                if out is not None and sem is not None:
                    sch.inc(out, sem)
            ENG[en](body)
            tok = sch.tok(sem) if sem else None
            for rn in reads:
                if sem:
                    R[rn].add_r(sem, tok)
            for wn in writes:
                if sem:
                    R[wn].set_w(sem, tok, 1 if en in ("VE", "SE", "GE") else 0)
            return tok

        def seal(en):
            sem = SEM[en]

            def body(eng):
                sch.drain_inc(eng, sem)
            ENG[en](body)
            t = sch.tok(sem)
            for rr in R.values():
                if rr.w and rr.w[0] == sem and rr.w[2] and rr.w[1] < t:
                    rr.w = (sem, t, 0)
            return t

        def _(s):
            for dst, src in ((identf, d_idf), (rms_sb, d_rms), (nrm_sb, d_nrm),
                             (gb_sb, d_gb), (b1_sb, d_b1), (b2f, d_b2)):
                i = s.dma_start(out=dst[:], in_=src[:, :])
                sch.inc(i, "Cs", 16)
        ENG["SY"](_)
        lx0 = sch.tok("Cs")

        def _(g):
            for dst, src in ((identb, d_idb), (maskt, d_mask), (psel, d_psel),
                             (oneh, d_oneh), (onescol, d_onec), (onesrow, d_oner),
                             (wg_sb, d_wg), (gw_sb, d_gw)):
                i = g.dma_start(out=dst[:], in_=src[:, :])
                sch.inc(i, "Cg", 16)
        ENG["GE"](_)
        lw0 = sch.tok("Cg")

        def _(v):
            v.memset(dscr[:], 0.0)
            return v.tensor_scalar(b2_sb[:], b2f[:], GELU_A, None, OP.mult)
        sec("VE", _, writes=("b2",), war_extra=(("Cs", lx0),))
        for _en, _eng in (("PE", "tensor"), ("VE", "vector"), ("SE", "scalar"), ("GE", "gpsimd")):
            def _w(e, _en=_en):
                sch.wait(e, _en_cur[0], "Cs", lx0)
                sch.wait(e, _en_cur[0], "Cg", lw0)
            _en_cur = [_en]
            ENG[_en](_w)
        seal("VE")

        state = {"store": 0, "w1_last": 0, "w2_last": 0}

        def cut(label):
            if DBG_STOP == label:
                raise _CutDone()

        nseq = DBG_NSEQ if DBG_NSEQ is not None else SPC
        for s_i in range(nseq):
          try:
            if s_i == 0:
                def _(s, s_i=s_i):
                    sch.wait(s, "SY", "So", state["store"])
                    for s_, t_ in R["x"].r.items():
                        sch.wait(s, "SY", s_, t_)
                    i = s.dma_start(out=x_sb[:].rearrange("p (c d) -> p c d", c=NCH),
                                    in_=xin[s_i].rearrange("(c p) d -> p c d", p=128))
                    sch.inc(i, "Cx", 16)
                ENG["SY"](_)
                R["x"].set_w("Cx", sch.tok("Cx"))

            cut("load")
            def emit_ln(is_ln2, precomputed=False):
                if not precomputed:
                    def _(sc):
                        i = None
                        for c in range(NCH):
                            xc = x_sb[:, c * 1024:(c + 1) * 1024]
                            sc.activation(sqscr[:], xc[:, 0:512], AF.Square,
                                          accum_out=sqs[:, c:c + 1])
                            i = sc.activation(sqscr[:], xc[:, 512:1024], AF.Square,
                                              accum_out=msq[:, c:c + 1])
                        return i
                    sec("SE", _, reads=("x",), writes=("sqscr", "stats"))
                    sq_t = seal("SE")

                    def _(v):
                        i = None
                        for c in range(NCH):
                            i = v.reduce_sum(sums[:, c:c + 1],
                                             x_sb[:, c * 1024:(c + 1) * 1024], axis=AX.X)
                        return i
                    sec("VE", _, reads=("x",), writes=("stats",))
                    seal("VE")
                else:
                    sq_t = seal("SE")

                def _(v):
                    v.tensor_tensor(vv[:], sqs[:], msq[:], OP.add)
                    v.tensor_scalar(nm[:], sums[:], -1.0 / D, None, OP.mult)
                    v.drain()
                    v.tensor_scalar(msq[:], vv[:], 1.0 / D, None, OP.mult)
                    v.tensor_tensor(vpe[:], nm[:], nm[:], OP.mult)
                    v.drain()
                    v.tensor_tensor(nv[:], vpe[:], msq[:], OP.subtract)
                    v.drain()
                    return v.tensor_scalar(vpe[:], nv[:], -1.0, EPS_LN, OP.mult, OP.add)
                sec("VE", _, reads=("stats",), writes=("stats",), war_extra=(("Ac", sq_t),))
                seal("VE")

                def _(sc):
                    sc.activation(lnv[:], vpe[:], AF.Ln)
                    sc.drain()
                    sc.activation(rstd[:], lnv[:], AF.Exp, scale=-0.5)
                    return sc.activation(rsq[:], lnv[:], AF.Exp, scale=-1.0)
                sec("SE", _, reads=("stats",), writes=("stats",))
                seal("SE")

                def _(v):
                    v.tensor_scalar(vv[:], nv[:], -1.0, None, OP.mult)
                    v.drain()
                    v.tensor_tensor(mt2[:], vv[:], rsq[:], OP.mult)
                    v.drain()
                    return v.tensor_scalar(mt2[:], mt2[:], EPS_RMS, None, OP.add)
                sec("VE", _, reads=("stats",), writes=("stats",))
                seal("VE")

                def _(sc):
                    sc.activation(lnw[:], mt2[:], AF.Ln)
                    sc.drain()
                    return sc.activation(vv[:], lnw[:], AF.Exp, scale=-0.5)
                sec("SE", _, reads=("stats",), writes=("stats",))
                seal("SE")

                def _(v):
                    i = v.tensor_tensor(alphav[:], rstd[:], vv[:], OP.mult)
                    if is_ln2:
                        v.tensor_scalar(cc1[:], rstd[:], 1.0, None, OP.add)
                        i = v.tensor_tensor(cc2[:], nm[:], rstd[:], OP.mult)
                    return i
                sec("VE", _, reads=("stats",), writes=("stats",))
                seal("VE")

            def emit_apply_T(wvec_sb, rewrite, war_pe=0):
                for g in range(NG):
                    def _(v, g=g):
                        i = None
                        for k in range(4):
                            c = g * 4 + k
                            i = v.tensor_scalar(tbf[:, k * 1024:(k + 1) * 1024],
                                                x_sb[:, c * 1024:(c + 1) * 1024],
                                                nm[:, c:c + 1], alphav[:, c:c + 1],
                                                OP.add, OP.mult)
                        return i
                    sec("VE", _, reads=("stats", "x"), writes=("tbf",),
                        war_extra=((("Pe", war_pe),) if g == 0 and war_pe else ()))
                    seal("VE")

                    for dcp in range(4):
                        def _(t, g=g, dcp=dcp):
                            i = None
                            for half, dc in ((0, 2 * dcp), (512, 2 * dcp + 1)):
                                for k in range(4):
                                    i = t.transpose(
                                        pbt[:, half + k * 128:half + (k + 1) * 128],
                                        tbf[:, k * 1024 + dc * 128:k * 1024 + (dc + 1) * 128],
                                        identb[:])
                            return i
                        sec("PE", _, reads=("tbf",), writes=("pbt",))

                        def _(v, g=g, dcp=dcp):
                            i = None
                            for half, dc in ((0, 2 * dcp), (512, 2 * dcp + 1)):
                                i = v.tensor_scalar(
                                    tT[:, dc * 2048 + g * 512:dc * 2048 + (g + 1) * 512],
                                    pbt[:, half:half + 512], wvec_sb[:, dc:dc + 1],
                                    None, OP.mult)
                            return i
                        sec("VE", _, reads=("pbt",), writes=("tT",))

                    if rewrite:
                        def _(v, g=g):
                            i = None
                            for k in range(4):
                                c = g * 4 + k
                                i = v.tensor_scalar(x_sb[:, c * 1024:(c + 1) * 1024],
                                                    x_sb[:, c * 1024:(c + 1) * 1024],
                                                    cc1[:, c:c + 1], cc2[:, c:c + 1],
                                                    OP.mult, OP.add)
                            return i
                        sec("VE", _, reads=("stats",), writes=("x",))
                seal("VE")

            emit_ln(is_ln2=False, precomputed=(s_i > 0))
            cut("ln1")
            emit_apply_T(rms_sb, rewrite=False, war_pe=state["w2_last"])
            cut("applyT")

            g_pe0 = sch.tok("Pe")

            def _(t, g_pe0=g_pe0):
                rrt = R["tT"].w
                sch.wait(t, "PE", rrt[0], rrt[1] + rrt[2])
                for s_, t_ in R["pb6"].r.items():
                    sch.wait(t, "PE", s_, t_)
                for g in range(NG):
                    if g > 0:
                        sch.wait(t, "PE", "Ac", sch.tok("Ac") + g)
                    for dc in range(DCH):
                        i = t.matmul(pb[6][0:8, 0:512], wg_sb[:, dc * 8:(dc + 1) * 8],
                                     tT[:, dc * 2048 + g * 512:dc * 2048 + (g + 1) * 512],
                                     start=(dc == 0), stop=(dc == DCH - 1))
                    sch.inc(i, "Pe")
            ENG["PE"](_)

            def _(sc, g_pe0=g_pe0):
                for g in range(NG):
                    sch.wait(sc, "SE", "Pe", g_pe0 + g + 1)
                    i = sc.activation(gTb[:, g * 512:(g + 1) * 512],
                                      pb[6][0:8, 0:512], AF.Sigmoid)
                    sch.inc(i, "Ac")
                i = sc.copy(dscr[:, 16:32], dscr[:, 0:16])
                sch.inc(i, "Ac")
            ENG["SE"](_)
            R["gT"].set_w("Ac", sch.tok("Ac"), 0)
            R["pb6"].set_w("Pe", sch.tok("Pe"), 0)
            R["pb6"].add_r("Ac", sch.tok("Ac"))
            cut("gates")

            def _(g_, s_i=s_i):
                for s_, t_ in R["wo"].r.items():
                    sch.wait(g_, "GE", s_, t_)
                sch.wait(g_, "GE", "Pe", state["w1_last"])
                i = g_.dma_start(out=woall[:].rearrange("r (p c) -> r p c", p=NPAIR),
                                 in_=d_wo.rearrange("p r c -> r p c"))
                sch.inc(i, "Cq", 16)
                R["wo"].set_w("Cq", sch.cnt["Cq"])
            ENG["GE"](_)

            state["prev_av"] = 0
            for pair in range(NPAIR):
                state["attn_ep"] = sch.tok("Dv")
                def wdma(src, bufi, pair=pair):
                    def _(g_):
                        for s_, t_ in R[wbufn[bufi]].r.items():
                            sch.wait(g_, "GE", s_, t_)
                        if bufi == 1:
                            rl = R["lgt"]
                            if rl.w:
                                sch.wait(g_, "GE", rl.w[0], rl.w[1] + rl.w[2])
                            for s_, t_ in rl.r.items():
                                sch.wait(g_, "GE", s_, t_)
                        sch.wait(g_, "GE", "Pe", state["w1_last"])
                        i = g_.dma_start(out=wbuf[bufi][:], in_=src[pair])
                        sch.inc(i, "Cq", 16)
                        R[wbufn[bufi]].set_w("Cq", sch.cnt["Cq"])
                    ENG["GE"](_)

                if pair == 0:
                    wdma(d_wq, 0)
                    wdma(d_wk, 1)

                for which in (0, 1):
                    dstq = qp if which == 0 else kp
                    bres = "qp" if which == 0 else "kp"
                    wbi = (pair + which) % 2
                    qk_pe0 = sch.tok("Pe")
                    qk_ac0 = sch.tok("Ac")

                    def _(t, qk_pe0=qk_pe0, qk_ac0=qk_ac0, wbi=wbi):
                        rr = R[wbufn[wbi]].w
                        sch.wait(t, "PE", rr[0], rr[1])
                        rrt = R["tT"].w
                        sch.wait(t, "PE", rrt[0], rrt[1] + rrt[2])
                        for bn in ("pb5", "pb6"):
                            for s_, t_ in R[bn].r.items():
                                sch.wait(t, "PE", s_, t_)
                        for g in range(NG):
                            bk = pb[5 + (g % 2)]
                            if g >= 2:
                                sch.wait(t, "PE", "Ac", qk_ac0 + g - 1)
                            for dc in range(DCH):
                                i = t.matmul(bk[:, 0:512], wbuf[wbi][:, dc * 128:(dc + 1) * 128],
                                             tT[:, dc * 2048 + g * 512:dc * 2048 + (g + 1) * 512],
                                             start=(dc == 0), stop=(dc == DCH - 1))
                            sch.inc(i, "Pe")
                    ENG["PE"](_)

                    def _(sc, dstq=dstq, qk_pe0=qk_pe0):
                        for g in range(NG):
                            sch.wait(sc, "SE", "Pe", qk_pe0 + g + 1)
                            i = sc.copy(dstq[:, g * 512:(g + 1) * 512], pb[5 + (g % 2)][:, 0:512])
                            sch.inc(i, "Ac")
                    ENG["SE"](_)
                    R[wbufn[wbi]].add_r("Pe", sch.tok("Pe"))
                    if wbi == 1:
                        R["lgt"].add_r("Pe", sch.tok("Pe"))
                    R[bres].set_w("Ac", sch.tok("Ac"), 1)
                    R["pb5"].set_w("Pe", qk_pe0 + 3, 0)
                    R["pb6"].set_w("Pe", qk_pe0 + 4, 0)
                    R["pb5"].add_r("Ac", qk_ac0 + 3)
                    R["pb6"].add_r("Ac", qk_ac0 + 4)
                wdma(d_wv, pair % 2)

                vp_pe0 = sch.tok("Pe")
                vp_dv0 = sch.tok("Dv")

                def _(t, vp_pe0=vp_pe0, vp_dv0=vp_dv0, wbi=pair % 2):
                    rr = R[wbufn[wbi]].w
                    sch.wait(t, "PE", rr[0], rr[1])
                    for bn in ("pb4", "pb5"):
                        for s_, t_ in R[bn].r.items():
                            sch.wait(t, "PE", s_, t_)
                    for c in range(NCH):
                        bk = pb[4 + c % 2]
                        if c >= 2:
                            sch.wait(t, "PE", "Dv", vp_dv0 + c - 1)
                        for dc in range(DCH):
                            i = t.matmul(bk[:, 0:128],
                                         tT[:, dc * 2048 + c * 128:dc * 2048 + (c + 1) * 128],
                                         wbuf[pair % 2][:, dc * 128:(dc + 1) * 128],
                                         start=(dc == 0), stop=(dc == DCH - 1))
                        sch.inc(i, "Pe")
                ENG["PE"](_)

                def _(v, vp_pe0=vp_pe0):
                    for c in range(NCH):
                        bk = pb[4 + c % 2]
                        sch.wait(v, "VE", "Pe", vp_pe0 + c + 1)
                        v.tensor_copy(vp[:, c * 130:c * 130 + 64], bk[:, 0:64])
                        i = v.tensor_copy(vp[:, c * 130 + 65:c * 130 + 129], bk[:, 64:128])
                        sch.inc(i, "Dv")
                    v3 = vp[:].rearrange("p (cc j) -> p cc j", cc=NCH)
                    v.memset(v3[:, :, 64:65], 1.0)
                    v.memset(v3[:, :, 129:130], 1.0)
                    i = v.tensor_copy(dscr[:, 48:64].bitcast(F32) if False else dscr[:, 32:48], dscr[:, 0:16])
                    sch.inc(i, "Dv")
                ENG["VE"](_)
                R[wbufn[pair % 2]].add_r("Pe", sch.tok("Pe"))
                if pair % 2 == 1:
                    R["lgt"].add_r("Pe", sch.tok("Pe"))
                R["vp"].set_w("Dv", sch.tok("Dv"), 0)
                R["pb4"].set_w("Pe", vp_pe0 + 15, 0)
                R["pb5"].set_w("Pe", vp_pe0 + 16, 0)
                R["pb4"].add_r("Dv", vp_dv0 + 15)
                R["pb5"].add_r("Dv", vp_dv0 + 16)
                if pair < NPAIR - 1:
                    wdma(d_wq, (pair + 1) % 2, pair=pair + 1)
                    wdma(d_wk, pair % 2, pair=pair + 1)

                def _(sc):
                    return sc.activation(q2[:, 0:2048], qp[:], AF.Square)
                sec("SE", _, reads=("qp",), writes=("q2",))
                seal("SE")

                def _(t):
                    for g4 in range(4):
                        i = t.matmul(pb[g4][0:1, 0:512], onescol[:],
                                     q2[:, g4 * 512:(g4 + 1) * 512], start=True, stop=True)
                        sch.inc(i, "Pe")
                    return None
                sec("PE", _, reads=("q2",), writes=("pb0", "pb1", "pb2", "pb3"))
                nq_t = sch.tok("Pe")
                for g4 in range(4):
                    R[f"pb{g4}"].set_w("Pe", nq_t, 0)

                def _(v):
                    sch.wait(v, "VE", "Pe", nq_t)
                    i = None
                    for g4 in range(4):
                        i = v.reduce_max(mcol[0:1, g4:g4 + 1], pb[g4][0:1, 0:512], axis=AX.X)
                    return i
                sec("VE", _, reads=("pb0", "pb1", "pb2", "pb3"), writes=("mh",))
                seal("VE")

                def _(sc):
                    return sc.activation(q2[:, 0:2048], kp[:], AF.Square)
                sec("SE", _, reads=("kp", "q2"), writes=("q2",))
                seal("SE")

                def _(t):
                    for g4 in range(4):
                        i = t.matmul(pb[g4][0:1, 0:512], onescol[:],
                                     q2[:, g4 * 512:(g4 + 1) * 512], start=True, stop=True)
                        sch.inc(i, "Pe")
                    return None
                sec("PE", _, reads=("q2",), writes=("pb0", "pb1", "pb2", "pb3"))
                nk_t = sch.tok("Pe")
                for g4 in range(4):
                    R[f"pb{g4}"].set_w("Pe", nk_t, 0)

                def _(v):
                    sch.wait(v, "VE", "Pe", nk_t)
                    for g4 in range(4):
                        v.reduce_max(mcol[0:1, 8 + g4:9 + g4], pb[g4][0:1, 0:512], axis=AX.X)
                    sch.drain_inc(v, "Dv")
                    v.reduce_max(msc[0:1, 0:1], mcol[0:1, 0:4], axis=AX.X)
                    v.reduce_max(msc[0:1, 1:2], mcol[0:1, 8:12], axis=AX.X)
                    sch.drain_inc(v, "Dv")
                    v.drain()
                    return v.tensor_tensor(msc[0:1, 2:3], msc[0:1, 0:1], msc[0:1, 1:2], OP.mult)
                sec("VE", _, reads=("pb0", "pb1", "pb2", "pb3"), writes=("mh",))
                seal("VE")

                def _(sc):
                    return sc.activation(msc[0:1, 3:4], msc[0:1, 2:3], AF.Sqrt)
                sec("SE", _, reads=("mh",), writes=("mh",))
                seal("SE")

                def _(v):
                    return v.tensor_scalar(nmb1[0:1, 0:1], msc[0:1, 3:4], -0.125, None, OP.mult)
                sec("VE", _, reads=("mh",), writes=("mh",))
                seal("VE")

                def _(t):
                    return t.matmul(pb[6][:, 0:1], onesrow[0:1, :], nmb1[0:1, 0:1],
                                    start=True, stop=True)
                sec("PE", _, reads=("mh",), writes=("pb6",))

                def _(v):
                    return v.tensor_copy(negM[:], pb[6][:, 0:1])
                sec("VE", _, reads=("pb6",), writes=("negM",))
                negM_t = seal("VE")
                cut(f"mhat{pair}")

                for g in range(NG):
                    J = 4 * g + 4
                    d0 = 4 * g

                    def qw(j):
                        return max(0, (j - d0) * 128) if j >= d0 else 0

                    peops = []
                    for j in range(J):
                        peops.append(("s", j))
                        if j >= 3:
                            peops.append(("av", j - 3))
                    peops += [("av", J - 3), ("av", J - 2), ("av", J - 1)]
                    c_pe0 = sch.tok("Pe")
                    c_ac0 = sch.tok("Ac")
                    c_dv0 = sch.tok("Dv")
                    p_pe = {op: c_pe0 + k + 1 for k, op in enumerate(peops)}
                    p_ex = {j: c_ac0 + 2 * j + 1 for j in range(J)}
                    p_exd = {j: c_ac0 + 2 * j + 2 for j in range(J)}
                    p_mkd = {d0 + c: c_dv0 + 2 * (c + 1) for c in range(4)}

                    def _(t, g=g, J=J, d0=d0):
                        sch.wait(t, "PE", "Dv", state["attn_ep"])
                        sch.wait(t, "PE", "Dv", negM_t)
                        for kind, j in peops:
                            if kind == "s":
                                w = 512 - qw(j)
                                if j >= 2:
                                    sch.wait(t, "PE", "Ac", p_ex(j - 2) if False else p_ex[j - 2])
                                q0 = g * 512 + qw(j)
                                t.matmul(pb[j % 2][:, 0:w], kp[0:64, j * 128:(j + 1) * 128],
                                         qp[0:64, q0:(g + 1) * 512], start=True, stop=True)
                                i = t.matmul(pb[2 + j % 2][:, 0:w],
                                             kp[64:128, j * 128:(j + 1) * 128],
                                             qp[64:128, q0:(g + 1) * 512], start=True, stop=True)
                            else:
                                w = 512 - qw(j)
                                if j >= d0:
                                    sch.wait(t, "PE", "Dv", p_mkd[j])
                                else:
                                    sch.wait(t, "PE", "Ac", p_exd[j])
                                o0 = qw(j)
                                t.matmul(pb[4][0:65, o0:512], vp[:, j * 130:j * 130 + 65],
                                         Ebufs[j % 3][:, 0:w], start=(j == 0), stop=(j == J - 1))
                                i = t.matmul(pb[5][0:65, o0:512],
                                             vp[:, j * 130 + 65:j * 130 + 130],
                                             Ebufs[3 + j % 3][:, 0:w], start=(j == 0), stop=(j == J - 1))
                            sch.inc(i, "Pe")
                    ENG["PE"](_)
                    assert sch.tok("Pe") == c_pe0 + len(peops)

                    def _(sc, g=g, J=J):
                        sch.wait(sc, "SE", "Dv", negM_t)
                        for j in range(J):
                            w = 512 - qw(j)
                            sch.wait(sc, "SE", "Pe", p_pe[("s", j)])
                            if j >= 3:
                                sch.wait(sc, "SE", "Pe", p_pe[("av", j - 3)])
                            elif state["prev_av"]:
                                sch.wait(sc, "SE", "Pe", state["prev_av"])
                            sc.activation(Ebufs[j % 3][:, 0:w], pb[j % 2][:, 0:w], AF.Exp,
                                          bias=negM[:, 0:1], scale=0.125)
                            i = sc.activation(Ebufs[3 + j % 3][:, 0:w], pb[2 + j % 2][:, 0:w],
                                              AF.Exp, bias=negM[:, 0:1], scale=0.125)
                            sch.inc(i, "Ac")
                            i = sc.copy(dscr[:, 16:32], dscr[:, 0:16])
                            sch.inc(i, "Ac")
                    ENG["SE"](_)
                    assert sch.tok("Ac") == c_ac0 + 2 * J

                    def _(v, g=g, J=J, d0=d0, pair=pair):
                        for j in range(d0, J):
                            c = j - d0
                            w = 512 - 128 * c
                            sch.wait(v, "VE", "Ac", p_exd[j])
                            v.tensor_tensor(Ebufs[j % 3][:, 0:w], Ebufs[j % 3][:, 0:w],
                                            maskt[:, 0:w], OP.mult)
                            i = v.tensor_tensor(Ebufs[3 + j % 3][:, 0:w],
                                                Ebufs[3 + j % 3][:, 0:w],
                                                maskt[:, 0:w], OP.mult)
                            sch.inc(i, "Dv")
                            i = v.tensor_copy(dscr[:, 32:48], dscr[:, 0:16])
                            sch.inc(i, "Dv")
                        sch.wait(v, "VE", "Pe", p_pe[("av", J - 1)])
                        col = pair * 2048 + g * 512
                        zc = g * 1024
                        v.tensor_copy(outT[0:64, col:col + 512], pb[4][0:64, 0:512])
                        v.tensor_copy(outT[64:128, col:col + 512], pb[5][0:64, 0:512])
                        v.tensor_copy(zst[0:1, zc:zc + 512], pb[4][64:65, 0:512])
                        i = v.tensor_copy(zst[0:1, zc + 512:zc + 1024], pb[5][64:65, 0:512])
                        sch.inc(i, "Dv")
                        i = v.tensor_copy(dscr[:, 32:48], dscr[:, 0:16])
                        sch.inc(i, "Dv")
                    ENG["VE"](_)
                    assert sch.tok("Dv") == c_dv0 + 2 * 4 + 2
                    state["attn_ep"] = sch.tok("Dv")
                    state["prev_av"] = p_pe[("av", J - 1)]
                    cut(f"core{pair}_{g}")
                R["outT"].set_w("Dv", sch.tok("Dv"), 0)
                R["zst"].set_w("Dv", sch.tok("Dv"), 0)
                R["pb4"].set_w("Pe", state["prev_av"], 0)
                R["pb5"].set_w("Pe", state["prev_av"], 0)
                R["pb4"].add_r("Dv", state["attn_ep"])
                R["pb5"].add_r("Dv", state["attn_ep"])
                for jj in range(4):
                    R[f"pb{jj}"].set_w("Pe", state["prev_av"], 0)
                    R[f"pb{jj}"].add_r("Dv", state["attn_ep"])
                cut(f"preZ{pair}")
                R["negM"].add_r("Ac", sch.tok("Ac"))
                cut(f"preZ{pair}")

                def _(g_, pair=pair):
                    rr = R["zst"]
                    sch.wait(g_, "GE", rr.w[0], rr.w[1] + rr.w[2])
                    for s_, t_ in R["zbuf"].r.items():
                        sch.wait(g_, "GE", s_, t_)
                    sch.wait(g_, "GE", "Cz2", sch.cnt["Cz2"])
                    i = g_.dma_start(out=d_zscr.rearrange("(o w) -> o w", o=1), in_=zst[0:1, :])
                    sch.inc(i, "Cz1", 16)
                    g_.wait_ge(sch.sem["Cz1"], sch.cnt["Cz1"])
                    i = g_.dma_start(
                        out=zbuf[2 * pair:2 * pair + 2, :].rearrange("h (g w) -> h g w", g=4),
                        in_=d_zscr.rearrange("(g h w) -> h g w", g=4, h=2))
                    sch.inc(i, "Cz2", 16)
                    R["zst"].add_r("Cz1", sch.cnt["Cz1"])
                ENG["GE"](_)
                R["zbuf"].set_w("Cz2", sch.tok("Cz2"))
                cut(f"pair{pair}")

            def _(v):
                return v.reciprocal(crec8[:], zbuf[0:8, :])
            sec("VE", _, reads=("zbuf",), writes=("crec",))
            seal("VE")

            def _(v):
                return v.tensor_tensor(cgb[:], crec8[:], gTb[:], OP.mult)
            sec("VE", _, reads=("crec", "gT"), writes=("cgb",))
            cgb_t2 = seal("VE")

            state["scale_done"] = cgb_t2
            state["pred_done"] = cgb_t2
            for g in range(NG):
                s_pe0 = sch.tok("Pe")
                s_dv0 = sch.tok("Dv")

                def _(t, g=g, s_pe0=s_pe0, s_dv0=s_dv0):
                    sch.wait(t, "PE", "Dv", cgb_t2)
                    sch.wait(t, "PE", "Dv", state["attn_ep"])
                    for pair in range(NPAIR):
                        if pair >= 2:
                            sch.wait(t, "PE", "Dv", s_dv0 + pair - 1)
                        else:
                            sch.wait(t, "PE", "Dv", state["scale_done"])
                        i = t.matmul(pb[2 + pair % 2][:, 0:512],
                                     psel[:, pair * 128:(pair + 1) * 128],
                                     cgb[0:8, g * 512:(g + 1) * 512], start=True, stop=True)
                        sch.inc(i, "Pe")
                    for k in range(8):
                        ns, dh = divmod(k, 2)
                        sch.wait(t, "PE", "Dv", s_dv0 + 5)
                        if k >= 2:
                            sch.wait(t, "PE", "Dv", s_dv0 + 5 + k - 1)
                        else:
                            sch.wait(t, "PE", "Dv", state["pred_done"])
                        for pair in range(NPAIR):
                            i = t.matmul(pb[k % 2][:, 0:512],
                                         outT[:, pair * 2048 + g * 512 + ns * 128:
                                              pair * 2048 + g * 512 + (ns + 1) * 128],
                                         woall[:, pair * 1024 + dh * 512:
                                               pair * 1024 + (dh + 1) * 512],
                                         start=(pair == 0), stop=(pair == NPAIR - 1))
                        sch.inc(i, "Pe")
                ENG["PE"](_)

                def _(v, g=g, s_pe0=s_pe0):
                    for pair in range(NPAIR):
                        sch.wait(v, "VE", "Pe", s_pe0 + pair + 1)
                        col = pair * 2048 + g * 512
                        i = v.tensor_tensor(outT[:, col:col + 512], outT[:, col:col + 512],
                                            pb[2 + pair % 2][:, 0:512], OP.mult)
                        sch.inc(i, "Dv")
                    i = v.tensor_copy(dscr[:, 32:48], dscr[:, 0:16])
                    sch.inc(i, "Dv")
                    for k in range(8):
                        ns, dh = divmod(k, 2)
                        c = g * 4 + ns
                        sch.wait(v, "VE", "Pe", s_pe0 + 4 + k + 1)
                        xsl = x_sb[:, c * 1024 + dh * 512:c * 1024 + (dh + 1) * 512]
                        i = v.tensor_tensor(xsl, xsl, pb[k % 2][:, 0:512], OP.add)
                        sch.inc(i, "Dv")
                ENG["VE"](_)
                state["scale_done"] = s_dv0 + 5
                state["pred_done"] = s_dv0 + 13

                def _(sc, g=g, s_dv0=s_dv0):
                    i = None
                    for ns in range(4):
                        c = g * 4 + ns
                        sch.wait(sc, "SE", "Dv", s_dv0 + 7 + 2 * ns)
                        xc = x_sb[:, c * 1024:(c + 1) * 1024]
                        sc.activation(sqscr[:], xc[:, 0:512], AF.Square,
                                      accum_out=sqs[:, c:c + 1])
                        i = sc.activation(sqscr[:], xc[:, 512:1024], AF.Square,
                                          accum_out=msq[:, c:c + 1])
                    sch.inc(i, "Ac")
                ENG["SE"](_)
                R["x"].add_r("Ac", sch.tok("Ac"))

                def _(v, g=g):
                    i = None
                    for ns in range(4):
                        c = g * 4 + ns
                        i = v.reduce_sum(sums[:, c:c + 1],
                                         x_sb[:, c * 1024:(c + 1) * 1024], axis=AX.X)
                    sch.inc(i, "Dv")
                ENG["VE"](_)
                R["x"].add_r("Dv", sch.tok("Dv"))
            def _(v):
                sch.drain_inc(v, "Dv")
            ENG["VE"](_)
            R["x"].set_w("Dv", sch.tok("Dv"), 0)
            state["pred_dv"] = sch.tok("Dv")
            cut("pred")
            emit_ln(is_ln2=True, precomputed=True)
            emit_apply_T(nrm_sb, rewrite=True)
            hT_t = sch.tok("Dv")
            cut("ln2")

            for g in range(NG):
                def _(t, g=g):
                    i = None
                    for dc in range(DCH):
                        i = t.matmul(pb[6][0:8, 0:512], gw_sb[:, dc * 8:(dc + 1) * 8],
                                     tT[:, dc * 2048 + g * 512:dc * 2048 + (g + 1) * 512],
                                     start=(dc == 0), stop=(dc == DCH - 1))
                    return i
                sec("PE", _, reads=("tT",), writes=("pb6",))

                def _(v, g=g):
                    return v.tensor_scalar(lgt[:], pb[6][0:8, 0:512], gb_sb[0:8, 0:1],
                                           None, OP.add)
                sec("VE", _, reads=("pb6",), writes=("lgt",))
                seal("VE")

                def _(t, g=g):
                    i = None
                    for k in range(4):
                        i = t.transpose(pb[5][:, k * 8:(k + 1) * 8],
                                        lgt[0:8, k * 128:(k + 1) * 128], identf[0:8, 0:8])
                    return i
                sec("PE", _, reads=("lgt",), writes=("pb5",))

                def _(v, g=g):
                    return v.tensor_copy(lgtm[:, g * 32:g * 32 + 32], pb[5][:, 0:32])
                sec("VE", _, reads=("pb5",), writes=("lgtm",))
            seal("VE")

            def _(v):
                L3 = lgtm[:].rearrange("p (s e) -> p s e", e=8)
                v.tensor_reduce(rm1[:], L3, axis=AX.X, op=OP.max)
                sch.drain_inc(v, "Dv")
                v.tensor_tensor(req1[:].rearrange("p (s e) -> p s e", e=8), L3,
                                rm1[:].rearrange("p s -> p s ()").broadcast_to([128, 16, 8]),
                                OP.is_equal)
                v.drain()
                v.scalar_tensor_tensor(rl2[:], req1[:], -1e30, lgtm[:], OP.mult, OP.add)
                sch.drain_inc(v, "Dv")
                v.tensor_reduce(rm2[:], rl2[:].rearrange("p (s e) -> p s e", e=8),
                                axis=AX.X, op=OP.max)
                sch.drain_inc(v, "Dv")
                v.tensor_tensor(req2[:].rearrange("p (s e) -> p s e", e=8),
                                rl2[:].rearrange("p (s e) -> p s e", e=8),
                                rm2[:].rearrange("p s -> p s ()").broadcast_to([128, 16, 8]),
                                OP.is_equal)
                v.drain()
                return v.tensor_tensor(rd12[:], rm1[:], rm2[:], OP.subtract)
            sec("VE", _, reads=("lgtm",), writes=("rtr",))
            seal("VE")

            def _(sc):
                return sc.activation(rw1[:], rd12[:], AF.Sigmoid)
            sec("SE", _, reads=("rtr",), writes=("rtr",))
            seal("SE")

            def _(v):
                v.tensor_scalar(rw2[:], rw1[:], -1.0, 1.0, OP.mult, OP.add)
                v.drain()
                v.tensor_tensor(rwa[:].rearrange("p (s e) -> p s e", e=8),
                                req1[:].rearrange("p (s e) -> p s e", e=8),
                                rw1[:].rearrange("p s -> p s ()").broadcast_to([128, 16, 8]),
                                OP.mult)
                v.tensor_tensor(rwf[:].rearrange("p (s e) -> p s e", e=8),
                                req2[:].rearrange("p (s e) -> p s e", e=8),
                                rw2[:].rearrange("p s -> p s ()").broadcast_to([128, 16, 8]),
                                OP.mult)
                v.drain()
                return v.tensor_tensor(rwf[:], rwf[:], rwa[:], OP.add)
            sec("VE", _, reads=("rtr",), writes=("rwf",))
            seal("VE")

            for g in range(NG):
                def _(t, g=g):
                    i = None
                    for k in range(4):
                        sub = g * 4 + k
                        i = t.transpose(pb[5][0:8, k * 128:(k + 1) * 128],
                                        rwf[:, sub * 8:(sub + 1) * 8], identf[:])
                    return i
                sec("PE", _, reads=("rwf",), writes=("pb5",))

                def _(v, g=g):
                    return v.tensor_scalar(wTb[:, g * 512:(g + 1) * 512], pb[5][0:8, 0:512],
                                           1.0 / GELU_A, None, OP.mult)
                sec("VE", _, reads=("pb5",), writes=("wTb",))
            seal("VE")
            cut("router")

            wtb_t2 = sch.tok("Dv")
            NT = NE * NG
            state["t1war"] = [("Dv", wtb_t2)] * 4
            state["m2war"] = [wtb_t2, wtb_t2]
            state["pb6war"] = ("Dv", wtb_t2)
            state["wbrR"] = [0, 0]
            state["actR"] = [0, 0]
            prev = {}
            w1tok, w2tok = {}, {}
            cx_toks = {}

            def _(s):
                sch.wait(s, "GE", "Dv", state["pred_dv"])
                sch.wait(s, "GE", "Pe", state["w1_last"])
                i = s.dma_start(out=w1b[:, 0:8192], in_=d_w1[0][:, 0:8192])
                sch.inc(i, "Cw1", 16)
                i = s.dma_start(out=w1b[:, 8192:16384], in_=d_w1[0][:, 8192:16384])
                sch.inc(i, "Cw1", 16)
                sch.wait(s, "GE", "Pe", state["w2_last"])
                i = s.dma_start(out=w2b[:], in_=d_w2[0])
                sch.inc(i, "Cw2", 16)
            ENG["GE"](_)
            w1tok[0] = (sch.tok("Cw1") - 16, sch.tok("Cw1"))
            w2tok[0] = sch.tok("Cw2")

            for n in range(NT + 1):
                cur = n if n < NT else None
                prv = n - 1 if n >= 1 else None
                par = n % 2
                if cur is not None:
                    e_c, t_c = divmod(cur, NG)

                f_pe0 = sch.tok("Pe")
                f_ac0 = sch.tok("Ac")
                f_dv0 = sch.tok("Dv")
                f_pl0 = sch.tok("Pl")
                xacc_base = f_dv0 + (32 if cur is not None else 0)

                def p_eg(c, b=f_ac0):
                    return b + (2 if c == 0 else 3 * c + 1)

                def p_el(c, b=f_ac0):
                    return b + (3 if c == 0 else 3 * c + 2)

                def p_si(c, b=f_ac0):
                    return b + (3 * c + 6 if c <= 6 else 26)

                def p_lcb(c, b=f_dv0):
                    return b + (2 if c == 0 else 4 * c)

                def p_a0(c, b=f_dv0):
                    return b + (4 * (c + 1) + 1 if c <= 6 else 31)

                def p_a0d(c, b=f_dv0):
                    return b + (4 * (c + 1) + 2 if c <= 6 else 32)

                m2_new = []

                def _(t, cur=cur, prv=prv, xacc_base=xacc_base, f_pe0=f_pe0,
                      p_eg=p_eg, p_el=p_el, f_pl0=f_pl0, m2_new=m2_new):
                    if cur is not None:
                        e, t_i = divmod(cur, NG)
                        sch.wait(t, "PE", "Dv", wtb_t2)
                        sch.wait(t, "PE", state["pb6war"][0], state["pb6war"][1])
                        i = t.matmul(pb[6][:, 0:512], oneh[:, e * 128:(e + 1) * 128],
                                     wTb[0:8, t_i * 512:(t_i + 1) * 512],
                                     start=True, stop=True)
                        sch.inc(i, "Pe")
                        sch.wait(t, "PE", "Cw1", w1tok[e][0])
                        rrt = R["tT"].w
                        sch.wait(t, "PE", rrt[0], rrt[1] + rrt[2])
                        for c in range(ICH):
                            if c == 4:
                                sch.wait(t, "PE", "Cw1", w1tok[e][1])
                            for which in (0, 1):
                                fc = 2 * c + which
                                bi = (c % 2) if which == 0 else 2 + c % 2
                                bk = pb[bi]
                                if c >= 2:
                                    sch.wait(t, "PE", "Ac",
                                             p_eg(c - 2) if which == 0 else p_el(c - 2))
                                else:
                                    s_, t_ = state["t1war"][bi]
                                    sch.wait(t, "PE", s_, t_)
                                for dc in range(DCH):
                                    i = t.matmul(
                                        bk[:, 0:512],
                                        w1b[:, fc * 1024 + dc * 128:fc * 1024 + (dc + 1) * 128],
                                        tT[:, dc * 2048 + t_i * 512:dc * 2048 + (t_i + 1) * 512],
                                        start=(dc == 0), stop=(dc == DCH - 1))
                                sch.inc(i, "Pe")
                    if prv is not None:
                        ep, tp = divmod(prv, NG)
                        sch.wait(t, "PE", "Cw2", w2tok[ep])
                        sch.wait(t, "PE", "Pl", f_pl0)
                        for k in range(8):
                            ns, dh = divmod(k, 2)
                            c2 = tp * 4 + ns
                            bk = pb[4 + k % 2]
                            if k >= 2:
                                sch.wait(t, "PE", "Dv", xacc_base + (k - 2) + 1)
                            else:
                                sch.wait(t, "PE", "Dv", state["m2war"][k % 2])
                            first = True
                            if ep == 0:
                                t.matmul(bk[:, 0:512], wTb[0:8, c2 * 128:(c2 + 1) * 128],
                                         b2_sb[0:8, dh * 512:(dh + 1) * 512],
                                         start=True, stop=False)
                                first = False
                            for ic in range(ICH):
                                i = t.matmul(
                                    bk[:, 0:512],
                                    actb[prv % 2][:, ic * 512 + ns * 128:ic * 512 + (ns + 1) * 128],
                                    w2b[:, ic * 1024 + dh * 512:ic * 1024 + (dh + 1) * 512],
                                    start=(first and ic == 0), stop=(ic == ICH - 1))
                                first = False
                            sch.inc(i, "Pe")
                            m2_new.append(sch.tok("Pe"))
                ENG["PE"](_)
                exp_pe = (1 + 16 if cur is not None else 0) + (8 if prv is not None else 0)
                assert sch.tok("Pe") == f_pe0 + exp_pe

                if cur is not None:
                    def _(sc, cur=cur, par=par, f_pe0=f_pe0, f_ac0=f_ac0,
                          p_lcb=p_lcb, p_a0=p_a0, prev=prev):
                        e, t_i = divmod(cur, NG)
                        sch.wait(sc, "SE", "Pe", f_pe0 + 1)
                        sch.wait(sc, "SE", "Pl", state["wbrR"][par])
                        i = sc.copy(wbr[par][:], pb[6][:, 0:512])
                        sch.inc(i, "Ac")

                        def eg(c):
                            bgc = b1_sb[:, e * 16 + 2 * c:e * 16 + 2 * c + 1]
                            sch.wait(sc, "SE", "Pe", f_pe0 + 2 + 2 * c)
                            if c >= 2:
                                sch.wait(sc, "SE", "Dv", p_lcb(c - 2))
                            elif prev:
                                sch.wait(sc, "SE", "Dv", prev["lcb"][c])
                            i = sc.activation(g1p[c % 2][:], pb[c % 2][:, 0:512],
                                              AF.Identity, bias=bgc)
                            sch.inc(i, "Ac")

                        def el(c):
                            blc = b1_sb[:, e * 16 + 2 * c + 1:e * 16 + 2 * c + 2]
                            sch.wait(sc, "SE", "Pe", f_pe0 + 3 + 2 * c)
                            if c >= 2:
                                sch.wait(sc, "SE", "Dv", p_lcb(c - 2))
                            elif prev:
                                sch.wait(sc, "SE", "Dv", prev["lcb"][c])
                            i = sc.activation(l1p[c % 2][:], pb[2 + c % 2][:, 0:512],
                                              AF.Identity, bias=blc)
                            sch.inc(i, "Ac")

                        def si(c):
                            sch.wait(sc, "SE", "Dv", p_lcb(c))
                            if c >= 1:
                                sch.wait(sc, "SE", "Dv", p_a0(c - 1))
                            elif prev:
                                sch.wait(sc, "SE", "Dv", prev["a07"])
                            i = sc.activation(p1[:], g1p[c % 2][:], AF.Silu, scale=GELU_A)
                            sch.inc(i, "Ac")

                        eg(0); el(0)
                        for c in range(1, ICH):
                            eg(c); el(c); si(c - 1)
                        i = sc.copy(dscr[:, 16:32], dscr[:, 0:16])
                        sch.inc(i, "Ac")
                        si(ICH - 1)
                        i = sc.copy(dscr[:, 16:32], dscr[:, 0:16])
                        sch.inc(i, "Ac")
                    ENG["SE"](_)
                    assert sch.tok("Ac") == f_ac0 + 27

                def _(v, cur=cur, prv=prv, f_pl0=f_pl0, m2_new=m2_new,
                      p_eg=p_eg, p_el=p_el, p_si=p_si):
                    if cur is not None:
                        def minglcb(c):
                            sch.wait(v, "VE", "Ac", p_eg(c) + 1)
                            i = v.tensor_scalar(g1p[c % 2][:], g1p[c % 2][:],
                                                LIMIT, None, OP.min)
                            sch.inc(i, "Dv")
                            sch.wait(v, "VE", "Ac", p_el(c) + 1)
                            i = v.tensor_scalar(lcb2[c % 2][:], l1p[c % 2][:],
                                                LIMIT, -LIMIT, OP.min, OP.max)
                            sch.inc(i, "Dv")

                        def a0op(c):
                            sch.wait(v, "VE", "Ac", p_si(c) + 1)
                            sch.wait(v, "VE", "Pl", f_pl0 + c)
                            i = v.scalar_tensor_tensor(a0[:], lcb2[c % 2][:], 1.0,
                                                       p1[:], OP.add, OP.mult)
                            sch.inc(i, "Dv")
                            i = v.tensor_copy(dscr[:, 32:48], dscr[:, 0:16])
                            sch.inc(i, "Dv")

                        minglcb(0)
                        for c in range(1, ICH):
                            minglcb(c)
                            a0op(c - 1)
                        a0op(ICH - 1)
                    if prv is not None:
                        tp = prv % NG
                        for k in range(8):
                            ns, dh = divmod(k, 2)
                            c2 = tp * 4 + ns
                            sch.wait(v, "VE", "Pe", m2_new[k])
                            xsl = x_sb[:, c2 * 1024 + dh * 512:c2 * 1024 + (dh + 1) * 512]
                            i = v.tensor_tensor(xsl, xsl, pb[4 + k % 2][:, 0:512], OP.add)
                            sch.inc(i, "Dv")
                ENG["VE"](_)
                exp_dv = (32 if cur is not None else 0) + (8 if prv is not None else 0)
                assert sch.tok("Dv") == f_dv0 + exp_dv

                if prv is not None and prv >= NT - NG:
                    tq = prv - (NT - NG)

                    def _(v):
                        sch.drain_inc(v, "Dv")
                    ENG["VE"](_)
                    seal_t = sch.tok("Dv")

                    def _(s, tq=tq, s_i=s_i, seal_t=seal_t):
                        sch.wait(s, "SY", "Dv", seal_t)
                        i = s.dma_start(
                            out=xout[s_i][4 * tq * 128:(4 * tq + 4) * 128]
                                .rearrange("(c p) d -> p c d", p=128),
                            in_=x_sb[:, 4 * tq * 1024:(4 * tq + 4) * 1024]
                                .rearrange("p (c d) -> p c d", c=4))
                        sch.inc(i, "So", 16)
                        if s_i < nseq - 1:
                            sch.wait(s, "SY", "So", sch.cnt["So"])
                            i = s.dma_start(
                                out=x_sb[:, 4 * tq * 1024:(4 * tq + 4) * 1024]
                                    .rearrange("p (c d) -> p c d", c=4),
                                in_=xin[s_i + 1][4 * tq * 128:(4 * tq + 4) * 128]
                                    .rearrange("(c p) d -> p c d", p=128))
                            sch.inc(i, "Cx", 16)
                    ENG["SY"](_)
                    if s_i < nseq - 1:
                        cx_toks[tq] = sch.tok("Cx")
                        R["x"].set_w("Cx", cx_toks[tq])

                if prv == NT - 1 and s_i < nseq - 1:
                    def _(sc, cx_toks=cx_toks):
                        i = None
                        for tq in range(NG):
                            sch.wait(sc, "SE", "Cx", cx_toks[tq])
                            for ns in range(4):
                                c = tq * 4 + ns
                                xc = x_sb[:, c * 1024:(c + 1) * 1024]
                                sc.activation(sqscr[:], xc[:, 0:512], AF.Square,
                                              accum_out=sqs[:, c:c + 1])
                                i = sc.activation(sqscr[:], xc[:, 512:1024], AF.Square,
                                                  accum_out=msq[:, c:c + 1])
                        sch.inc(i, "Ac")
                    ENG["SE"](_)
                    R["x"].add_r("Ac", sch.tok("Ac"))

                    def _(v, cx_toks=cx_toks):
                        i = None
                        for tq in range(NG):
                            sch.wait(v, "VE", "Cx", cx_toks[tq])
                            for ns in range(4):
                                c = tq * 4 + ns
                                i = v.reduce_sum(sums[:, c:c + 1],
                                                 x_sb[:, c * 1024:(c + 1) * 1024],
                                                 axis=AX.X)
                        sch.inc(i, "Dv")
                    ENG["VE"](_)
                    R["x"].add_r("Dv", sch.tok("Dv"))

                if cur is not None:
                    def _(p, cur=cur, par=par, f_ac0=f_ac0, p_a0d=p_a0d):
                        sch.wait(p, "GE", "Ac", f_ac0 + 2)
                        sch.wait(p, "GE", "Pe", state["actR"][par])
                        for c in range(ICH):
                            sch.wait(p, "GE", "Dv", p_a0d(c))
                            i = p.tensor_tensor(actb[par][:, c * 512:(c + 1) * 512],
                                                a0[:], wbr[par][:], OP.mult)
                            sch.inc(i, "Pl")
                        i = p.tensor_copy(dscr[:, 48:64], dscr[:, 0:16])
                        sch.inc(i, "Pl")
                    ENG["GE"](_)
                    assert sch.tok("Pl") == f_pl0 + 9

                if cur is not None and t_c == NG - 1 and e_c < NE - 1:
                    def _(s, e_c=e_c, f_pe0=f_pe0):
                        sch.wait(s, "GE", "Pe", f_pe0 + 9)
                        i = s.dma_start(out=w1b[:, 0:8192], in_=d_w1[e_c + 1][:, 0:8192])
                        sch.inc(i, "Cw1", 16)
                        sch.wait(s, "GE", "Pe", f_pe0 + 17)
                        i = s.dma_start(out=w1b[:, 8192:16384],
                                        in_=d_w1[e_c + 1][:, 8192:16384])
                        sch.inc(i, "Cw1", 16)
                    ENG["GE"](_)
                    w1tok[e_c + 1] = (sch.tok("Cw1") - 16, sch.tok("Cw1"))
                if cur is not None and t_c == 0 and e_c >= 1:
                    def _(s, e_c=e_c, f_pe0=f_pe0):
                        sch.wait(s, "GE", "Pe", f_pe0 + 25)
                        i = s.dma_start(out=w2b[:], in_=d_w2[e_c])
                        sch.inc(i, "Cw2", 16)
                    ENG["GE"](_)
                    w2tok[e_c] = sch.tok("Cw2")
                if cur is not None:
                    state["t1war"] = [("Ac", p_eg(6)), ("Ac", p_eg(7)),
                                      ("Ac", p_el(6)), ("Ac", p_el(7))]
                    state["pb6war"] = ("Ac", f_ac0 + 1)
                    state["wbrR"][par] = f_pl0 + 9
                    prev = {"lcb": (p_lcb(6), p_lcb(7)), "a07": p_a0(7)}
                    if cur == NT - 1:
                        state["w1_last"] = f_pe0 + 17
                if prv is not None:
                    state["m2war"] = [xacc_base + 7, xacc_base + 8]
                    state["actR"][prv % 2] = m2_new[-1]
                    if prv == NT - 1:
                        state["w2_last"] = m2_new[-1]
                cut(f"stage_{n}")
            def _(v):
                sch.drain_inc(v, "Dv")
            ENG["VE"](_)
            ffn_t = sch.tok("Dv")
            cut("full")
          except _CutDone:
            pass
          if True:
            if DBG_STOP:
                def _(v):
                    i = None
                    for ix, tt in enumerate((sums, sqs, msq, nm, nv, vpe, lnv, rstd,
                                             rsq, vv, mt2, lnw, alphav, cc1, cc2)):
                        i = v.tensor_copy(q2[:, 0:2048].bitcast(F32)[:, ix * 16:(ix + 1) * 16], tt[:])
                    q2f = q2[:, 0:2048].bitcast(F32)
                    v.tensor_copy(q2f[:, 256:384], lgtm[:])
                    v.tensor_copy(q2f[:, 384:512], rwf[:])
                    v.tensor_copy(q2f[:, 512:528], rm1[:])
                    v.tensor_copy(q2f[:, 528:544], rm2[:])
                    v.tensor_copy(q2f[:, 544:560], rw1[:])
                    i = v.tensor_copy(q2f[:, 560:688], req1[:])
                    return i
                sec("VE", _, reads=("stats",), writes=("q2",))
            ffn_t = seal("VE")
            se_t = seal("SE")
            pl_t = seal("GE")

            def _(s):
                sch.wait(s, "SY", "Dv", ffn_t)
                sch.wait(s, "SY", "Ac", se_t)
                sch.wait(s, "SY", "Pl", pl_t)
                sch.wait(s, "SY", "Pe", sch.tok("Pe"))
                for cn in ("Cq", "Cz1", "Cz2", "Cw1", "Cw2"):
                    sch.wait(s, "SY", cn, sch.tok(cn))
                i = s.dma_start(out=d_dbg[:, :],
                                in_=arena[:, 0:57600].bitcast(F32)[:, 0:28800])
                sch.inc(i, "So", 16)
            if DBG_STOP and DBG_DUMP:
                ENG["SY"](_)

          if True:
            state["store"] = sch.tok("So")

        def _(g_):
            g_.wait_ge(sch.sem["So"], sch.cnt["So"])
        ENG["GE"](_)

    return nc


_NC_CACHE = None


def kernel(**inputs):
    global _NC_CACHE
    maps = _host_prep(inputs)
    if _NC_CACHE is None:
        _NC_CACHE = build()
    res = run_bass_kernel_spmd(_NC_CACHE, maps, core_ids=list(range(CORES)))
    out = np.empty((B, S, D), np.float32)
    for c in range(CORES):
        out[c * SPC:(c + 1) * SPC] = res.results[c]["xout"]
    return out


if __name__ == "__main__":
    import time
    t0 = time.time()
    nc = build()
    n = sum(1 for _ in nc.all_instructions())
    print("build s:", round(time.time() - t0, 1), "instrs:", n)


# version 11
# speedup vs baseline: 1.0648x; 1.0006x over previous
import contextlib
import numpy as np
import ml_dtypes
import concourse.bass as bass
import concourse.mybir as mybir
from concourse.bass_utils import run_bass_kernel_spmd
from concourse.alu_op_type import AluOpType as OP

F32 = mybir.dt.float32
BF16 = mybir.dt.bfloat16
AF = mybir.ActivationFunctionType
AX = mybir.AxisListType

B, S, D, H, DH, NE = 32, 2048, 1024, 8, 64, 8
CORES = 8
SPC = B // CORES
NCH, NG, NPAIR, FCH, ICH, DCH = 16, 4, 4, 16, 8, 8
EPS_LN = 1e-5
EPS_RMS = float(np.finfo(np.float32).eps)
GELU_A = 1.702
LIMIT = 7.0
bf16 = ml_dtypes.bfloat16

DBG_STOP = None
DBG_DUMP = True
DBG_NSEQ = None


class _CutDone(Exception):
    pass


ARENA_OFF = {}


def _host_prep(inp):
    f = np.float32
    wq = np.asarray(inp["L0_attn_wq"], f)
    wk = np.asarray(inp["L0_attn_wk"], f)
    wv = np.asarray(inp["L0_attn_wv"], f)
    wo = np.asarray(inp["L0_attn_wo"], f)
    wg = np.asarray(inp["L0_attn_w_gates"], f)
    rmsw = np.asarray(inp["L0_attn_rms_w"], f)
    nrmw = np.asarray(inp["L0_ffn_norm_w"], f)
    gw = np.asarray(inp["L0_ffn_gate_w"], f)
    gb = np.asarray(inp["L0_ffn_gate_b"], f)
    w1 = np.asarray(inp["L0_ffn_mlp1_w"], f)
    b1 = np.asarray(inp["L0_ffn_mlp1_b"], f)
    w2 = np.asarray(inp["L0_ffn_mlp2_w"], f)
    b2 = np.asarray(inp["L0_ffn_mlp2_b"], f)
    tok = np.asarray(inp["L0_attn_tokens"], f)

    def pairstack(w):
        a = np.stack([np.concatenate([w[2 * p], w[2 * p + 1]], axis=1) for p in range(NPAIR)])
        a = a.reshape(NPAIR, DCH, 128, 128).transpose(0, 2, 1, 3)
        return np.ascontiguousarray(a.reshape(NPAIR, 128, DCH * 128))

    wq_h, wk_h, wv_h = pairstack(wq), pairstack(wk), pairstack(wv)
    wo_h = np.ascontiguousarray(
        np.stack([wo[2 * p:2 * p + 2].reshape(128, 1024) for p in range(NPAIR)]))
    wg_h = np.ascontiguousarray(wg.reshape(DCH, 128, H).transpose(1, 0, 2).reshape(128, DCH * H))
    rms_h = np.ascontiguousarray(rmsw.reshape(DCH, 128).T)
    nrm_h = np.ascontiguousarray(nrmw.reshape(DCH, 128).T)
    gw_h = np.ascontiguousarray(gw.T.reshape(DCH, 128, NE).transpose(1, 0, 2).reshape(128, DCH * NE))
    gb_h = np.ascontiguousarray(gb.reshape(NE, 1))

    perm = np.empty(2048, np.int64)
    for fc in range(FCH):
        perm[fc * 128:(fc + 1) * 128] = 256 * (fc // 2) + 2 * np.arange(128) + (fc % 2)
    w1p = w1[:, perm, :]
    a = w1p.transpose(0, 2, 1).reshape(NE, DCH, 128, FCH, 128).transpose(0, 2, 3, 1, 4)
    w1_h = np.ascontiguousarray(a.reshape(NE, 128, FCH * DCH * 128))
    b1_h = np.ascontiguousarray(
        b1[:, perm].reshape(NE, FCH, 128).transpose(2, 0, 1).reshape(128, NE * FCH))
    a = w2.transpose(0, 2, 1).reshape(NE, ICH, 128, 1024).transpose(0, 2, 1, 3)
    w2_h = np.ascontiguousarray(a.reshape(NE, 128, ICH * 1024))
    b2_h = np.ascontiguousarray(b2)

    identb = np.eye(128, dtype=bf16)
    identf = np.eye(128, dtype=np.float32)
    p_ = np.arange(128)[:, None]
    q_ = np.arange(512)[None, :]
    maskt = np.concatenate([(128 * c + p_ <= q_) for c in range(4)], axis=1).astype(bf16)
    psel = np.zeros((8, 512), bf16)
    for p in range(NPAIR):
        psel[2 * p, p * 128:p * 128 + 64] = 1
        psel[2 * p + 1, p * 128 + 64:p * 128 + 128] = 1
    oneh = np.zeros((8, 1024), bf16)
    for e in range(NE):
        oneh[e, e * 128:(e + 1) * 128] = 1
    onescol = np.ones((128, 1), bf16)
    onesrow = np.ones((1, 128), bf16)

    shared = dict(wq=wq_h, wk=wk_h, wv=wv_h, wo=wo_h, wg=wg_h, rmsw=rms_h,
                  nrmw=nrm_h, gw=gw_h, gb=gb_h, w1=w1_h, b1=b1_h, w2=w2_h, b2=b2_h,
                  identb=identb, identf=identf, maskt=maskt, psel=psel, oneh=oneh,
                  onescol=onescol, onesrow=onesrow)
    maps = []
    for c in range(CORES):
        m = dict(shared)
        m["xin"] = np.ascontiguousarray(tok[c * SPC:(c + 1) * SPC])
        maps.append(m)
    return maps


class Sch:
    def __init__(self):
        self.sem, self.cnt, self.lastw = {}, {}, {}

    def add(self, n, h):
        self.sem[n] = h
        self.cnt[n] = 0

    def inc(self, instr, s, n=1):
        instr.then_inc(self.sem[s], n)
        self.cnt[s] += n
        return self.cnt[s]

    def tok(self, s):
        return self.cnt[s]

    def wait(self, eng, en, s, val):
        if not val or val <= 0:
            return
        k = (en, s)
        if self.lastw.get(k, 0) >= val:
            return
        eng.wait_ge(self.sem[s], val)
        self.lastw[k] = val

    def drain_inc(self, eng, s):
        eng.drain()
        i = eng.nop()
        i.then_inc(self.sem[s], 1)
        self.cnt[s] += 1
        return self.cnt[s]


class Res:
    def __init__(self):
        self.w = None
        self.r = {}

    def set_w(self, sem, tok, slack=0):
        self.w = (sem, tok, slack)
        self.r = {}

    def add_r(self, sem, tok):
        if self.r.get(sem, 0) < tok:
            self.r[sem] = tok


def build():
    nc = bass.Bass("TRN2", target_bir_lowering=False)
    dt = nc.dram_tensor
    xin = dt("xin", [SPC, S, D], F32, kind="ExternalInput").ap()
    xout = dt("xout", [SPC, S, D], F32, kind="ExternalOutput").ap()
    d_wq = dt("wq", [NPAIR, 128, 1024], F32, kind="ExternalInput").ap()
    d_wk = dt("wk", [NPAIR, 128, 1024], F32, kind="ExternalInput").ap()
    d_wv = dt("wv", [NPAIR, 128, 1024], F32, kind="ExternalInput").ap()
    d_wo = dt("wo", [NPAIR, 128, 1024], F32, kind="ExternalInput").ap()
    d_wg = dt("wg", [128, 64], F32, kind="ExternalInput").ap()
    d_rms = dt("rmsw", [128, 8], F32, kind="ExternalInput").ap()
    d_nrm = dt("nrmw", [128, 8], F32, kind="ExternalInput").ap()
    d_gw = dt("gw", [128, 64], F32, kind="ExternalInput").ap()
    d_gb = dt("gb", [8, 1], F32, kind="ExternalInput").ap()
    d_w1 = dt("w1", [NE, 128, 16384], F32, kind="ExternalInput").ap()
    d_b1 = dt("b1", [128, 128], F32, kind="ExternalInput").ap()
    d_w2 = dt("w2", [NE, 128, 8192], F32, kind="ExternalInput").ap()
    d_b2 = dt("b2", [8, 1024], F32, kind="ExternalInput").ap()
    d_idb = dt("identb", [128, 128], BF16, kind="ExternalInput").ap()
    d_idf = dt("identf", [128, 128], F32, kind="ExternalInput").ap()
    d_mask = dt("maskt", [128, 2048], BF16, kind="ExternalInput").ap()
    d_psel = dt("psel", [8, 512], BF16, kind="ExternalInput").ap()
    d_oneh = dt("oneh", [8, 1024], BF16, kind="ExternalInput").ap()
    d_onec = dt("onescol", [128, 1], BF16, kind="ExternalInput").ap()
    d_oner = dt("onesrow", [1, 128], BF16, kind="ExternalInput").ap()
    d_zscr = dt("zscr", [4096], F32, kind="Internal").ap()
    d_dbg = dt("dbg", [128, 28800], F32, kind="ExternalOutput").ap() if DBG_STOP else None

    with contextlib.ExitStack() as es:
        sbt = lambda n, sh, d_: es.enter_context(nc.sbuf_tensor(n, sh, d_))
        pst = lambda n, sh, d_: es.enter_context(nc.psum_tensor(n, sh, d_))

        x_sb = sbt("x_sb", [128, NCH * 1024], F32)
        identb = sbt("s_identb", [128, 128], BF16)
        identf = sbt("s_identf", [128, 128], F32)
        maskt = sbt("s_maskt", [128, 2048], BF16)
        psel = sbt("s_psel", [8, 512], BF16)
        oneh = sbt("s_oneh", [8, 1024], BF16)
        onescol = sbt("s_onescol", [128, 1], BF16)
        onesrow = sbt("s_onesrow", [1, 128], BF16)
        rms_sb = sbt("rms_sb", [128, 8], F32)
        nrm_sb = sbt("nrm_sb", [128, 8], F32)
        wg_sb = sbt("wg_sb", [128, 64], BF16)
        gw_sb = sbt("gw_sb", [128, 64], BF16)
        gb_sb = sbt("gb_sb", [8, 1], F32)
        b1_sb = sbt("b1_sb", [128, 128], F32)
        b2_sb = sbt("b2_sb", [8, 1024], BF16)
        b2f = sbt("b2f", [8, 1024], F32)
        sums = sbt("sums", [128, 16], F32)
        sqs = sbt("sqs", [128, 16], F32)
        nm = sbt("nm", [128, 16], F32)
        msq = sbt("msq", [128, 16], F32)
        nv = sbt("nv", [128, 16], F32)
        vpe = sbt("vpe", [128, 16], F32)
        lnv = sbt("lnv", [128, 16], F32)
        rstd = sbt("rstd", [128, 16], F32)
        rsq = sbt("rsq", [128, 16], F32)
        vv = sbt("vv", [128, 16], F32)
        mt2 = sbt("mt2", [128, 16], F32)
        lnw = sbt("lnw", [128, 16], F32)
        alphav = sbt("alphav", [128, 16], F32)
        cc1 = sbt("cc1", [128, 16], F32)
        cc2 = sbt("cc2", [128, 16], F32)
        mcol = sbt("mcol", [1, 16], F32)
        msc = sbt("msc", [1, 4], F32)
        nmb1 = sbt("nmb1", [1, 1], BF16)
        negM = sbt("negM", [128, 1], F32)
        lgtm = sbt("lgtm", [128, 128], F32)
        rm1 = sbt("rm1", [128, 16], F32)
        rm2 = sbt("rm2", [128, 16], F32)
        req1 = sbt("req1", [128, 128], F32)
        rl2 = sbt("rl2", [128, 128], F32)
        req2 = sbt("req2", [128, 128], F32)
        rd12 = sbt("rd12", [128, 16], F32)
        rw1 = sbt("rw1", [128, 16], F32)
        rw2 = sbt("rw2", [128, 16], F32)
        rwa = sbt("rwa", [128, 128], F32)
        rwf = sbt("rwf", [128, 128], F32)
        sqscr = sbt("sqscr", [128, 512], F32)
        dscr = sbt("dscr", [128, 64], F32)

        AR_COLS = 61440
        arena = sbt("arena", [128, AR_COLS], BF16)

        class Arena:
            def __init__(self):
                self.off = 0

            def take(self, cols):
                cols = (cols + 31) & ~31
                assert self.off + cols <= AR_COLS, f"arena overflow {self.off}+{cols}"
                o = self.off
                self.off += cols
                return o

            def b16(self, cols, parts=128, name=None):
                o = self.take(cols)
                if name:
                    ARENA_OFF[name] = (o, cols, 'b16', parts)
                return arena[0:parts, o:o + cols]

            def f32(self, cols, parts=128, name=None):
                o = self.take(2 * cols)
                if name:
                    ARENA_OFF[name] = (o, 2 * cols, 'f32', parts)
                return arena[0:parts, o:o + 2 * cols].bitcast(F32)

        A = Arena()
        tbf = A.b16(4096, name='tbf')
        tT = A.b16(16384, name='tT')
        mark = A.off
        qp = A.b16(2048, name='qp')
        kp = A.b16(2048, name='kp')
        vp = A.b16(2080, name='vp')
        q2 = A.b16(2048, name='q2')
        wqkv = A.b16(1024, name='wqkv')
        woall = A.b16(4096, name='woall')
        gTb = A.b16(2048, parts=8, name='gTb')
        outT = A.b16(8192, name='outT')
        Ebufs = [A.b16(512, name=f'E{_i}') for _i in range(6)]
        zst = A.b16(4096, parts=1, name='zst')
        zbuf = A.b16(2048, parts=8, name='zbuf')
        crec8 = A.f32(2048, parts=8, name='crec')
        cgb = A.b16(2048, parts=8, name='cgb')
        attn_end = A.off
        A.off = mark
        w1b = A.b16(16384, name='w1b')
        w2b = A.b16(8192, name='w2b')
        actb = [A.b16(4096, name='act0'), tbf]
        wbr = [A.b16(512, name='wbr0'), A.b16(512, name='wbr1')]
        g1p = [A.f32(512, name='g1a'), A.f32(512, name='g1b')]
        l1p = [A.f32(512, name='l1a'), A.f32(512, name='l1b')]
        lcb2 = [A.f32(512, name='lcba'), A.f32(512, name='lcbb')]
        p1 = A.f32(512, name='p1')
        a0 = A.f32(512, name='a0')
        lgt = A.f32(512, parts=8, name='lgt')
        wTb = A.b16(2048, parts=8, name='wTb')
        ffn_end = A.off
        wqkvB = arena[0:128, ARENA_OFF['lgt'][0]:ARENA_OFF['lgt'][0] + 1024]
        wbuf = [wqkv, wqkvB]
        wbufn = ["wqkv", "wqkvB"]
        assert max(attn_end, ffn_end) <= AR_COLS, (attn_end, ffn_end)

        pb = [pst(f"pb{i}", [128, 512], F32) for i in range(7)]
        pbt = pst("pbt", [128, 1024], BF16)

        sch = Sch()
        for n in ["Cs", "Cg", "Cx", "Cq", "Cz1", "Cz2", "Cw1", "Cw2",
                  "Pe", "Ac", "Dv", "Pl", "So"]:
            sch.add(n, es.enter_context(nc.semaphore(n)))
        block = es.enter_context(nc.Block())
        ENG = {"PE": block.tensor, "VE": block.vector, "SE": block.scalar,
               "GE": block.gpsimd, "SY": block.sync}
        SEM = {"PE": "Pe", "VE": "Dv", "SE": "Ac", "GE": "Pl"}

        R = {}
        for name in (["wqkvB", "lgt2"]
                     + ["x", "tbf", "tT", "qp", "kp", "vp", "q2", "wo", "gT", "outT",
                      "zst", "zbuf", "crec", "cgb", "w1", "w2", "act", "g1", "lc",
                      "p1", "a0", "lgt", "wTb", "lgtm", "rwf", "stats", "mh",
                      "negM", "b2", "rtr", "wqkv", "sqscr"]
                     + [f"pb{i}" for i in range(7)] + ["pbt"]
                     + [f"wbr{i}" for i in range(2)]):
            R[name] = Res()

        def sec(en, fn, reads=(), writes=(), war_extra=()):
            sem = SEM.get(en)

            def body(eng):
                for rn in reads:
                    rr = R[rn]
                    if rr.w and rr.w[0] != sem:
                        sch.wait(eng, en, rr.w[0], rr.w[1] + rr.w[2])
                for wn in writes:
                    rr = R[wn]
                    for s_, t_ in rr.r.items():
                        if s_ != sem:
                            sch.wait(eng, en, s_, t_)
                    if rr.w and rr.w[0] != sem:
                        sch.wait(eng, en, rr.w[0], rr.w[1])
                for s_, t_ in war_extra:
                    sch.wait(eng, en, s_, t_)
                out = fn(eng)
                if out is not None and sem is not None:
                    sch.inc(out, sem)
            ENG[en](body)
            tok = sch.tok(sem) if sem else None
            for rn in reads:
                if sem:
                    R[rn].add_r(sem, tok)
            for wn in writes:
                if sem:
                    R[wn].set_w(sem, tok, 1 if en in ("VE", "SE", "GE") else 0)
            return tok

        def seal(en):
            sem = SEM[en]

            def body(eng):
                sch.drain_inc(eng, sem)
            ENG[en](body)
            t = sch.tok(sem)
            for rr in R.values():
                if rr.w and rr.w[0] == sem and rr.w[2] and rr.w[1] < t:
                    rr.w = (sem, t, 0)
            return t

        def _(s):
            for dst, src in ((identf, d_idf), (rms_sb, d_rms), (nrm_sb, d_nrm),
                             (gb_sb, d_gb), (b1_sb, d_b1), (b2f, d_b2)):
                i = s.dma_start(out=dst[:], in_=src[:, :])
                sch.inc(i, "Cs", 16)
        ENG["SY"](_)
        lx0 = sch.tok("Cs")

        def _(g):
            for dst, src in ((identb, d_idb), (maskt, d_mask), (psel, d_psel),
                             (oneh, d_oneh), (onescol, d_onec), (onesrow, d_oner),
                             (wg_sb, d_wg), (gw_sb, d_gw)):
                i = g.dma_start(out=dst[:], in_=src[:, :])
                sch.inc(i, "Cg", 16)
        ENG["GE"](_)
        lw0 = sch.tok("Cg")

        def _(v):
            v.memset(dscr[:], 0.0)
            return v.tensor_scalar(b2_sb[:], b2f[:], GELU_A, None, OP.mult)
        sec("VE", _, writes=("b2",), war_extra=(("Cs", lx0),))
        for _en, _eng in (("PE", "tensor"), ("VE", "vector"), ("SE", "scalar"), ("GE", "gpsimd")):
            def _w(e, _en=_en):
                sch.wait(e, _en_cur[0], "Cs", lx0)
                sch.wait(e, _en_cur[0], "Cg", lw0)
            _en_cur = [_en]
            ENG[_en](_w)
        seal("VE")

        state = {"store": 0, "w1_last": 0, "w2_last": 0}

        def cut(label):
            if DBG_STOP == label:
                raise _CutDone()

        nseq = DBG_NSEQ if DBG_NSEQ is not None else SPC
        for s_i in range(nseq):
          try:
            if s_i == 0:
                def _(s, s_i=s_i):
                    sch.wait(s, "SY", "So", state["store"])
                    for s_, t_ in R["x"].r.items():
                        sch.wait(s, "SY", s_, t_)
                    i = s.dma_start(out=x_sb[:].rearrange("p (c d) -> p c d", c=NCH),
                                    in_=xin[s_i].rearrange("(c p) d -> p c d", p=128))
                    sch.inc(i, "Cx", 16)
                ENG["SY"](_)
                R["x"].set_w("Cx", sch.tok("Cx"))

            cut("load")
            def emit_ln(is_ln2, precomputed=False):
                if not precomputed:
                    def _(sc):
                        i = None
                        for c in range(NCH):
                            xc = x_sb[:, c * 1024:(c + 1) * 1024]
                            sc.activation(sqscr[:], xc[:, 0:512], AF.Square,
                                          accum_out=sqs[:, c:c + 1])
                            i = sc.activation(sqscr[:], xc[:, 512:1024], AF.Square,
                                              accum_out=msq[:, c:c + 1])
                        return i
                    sec("SE", _, reads=("x",), writes=("sqscr", "stats"))
                    sq_t = seal("SE")

                    def _(v):
                        i = None
                        for c in range(NCH):
                            i = v.reduce_sum(sums[:, c:c + 1],
                                             x_sb[:, c * 1024:(c + 1) * 1024], axis=AX.X)
                        return i
                    sec("VE", _, reads=("x",), writes=("stats",))
                    seal("VE")
                else:
                    sq_t = seal("SE")

                def _(v):
                    v.tensor_tensor(vv[:], sqs[:], msq[:], OP.add)
                    v.tensor_scalar(nm[:], sums[:], -1.0 / D, None, OP.mult)
                    v.drain()
                    v.tensor_scalar(msq[:], vv[:], 1.0 / D, None, OP.mult)
                    v.tensor_tensor(vpe[:], nm[:], nm[:], OP.mult)
                    v.drain()
                    v.tensor_tensor(nv[:], vpe[:], msq[:], OP.subtract)
                    v.drain()
                    return v.tensor_scalar(vpe[:], nv[:], -1.0, EPS_LN, OP.mult, OP.add)
                sec("VE", _, reads=("stats",), writes=("stats",), war_extra=(("Ac", sq_t),))
                seal("VE")

                def _(sc):
                    sc.activation(lnv[:], vpe[:], AF.Ln)
                    sc.drain()
                    sc.activation(rstd[:], lnv[:], AF.Exp, scale=-0.5)
                    return sc.activation(rsq[:], lnv[:], AF.Exp, scale=-1.0)
                sec("SE", _, reads=("stats",), writes=("stats",))
                seal("SE")

                def _(v):
                    v.tensor_scalar(vv[:], nv[:], -1.0, None, OP.mult)
                    v.drain()
                    v.tensor_tensor(mt2[:], vv[:], rsq[:], OP.mult)
                    v.drain()
                    return v.tensor_scalar(mt2[:], mt2[:], EPS_RMS, None, OP.add)
                sec("VE", _, reads=("stats",), writes=("stats",))
                seal("VE")

                def _(sc):
                    sc.activation(lnw[:], mt2[:], AF.Ln)
                    sc.drain()
                    return sc.activation(vv[:], lnw[:], AF.Exp, scale=-0.5)
                sec("SE", _, reads=("stats",), writes=("stats",))
                seal("SE")

                def _(v):
                    i = v.tensor_tensor(alphav[:], rstd[:], vv[:], OP.mult)
                    if is_ln2:
                        v.tensor_scalar(cc1[:], rstd[:], 1.0, None, OP.add)
                        i = v.tensor_tensor(cc2[:], nm[:], rstd[:], OP.mult)
                    return i
                sec("VE", _, reads=("stats",), writes=("stats",))
                seal("VE")

            def emit_apply_T(wvec_sb, rewrite, war_pe=0):
                for g in range(NG):
                    def _(v, g=g):
                        i = None
                        for k in range(4):
                            c = g * 4 + k
                            i = v.tensor_scalar(tbf[:, k * 1024:(k + 1) * 1024],
                                                x_sb[:, c * 1024:(c + 1) * 1024],
                                                nm[:, c:c + 1], alphav[:, c:c + 1],
                                                OP.add, OP.mult)
                        return i
                    sec("VE", _, reads=("stats", "x"), writes=("tbf",),
                        war_extra=((("Pe", war_pe),) if g == 0 and war_pe else ()))
                    seal("VE")

                    for dcp in range(4):
                        def _(t, g=g, dcp=dcp):
                            i = None
                            for half, dc in ((0, 2 * dcp), (512, 2 * dcp + 1)):
                                for k in range(4):
                                    i = t.transpose(
                                        pbt[:, half + k * 128:half + (k + 1) * 128],
                                        tbf[:, k * 1024 + dc * 128:k * 1024 + (dc + 1) * 128],
                                        identb[:])
                            return i
                        sec("PE", _, reads=("tbf",), writes=("pbt",))

                        def _(v, g=g, dcp=dcp):
                            i = None
                            for half, dc in ((0, 2 * dcp), (512, 2 * dcp + 1)):
                                i = v.tensor_scalar(
                                    tT[:, dc * 2048 + g * 512:dc * 2048 + (g + 1) * 512],
                                    pbt[:, half:half + 512], wvec_sb[:, dc:dc + 1],
                                    None, OP.mult)
                            return i
                        sec("VE", _, reads=("pbt",), writes=("tT",))

                    if rewrite:
                        def _(v, g=g):
                            i = None
                            for k in range(4):
                                c = g * 4 + k
                                i = v.tensor_scalar(x_sb[:, c * 1024:(c + 1) * 1024],
                                                    x_sb[:, c * 1024:(c + 1) * 1024],
                                                    cc1[:, c:c + 1], cc2[:, c:c + 1],
                                                    OP.mult, OP.add)
                            return i
                        sec("VE", _, reads=("stats",), writes=("x",))
                seal("VE")

            emit_ln(is_ln2=False, precomputed=(s_i > 0))
            cut("ln1")
            emit_apply_T(rms_sb, rewrite=False, war_pe=state["w2_last"])
            cut("applyT")

            g_pe0 = sch.tok("Pe")

            def _(t, g_pe0=g_pe0):
                rrt = R["tT"].w
                sch.wait(t, "PE", rrt[0], rrt[1] + rrt[2])
                for s_, t_ in R["pb6"].r.items():
                    sch.wait(t, "PE", s_, t_)
                for g in range(NG):
                    if g > 0:
                        sch.wait(t, "PE", "Ac", sch.tok("Ac") + g)
                    for dc in range(DCH):
                        i = t.matmul(pb[6][0:8, 0:512], wg_sb[:, dc * 8:(dc + 1) * 8],
                                     tT[:, dc * 2048 + g * 512:dc * 2048 + (g + 1) * 512],
                                     start=(dc == 0), stop=(dc == DCH - 1))
                    sch.inc(i, "Pe")
            ENG["PE"](_)

            def _(sc, g_pe0=g_pe0):
                for g in range(NG):
                    sch.wait(sc, "SE", "Pe", g_pe0 + g + 1)
                    i = sc.activation(gTb[:, g * 512:(g + 1) * 512],
                                      pb[6][0:8, 0:512], AF.Sigmoid)
                    sch.inc(i, "Ac")
                i = sc.copy(dscr[:, 16:32], dscr[:, 0:16])
                sch.inc(i, "Ac")
            ENG["SE"](_)
            R["gT"].set_w("Ac", sch.tok("Ac"), 0)
            R["pb6"].set_w("Pe", sch.tok("Pe"), 0)
            R["pb6"].add_r("Ac", sch.tok("Ac"))
            cut("gates")

            def _(g_, s_i=s_i):
                for s_, t_ in R["wo"].r.items():
                    sch.wait(g_, "GE", s_, t_)
                sch.wait(g_, "GE", "Pe", state["w1_last"])
                i = g_.dma_start(out=woall[:].rearrange("r (p c) -> r p c", p=NPAIR),
                                 in_=d_wo.rearrange("p r c -> r p c"))
                sch.inc(i, "Cq", 16)
                R["wo"].set_w("Cq", sch.cnt["Cq"])
            ENG["GE"](_)

            state["prev_av"] = 0
            for pair in range(NPAIR):
                state["attn_ep"] = sch.tok("Dv")
                def wdma(src, bufi, pair=pair):
                    def _(g_):
                        for s_, t_ in R[wbufn[bufi]].r.items():
                            sch.wait(g_, "GE", s_, t_)
                        if bufi == 1:
                            rl = R["lgt"]
                            if rl.w:
                                sch.wait(g_, "GE", rl.w[0], rl.w[1] + rl.w[2])
                            for s_, t_ in rl.r.items():
                                sch.wait(g_, "GE", s_, t_)
                        sch.wait(g_, "GE", "Pe", state["w1_last"])
                        i = g_.dma_start(out=wbuf[bufi][:], in_=src[pair])
                        sch.inc(i, "Cq", 16)
                        R[wbufn[bufi]].set_w("Cq", sch.cnt["Cq"])
                    ENG["GE"](_)

                if pair == 0:
                    wdma(d_wq, 0)
                    wdma(d_wk, 1)

                for which in (0, 1):
                    dstq = qp if which == 0 else kp
                    bres = "qp" if which == 0 else "kp"
                    wbi = (pair + which) % 2
                    qk_pe0 = sch.tok("Pe")
                    qk_ac0 = sch.tok("Ac")

                    def _(t, qk_pe0=qk_pe0, qk_ac0=qk_ac0, wbi=wbi):
                        rr = R[wbufn[wbi]].w
                        sch.wait(t, "PE", rr[0], rr[1])
                        rrt = R["tT"].w
                        sch.wait(t, "PE", rrt[0], rrt[1] + rrt[2])
                        for bn in ("pb5", "pb6"):
                            for s_, t_ in R[bn].r.items():
                                sch.wait(t, "PE", s_, t_)
                        for g in range(NG):
                            bk = pb[5 + (g % 2)]
                            if g >= 2:
                                sch.wait(t, "PE", "Ac", qk_ac0 + g - 1)
                            for dc in range(DCH):
                                i = t.matmul(bk[:, 0:512], wbuf[wbi][:, dc * 128:(dc + 1) * 128],
                                             tT[:, dc * 2048 + g * 512:dc * 2048 + (g + 1) * 512],
                                             start=(dc == 0), stop=(dc == DCH - 1))
                            sch.inc(i, "Pe")
                    ENG["PE"](_)

                    def _(sc, dstq=dstq, qk_pe0=qk_pe0):
                        for g in range(NG):
                            sch.wait(sc, "SE", "Pe", qk_pe0 + g + 1)
                            i = sc.copy(dstq[:, g * 512:(g + 1) * 512], pb[5 + (g % 2)][:, 0:512])
                            sch.inc(i, "Ac")
                    ENG["SE"](_)
                    R[wbufn[wbi]].add_r("Pe", sch.tok("Pe"))
                    if wbi == 1:
                        R["lgt"].add_r("Pe", sch.tok("Pe"))
                    R[bres].set_w("Ac", sch.tok("Ac"), 1)
                    R["pb5"].set_w("Pe", qk_pe0 + 3, 0)
                    R["pb6"].set_w("Pe", qk_pe0 + 4, 0)
                    R["pb5"].add_r("Ac", qk_ac0 + 3)
                    R["pb6"].add_r("Ac", qk_ac0 + 4)
                wdma(d_wv, pair % 2)

                vp_pe0 = sch.tok("Pe")
                vp_dv0 = sch.tok("Dv")

                def _(t, vp_pe0=vp_pe0, vp_dv0=vp_dv0, wbi=pair % 2):
                    rr = R[wbufn[wbi]].w
                    sch.wait(t, "PE", rr[0], rr[1])
                    for bn in ("pb4", "pb5"):
                        for s_, t_ in R[bn].r.items():
                            sch.wait(t, "PE", s_, t_)
                    for c in range(NCH):
                        bk = pb[4 + c % 2]
                        if c >= 2:
                            sch.wait(t, "PE", "Dv", vp_dv0 + c - 1)
                        for dc in range(DCH):
                            i = t.matmul(bk[:, 0:128],
                                         tT[:, dc * 2048 + c * 128:dc * 2048 + (c + 1) * 128],
                                         wbuf[pair % 2][:, dc * 128:(dc + 1) * 128],
                                         start=(dc == 0), stop=(dc == DCH - 1))
                        sch.inc(i, "Pe")
                ENG["PE"](_)

                def _(v, vp_pe0=vp_pe0):
                    for c in range(NCH):
                        bk = pb[4 + c % 2]
                        sch.wait(v, "VE", "Pe", vp_pe0 + c + 1)
                        v.tensor_copy(vp[:, c * 130:c * 130 + 64], bk[:, 0:64])
                        i = v.tensor_copy(vp[:, c * 130 + 65:c * 130 + 129], bk[:, 64:128])
                        sch.inc(i, "Dv")
                    v3 = vp[:].rearrange("p (cc j) -> p cc j", cc=NCH)
                    v.memset(v3[:, :, 64:65], 1.0)
                    v.memset(v3[:, :, 129:130], 1.0)
                    i = v.tensor_copy(dscr[:, 48:64].bitcast(F32) if False else dscr[:, 32:48], dscr[:, 0:16])
                    sch.inc(i, "Dv")
                ENG["VE"](_)
                R[wbufn[pair % 2]].add_r("Pe", sch.tok("Pe"))
                if pair % 2 == 1:
                    R["lgt"].add_r("Pe", sch.tok("Pe"))
                R["vp"].set_w("Dv", sch.tok("Dv"), 0)
                R["pb4"].set_w("Pe", vp_pe0 + 15, 0)
                R["pb5"].set_w("Pe", vp_pe0 + 16, 0)
                R["pb4"].add_r("Dv", vp_dv0 + 15)
                R["pb5"].add_r("Dv", vp_dv0 + 16)
                if pair < NPAIR - 1:
                    wdma(d_wq, (pair + 1) % 2, pair=pair + 1)
                    wdma(d_wk, pair % 2, pair=pair + 1)

                def _(sc):
                    return sc.activation(q2[:, 0:2048], qp[:], AF.Square)
                sec("SE", _, reads=("qp",), writes=("q2",))
                seal("SE")

                def _(t):
                    for g4 in range(4):
                        i = t.matmul(pb[g4][0:1, 0:512], onescol[:],
                                     q2[:, g4 * 512:(g4 + 1) * 512], start=True, stop=True)
                        sch.inc(i, "Pe")
                    return None
                sec("PE", _, reads=("q2",), writes=("pb0", "pb1", "pb2", "pb3"))
                nq_t = sch.tok("Pe")
                for g4 in range(4):
                    R[f"pb{g4}"].set_w("Pe", nq_t, 0)

                def _(v):
                    sch.wait(v, "VE", "Pe", nq_t)
                    i = None
                    for g4 in range(4):
                        i = v.reduce_max(mcol[0:1, g4:g4 + 1], pb[g4][0:1, 0:512], axis=AX.X)
                    return i
                sec("VE", _, reads=("pb0", "pb1", "pb2", "pb3"), writes=("mh",))
                seal("VE")

                def _(sc):
                    return sc.activation(q2[:, 0:2048], kp[:], AF.Square)
                sec("SE", _, reads=("kp", "q2"), writes=("q2",))
                seal("SE")

                def _(t):
                    for g4 in range(4):
                        i = t.matmul(pb[g4][0:1, 0:512], onescol[:],
                                     q2[:, g4 * 512:(g4 + 1) * 512], start=True, stop=True)
                        sch.inc(i, "Pe")
                    return None
                sec("PE", _, reads=("q2",), writes=("pb0", "pb1", "pb2", "pb3"))
                nk_t = sch.tok("Pe")
                for g4 in range(4):
                    R[f"pb{g4}"].set_w("Pe", nk_t, 0)

                def _(v):
                    sch.wait(v, "VE", "Pe", nk_t)
                    for g4 in range(4):
                        v.reduce_max(mcol[0:1, 8 + g4:9 + g4], pb[g4][0:1, 0:512], axis=AX.X)
                    sch.drain_inc(v, "Dv")
                    v.reduce_max(msc[0:1, 0:1], mcol[0:1, 0:4], axis=AX.X)
                    v.reduce_max(msc[0:1, 1:2], mcol[0:1, 8:12], axis=AX.X)
                    sch.drain_inc(v, "Dv")
                    v.drain()
                    return v.tensor_tensor(msc[0:1, 2:3], msc[0:1, 0:1], msc[0:1, 1:2], OP.mult)
                sec("VE", _, reads=("pb0", "pb1", "pb2", "pb3"), writes=("mh",))
                seal("VE")

                def _(sc):
                    return sc.activation(msc[0:1, 3:4], msc[0:1, 2:3], AF.Sqrt)
                sec("SE", _, reads=("mh",), writes=("mh",))
                seal("SE")

                def _(v):
                    return v.tensor_scalar(nmb1[0:1, 0:1], msc[0:1, 3:4], -0.125, None, OP.mult)
                sec("VE", _, reads=("mh",), writes=("mh",))
                seal("VE")

                def _(t):
                    return t.matmul(pb[6][:, 0:1], onesrow[0:1, :], nmb1[0:1, 0:1],
                                    start=True, stop=True)
                sec("PE", _, reads=("mh",), writes=("pb6",))

                def _(v):
                    return v.tensor_copy(negM[:], pb[6][:, 0:1])
                sec("VE", _, reads=("pb6",), writes=("negM",))
                negM_t = seal("VE")
                cut(f"mhat{pair}")

                for g in range(NG):
                    J = 4 * g + 4
                    d0 = 4 * g

                    def qw(j):
                        return max(0, (j - d0) * 128) if j >= d0 else 0

                    peops = []
                    for j in range(J):
                        peops.append(("s", j))
                        if j >= 3:
                            peops.append(("av", j - 3))
                    peops += [("av", J - 3), ("av", J - 2), ("av", J - 1)]
                    c_pe0 = sch.tok("Pe")
                    c_ac0 = sch.tok("Ac")
                    c_dv0 = sch.tok("Dv")
                    p_pe = {op: c_pe0 + k + 1 for k, op in enumerate(peops)}
                    p_ex = {j: c_ac0 + 2 * j + 1 for j in range(J)}
                    p_exd = {j: c_ac0 + 2 * j + 2 for j in range(J)}
                    p_mkd = {d0 + c: c_dv0 + 2 * (c + 1) for c in range(4)}

                    def _(t, g=g, J=J, d0=d0):
                        sch.wait(t, "PE", "Dv", state["attn_ep"])
                        sch.wait(t, "PE", "Dv", negM_t)
                        for kind, j in peops:
                            if kind == "s":
                                w = 512 - qw(j)
                                if j >= 2:
                                    sch.wait(t, "PE", "Ac", p_ex(j - 2) if False else p_ex[j - 2])
                                q0 = g * 512 + qw(j)
                                t.matmul(pb[j % 2][:, 0:w], kp[0:64, j * 128:(j + 1) * 128],
                                         qp[0:64, q0:(g + 1) * 512], start=True, stop=True,
                                         tile_position=(0, 0))
                                i = t.matmul(pb[2 + j % 2][:, 0:w],
                                             kp[64:128, j * 128:(j + 1) * 128],
                                             qp[64:128, q0:(g + 1) * 512], start=True, stop=True,
                                             tile_position=(64, 0))
                            else:
                                w = 512 - qw(j)
                                if j >= d0:
                                    sch.wait(t, "PE", "Dv", p_mkd[j])
                                else:
                                    sch.wait(t, "PE", "Ac", p_exd[j])
                                o0 = qw(j)
                                t.matmul(pb[4][0:65, o0:512], vp[:, j * 130:j * 130 + 65],
                                         Ebufs[j % 3][:, 0:w], start=(j == 0), stop=(j == J - 1))
                                i = t.matmul(pb[5][0:65, o0:512],
                                             vp[:, j * 130 + 65:j * 130 + 130],
                                             Ebufs[3 + j % 3][:, 0:w], start=(j == 0), stop=(j == J - 1))
                            sch.inc(i, "Pe")
                    ENG["PE"](_)
                    assert sch.tok("Pe") == c_pe0 + len(peops)

                    def _(sc, g=g, J=J):
                        sch.wait(sc, "SE", "Dv", negM_t)
                        for j in range(J):
                            w = 512 - qw(j)
                            sch.wait(sc, "SE", "Pe", p_pe[("s", j)])
                            if j >= 3:
                                sch.wait(sc, "SE", "Pe", p_pe[("av", j - 3)])
                            elif state["prev_av"]:
                                sch.wait(sc, "SE", "Pe", state["prev_av"])
                            sc.activation(Ebufs[j % 3][:, 0:w], pb[j % 2][:, 0:w], AF.Exp,
                                          bias=negM[:, 0:1], scale=0.125)
                            i = sc.activation(Ebufs[3 + j % 3][:, 0:w], pb[2 + j % 2][:, 0:w],
                                              AF.Exp, bias=negM[:, 0:1], scale=0.125)
                            sch.inc(i, "Ac")
                            i = sc.copy(dscr[:, 16:32], dscr[:, 0:16])
                            sch.inc(i, "Ac")
                    ENG["SE"](_)
                    assert sch.tok("Ac") == c_ac0 + 2 * J

                    def _(v, g=g, J=J, d0=d0, pair=pair):
                        for j in range(d0, J):
                            c = j - d0
                            w = 512 - 128 * c
                            sch.wait(v, "VE", "Ac", p_exd[j])
                            v.tensor_tensor(Ebufs[j % 3][:, 0:w], Ebufs[j % 3][:, 0:w],
                                            maskt[:, 0:w], OP.mult)
                            i = v.tensor_tensor(Ebufs[3 + j % 3][:, 0:w],
                                                Ebufs[3 + j % 3][:, 0:w],
                                                maskt[:, 0:w], OP.mult)
                            sch.inc(i, "Dv")
                            i = v.tensor_copy(dscr[:, 32:48], dscr[:, 0:16])
                            sch.inc(i, "Dv")
                        sch.wait(v, "VE", "Pe", p_pe[("av", J - 1)])
                        col = pair * 2048 + g * 512
                        zc = g * 1024
                        v.tensor_copy(outT[0:64, col:col + 512], pb[4][0:64, 0:512])
                        v.tensor_copy(outT[64:128, col:col + 512], pb[5][0:64, 0:512])
                        v.tensor_copy(zst[0:1, zc:zc + 512], pb[4][64:65, 0:512])
                        i = v.tensor_copy(zst[0:1, zc + 512:zc + 1024], pb[5][64:65, 0:512])
                        sch.inc(i, "Dv")
                        i = v.tensor_copy(dscr[:, 32:48], dscr[:, 0:16])
                        sch.inc(i, "Dv")
                    ENG["VE"](_)
                    assert sch.tok("Dv") == c_dv0 + 2 * 4 + 2
                    state["attn_ep"] = sch.tok("Dv")
                    state["prev_av"] = p_pe[("av", J - 1)]
                    cut(f"core{pair}_{g}")
                R["outT"].set_w("Dv", sch.tok("Dv"), 0)
                R["zst"].set_w("Dv", sch.tok("Dv"), 0)
                R["pb4"].set_w("Pe", state["prev_av"], 0)
                R["pb5"].set_w("Pe", state["prev_av"], 0)
                R["pb4"].add_r("Dv", state["attn_ep"])
                R["pb5"].add_r("Dv", state["attn_ep"])
                for jj in range(4):
                    R[f"pb{jj}"].set_w("Pe", state["prev_av"], 0)
                    R[f"pb{jj}"].add_r("Dv", state["attn_ep"])
                cut(f"preZ{pair}")
                R["negM"].add_r("Ac", sch.tok("Ac"))
                cut(f"preZ{pair}")

                def _(g_, pair=pair):
                    rr = R["zst"]
                    sch.wait(g_, "GE", rr.w[0], rr.w[1] + rr.w[2])
                    for s_, t_ in R["zbuf"].r.items():
                        sch.wait(g_, "GE", s_, t_)
                    sch.wait(g_, "GE", "Cz2", sch.cnt["Cz2"])
                    i = g_.dma_start(out=d_zscr.rearrange("(o w) -> o w", o=1), in_=zst[0:1, :])
                    sch.inc(i, "Cz1", 16)
                    g_.wait_ge(sch.sem["Cz1"], sch.cnt["Cz1"])
                    i = g_.dma_start(
                        out=zbuf[2 * pair:2 * pair + 2, :].rearrange("h (g w) -> h g w", g=4),
                        in_=d_zscr.rearrange("(g h w) -> h g w", g=4, h=2))
                    sch.inc(i, "Cz2", 16)
                    R["zst"].add_r("Cz1", sch.cnt["Cz1"])
                ENG["GE"](_)
                R["zbuf"].set_w("Cz2", sch.tok("Cz2"))
                cut(f"pair{pair}")

            def _(v):
                return v.reciprocal(crec8[:], zbuf[0:8, :])
            sec("VE", _, reads=("zbuf",), writes=("crec",))
            seal("VE")

            def _(v):
                return v.tensor_tensor(cgb[:], crec8[:], gTb[:], OP.mult)
            sec("VE", _, reads=("crec", "gT"), writes=("cgb",))
            cgb_t2 = seal("VE")

            state["scale_done"] = cgb_t2
            state["pred_done"] = cgb_t2
            for g in range(NG):
                s_pe0 = sch.tok("Pe")
                s_dv0 = sch.tok("Dv")

                def _(t, g=g, s_pe0=s_pe0, s_dv0=s_dv0):
                    sch.wait(t, "PE", "Dv", cgb_t2)
                    sch.wait(t, "PE", "Dv", state["attn_ep"])
                    for pair in range(NPAIR):
                        if pair >= 2:
                            sch.wait(t, "PE", "Dv", s_dv0 + pair - 1)
                        else:
                            sch.wait(t, "PE", "Dv", state["scale_done"])
                        i = t.matmul(pb[2 + pair % 2][:, 0:512],
                                     psel[:, pair * 128:(pair + 1) * 128],
                                     cgb[0:8, g * 512:(g + 1) * 512], start=True, stop=True)
                        sch.inc(i, "Pe")
                    for k in range(8):
                        ns, dh = divmod(k, 2)
                        sch.wait(t, "PE", "Dv", s_dv0 + 5)
                        if k >= 2:
                            sch.wait(t, "PE", "Dv", s_dv0 + 5 + k - 1)
                        else:
                            sch.wait(t, "PE", "Dv", state["pred_done"])
                        for pair in range(NPAIR):
                            i = t.matmul(pb[k % 2][:, 0:512],
                                         outT[:, pair * 2048 + g * 512 + ns * 128:
                                              pair * 2048 + g * 512 + (ns + 1) * 128],
                                         woall[:, pair * 1024 + dh * 512:
                                               pair * 1024 + (dh + 1) * 512],
                                         start=(pair == 0), stop=(pair == NPAIR - 1))
                        sch.inc(i, "Pe")
                ENG["PE"](_)

                def _(v, g=g, s_pe0=s_pe0):
                    for pair in range(NPAIR):
                        sch.wait(v, "VE", "Pe", s_pe0 + pair + 1)
                        col = pair * 2048 + g * 512
                        i = v.tensor_tensor(outT[:, col:col + 512], outT[:, col:col + 512],
                                            pb[2 + pair % 2][:, 0:512], OP.mult)
                        sch.inc(i, "Dv")
                    i = v.tensor_copy(dscr[:, 32:48], dscr[:, 0:16])
                    sch.inc(i, "Dv")
                    for k in range(8):
                        ns, dh = divmod(k, 2)
                        c = g * 4 + ns
                        sch.wait(v, "VE", "Pe", s_pe0 + 4 + k + 1)
                        xsl = x_sb[:, c * 1024 + dh * 512:c * 1024 + (dh + 1) * 512]
                        i = v.tensor_tensor(xsl, xsl, pb[k % 2][:, 0:512], OP.add)
                        sch.inc(i, "Dv")
                ENG["VE"](_)
                state["scale_done"] = s_dv0 + 5
                state["pred_done"] = s_dv0 + 13

                def _(sc, g=g, s_dv0=s_dv0):
                    i = None
                    for ns in range(4):
                        c = g * 4 + ns
                        sch.wait(sc, "SE", "Dv", s_dv0 + 7 + 2 * ns)
                        xc = x_sb[:, c * 1024:(c + 1) * 1024]
                        sc.activation(sqscr[:], xc[:, 0:512], AF.Square,
                                      accum_out=sqs[:, c:c + 1])
                        i = sc.activation(sqscr[:], xc[:, 512:1024], AF.Square,
                                          accum_out=msq[:, c:c + 1])
                    sch.inc(i, "Ac")
                ENG["SE"](_)
                R["x"].add_r("Ac", sch.tok("Ac"))

                def _(v, g=g):
                    i = None
                    for ns in range(4):
                        c = g * 4 + ns
                        i = v.reduce_sum(sums[:, c:c + 1],
                                         x_sb[:, c * 1024:(c + 1) * 1024], axis=AX.X)
                    sch.inc(i, "Dv")
                ENG["VE"](_)
                R["x"].add_r("Dv", sch.tok("Dv"))
            def _(v):
                sch.drain_inc(v, "Dv")
            ENG["VE"](_)
            R["x"].set_w("Dv", sch.tok("Dv"), 0)
            state["pred_dv"] = sch.tok("Dv")
            cut("pred")
            emit_ln(is_ln2=True, precomputed=True)
            emit_apply_T(nrm_sb, rewrite=True)
            hT_t = sch.tok("Dv")
            cut("ln2")

            for g in range(NG):
                def _(t, g=g):
                    i = None
                    for dc in range(DCH):
                        i = t.matmul(pb[6][0:8, 0:512], gw_sb[:, dc * 8:(dc + 1) * 8],
                                     tT[:, dc * 2048 + g * 512:dc * 2048 + (g + 1) * 512],
                                     start=(dc == 0), stop=(dc == DCH - 1))
                    return i
                sec("PE", _, reads=("tT",), writes=("pb6",))

                def _(v, g=g):
                    return v.tensor_scalar(lgt[:], pb[6][0:8, 0:512], gb_sb[0:8, 0:1],
                                           None, OP.add)
                sec("VE", _, reads=("pb6",), writes=("lgt",))
                seal("VE")

                def _(t, g=g):
                    i = None
                    for k in range(4):
                        i = t.transpose(pb[5][:, k * 8:(k + 1) * 8],
                                        lgt[0:8, k * 128:(k + 1) * 128], identf[0:8, 0:8])
                    return i
                sec("PE", _, reads=("lgt",), writes=("pb5",))

                def _(v, g=g):
                    return v.tensor_copy(lgtm[:, g * 32:g * 32 + 32], pb[5][:, 0:32])
                sec("VE", _, reads=("pb5",), writes=("lgtm",))
            seal("VE")

            def _(v):
                L3 = lgtm[:].rearrange("p (s e) -> p s e", e=8)
                v.tensor_reduce(rm1[:], L3, axis=AX.X, op=OP.max)
                sch.drain_inc(v, "Dv")
                v.tensor_tensor(req1[:].rearrange("p (s e) -> p s e", e=8), L3,
                                rm1[:].rearrange("p s -> p s ()").broadcast_to([128, 16, 8]),
                                OP.is_equal)
                v.drain()
                v.scalar_tensor_tensor(rl2[:], req1[:], -1e30, lgtm[:], OP.mult, OP.add)
                sch.drain_inc(v, "Dv")
                v.tensor_reduce(rm2[:], rl2[:].rearrange("p (s e) -> p s e", e=8),
                                axis=AX.X, op=OP.max)
                sch.drain_inc(v, "Dv")
                v.tensor_tensor(req2[:].rearrange("p (s e) -> p s e", e=8),
                                rl2[:].rearrange("p (s e) -> p s e", e=8),
                                rm2[:].rearrange("p s -> p s ()").broadcast_to([128, 16, 8]),
                                OP.is_equal)
                v.drain()
                return v.tensor_tensor(rd12[:], rm1[:], rm2[:], OP.subtract)
            sec("VE", _, reads=("lgtm",), writes=("rtr",))
            seal("VE")

            def _(sc):
                return sc.activation(rw1[:], rd12[:], AF.Sigmoid)
            sec("SE", _, reads=("rtr",), writes=("rtr",))
            seal("SE")

            def _(v):
                v.tensor_scalar(rw2[:], rw1[:], -1.0, 1.0, OP.mult, OP.add)
                v.drain()
                v.tensor_tensor(rwa[:].rearrange("p (s e) -> p s e", e=8),
                                req1[:].rearrange("p (s e) -> p s e", e=8),
                                rw1[:].rearrange("p s -> p s ()").broadcast_to([128, 16, 8]),
                                OP.mult)
                v.tensor_tensor(rwf[:].rearrange("p (s e) -> p s e", e=8),
                                req2[:].rearrange("p (s e) -> p s e", e=8),
                                rw2[:].rearrange("p s -> p s ()").broadcast_to([128, 16, 8]),
                                OP.mult)
                v.drain()
                return v.tensor_tensor(rwf[:], rwf[:], rwa[:], OP.add)
            sec("VE", _, reads=("rtr",), writes=("rwf",))
            seal("VE")

            for g in range(NG):
                def _(t, g=g):
                    i = None
                    for k in range(4):
                        sub = g * 4 + k
                        i = t.transpose(pb[5][0:8, k * 128:(k + 1) * 128],
                                        rwf[:, sub * 8:(sub + 1) * 8], identf[:])
                    return i
                sec("PE", _, reads=("rwf",), writes=("pb5",))

                def _(v, g=g):
                    return v.tensor_scalar(wTb[:, g * 512:(g + 1) * 512], pb[5][0:8, 0:512],
                                           1.0 / GELU_A, None, OP.mult)
                sec("VE", _, reads=("pb5",), writes=("wTb",))
            seal("VE")
            cut("router")

            wtb_t2 = sch.tok("Dv")
            NT = NE * NG
            state["t1war"] = [("Dv", wtb_t2)] * 4
            state["m2war"] = [wtb_t2, wtb_t2]
            state["pb6war"] = ("Dv", wtb_t2)
            state["wbrR"] = [0, 0]
            state["actR"] = [0, 0]
            prev = {}
            w1tok, w2tok = {}, {}
            cx_toks = {}

            def _(s):
                sch.wait(s, "GE", "Dv", state["pred_dv"])
                sch.wait(s, "GE", "Pe", state["w1_last"])
                i = s.dma_start(out=w1b[:, 0:8192], in_=d_w1[0][:, 0:8192])
                sch.inc(i, "Cw1", 16)
                i = s.dma_start(out=w1b[:, 8192:16384], in_=d_w1[0][:, 8192:16384])
                sch.inc(i, "Cw1", 16)
                sch.wait(s, "GE", "Pe", state["w2_last"])
                i = s.dma_start(out=w2b[:], in_=d_w2[0])
                sch.inc(i, "Cw2", 16)
            ENG["GE"](_)
            w1tok[0] = (sch.tok("Cw1") - 16, sch.tok("Cw1"))
            w2tok[0] = sch.tok("Cw2")

            for n in range(NT + 1):
                cur = n if n < NT else None
                prv = n - 1 if n >= 1 else None
                par = n % 2
                if cur is not None:
                    e_c, t_c = divmod(cur, NG)

                f_pe0 = sch.tok("Pe")
                f_ac0 = sch.tok("Ac")
                f_dv0 = sch.tok("Dv")
                f_pl0 = sch.tok("Pl")
                xacc_base = f_dv0 + (32 if cur is not None else 0)

                def p_eg(c, b=f_ac0):
                    return b + (2 if c == 0 else 3 * c + 1)

                def p_el(c, b=f_ac0):
                    return b + (3 if c == 0 else 3 * c + 2)

                def p_si(c, b=f_ac0):
                    return b + (3 * c + 6 if c <= 6 else 26)

                def p_lcb(c, b=f_dv0):
                    return b + (2 if c == 0 else 4 * c)

                def p_a0(c, b=f_dv0):
                    return b + (4 * (c + 1) + 1 if c <= 6 else 31)

                def p_a0d(c, b=f_dv0):
                    return b + (4 * (c + 1) + 2 if c <= 6 else 32)

                m2_new = []

                def _(t, cur=cur, prv=prv, xacc_base=xacc_base, f_pe0=f_pe0,
                      p_eg=p_eg, p_el=p_el, f_pl0=f_pl0, m2_new=m2_new):
                    if cur is not None:
                        e, t_i = divmod(cur, NG)
                        sch.wait(t, "PE", "Dv", wtb_t2)
                        sch.wait(t, "PE", state["pb6war"][0], state["pb6war"][1])
                        i = t.matmul(pb[6][:, 0:512], oneh[:, e * 128:(e + 1) * 128],
                                     wTb[0:8, t_i * 512:(t_i + 1) * 512],
                                     start=True, stop=True)
                        sch.inc(i, "Pe")
                        sch.wait(t, "PE", "Cw1", w1tok[e][0])
                        rrt = R["tT"].w
                        sch.wait(t, "PE", rrt[0], rrt[1] + rrt[2])
                        for c in range(ICH):
                            if c == 4:
                                sch.wait(t, "PE", "Cw1", w1tok[e][1])
                            for which in (0, 1):
                                fc = 2 * c + which
                                bi = (c % 2) if which == 0 else 2 + c % 2
                                bk = pb[bi]
                                if c >= 2:
                                    sch.wait(t, "PE", "Ac",
                                             p_eg(c - 2) if which == 0 else p_el(c - 2))
                                else:
                                    s_, t_ = state["t1war"][bi]
                                    sch.wait(t, "PE", s_, t_)
                                for dc in range(DCH):
                                    i = t.matmul(
                                        bk[:, 0:512],
                                        w1b[:, fc * 1024 + dc * 128:fc * 1024 + (dc + 1) * 128],
                                        tT[:, dc * 2048 + t_i * 512:dc * 2048 + (t_i + 1) * 512],
                                        start=(dc == 0), stop=(dc == DCH - 1))
                                sch.inc(i, "Pe")
                    if prv is not None:
                        ep, tp = divmod(prv, NG)
                        sch.wait(t, "PE", "Cw2", w2tok[ep])
                        sch.wait(t, "PE", "Pl", f_pl0)
                        for k in range(8):
                            ns, dh = divmod(k, 2)
                            c2 = tp * 4 + ns
                            bk = pb[4 + k % 2]
                            if k >= 2:
                                sch.wait(t, "PE", "Dv", xacc_base + (k - 2) + 1)
                            else:
                                sch.wait(t, "PE", "Dv", state["m2war"][k % 2])
                            first = True
                            if ep == 0:
                                t.matmul(bk[:, 0:512], wTb[0:8, c2 * 128:(c2 + 1) * 128],
                                         b2_sb[0:8, dh * 512:(dh + 1) * 512],
                                         start=True, stop=False)
                                first = False
                            for ic in range(ICH):
                                i = t.matmul(
                                    bk[:, 0:512],
                                    actb[prv % 2][:, ic * 512 + ns * 128:ic * 512 + (ns + 1) * 128],
                                    w2b[:, ic * 1024 + dh * 512:ic * 1024 + (dh + 1) * 512],
                                    start=(first and ic == 0), stop=(ic == ICH - 1))
                                first = False
                            sch.inc(i, "Pe")
                            m2_new.append(sch.tok("Pe"))
                ENG["PE"](_)
                exp_pe = (1 + 16 if cur is not None else 0) + (8 if prv is not None else 0)
                assert sch.tok("Pe") == f_pe0 + exp_pe

                if cur is not None:
                    def _(sc, cur=cur, par=par, f_pe0=f_pe0, f_ac0=f_ac0,
                          p_lcb=p_lcb, p_a0=p_a0, prev=prev):
                        e, t_i = divmod(cur, NG)
                        sch.wait(sc, "SE", "Pe", f_pe0 + 1)
                        sch.wait(sc, "SE", "Pl", state["wbrR"][par])
                        i = sc.copy(wbr[par][:], pb[6][:, 0:512])
                        sch.inc(i, "Ac")

                        def eg(c):
                            bgc = b1_sb[:, e * 16 + 2 * c:e * 16 + 2 * c + 1]
                            sch.wait(sc, "SE", "Pe", f_pe0 + 2 + 2 * c)
                            if c >= 2:
                                sch.wait(sc, "SE", "Dv", p_lcb(c - 2))
                            elif prev:
                                sch.wait(sc, "SE", "Dv", prev["lcb"][c])
                            i = sc.activation(g1p[c % 2][:], pb[c % 2][:, 0:512],
                                              AF.Identity, bias=bgc)
                            sch.inc(i, "Ac")

                        def el(c):
                            blc = b1_sb[:, e * 16 + 2 * c + 1:e * 16 + 2 * c + 2]
                            sch.wait(sc, "SE", "Pe", f_pe0 + 3 + 2 * c)
                            if c >= 2:
                                sch.wait(sc, "SE", "Dv", p_lcb(c - 2))
                            elif prev:
                                sch.wait(sc, "SE", "Dv", prev["lcb"][c])
                            i = sc.activation(l1p[c % 2][:], pb[2 + c % 2][:, 0:512],
                                              AF.Identity, bias=blc)
                            sch.inc(i, "Ac")

                        def si(c):
                            sch.wait(sc, "SE", "Dv", p_lcb(c))
                            if c >= 1:
                                sch.wait(sc, "SE", "Dv", p_a0(c - 1))
                            elif prev:
                                sch.wait(sc, "SE", "Dv", prev["a07"])
                            i = sc.activation(p1[:], g1p[c % 2][:], AF.Silu, scale=GELU_A)
                            sch.inc(i, "Ac")

                        eg(0); el(0)
                        for c in range(1, ICH):
                            eg(c); el(c); si(c - 1)
                        i = sc.copy(dscr[:, 16:32], dscr[:, 0:16])
                        sch.inc(i, "Ac")
                        si(ICH - 1)
                        i = sc.copy(dscr[:, 16:32], dscr[:, 0:16])
                        sch.inc(i, "Ac")
                    ENG["SE"](_)
                    assert sch.tok("Ac") == f_ac0 + 27

                def _(v, cur=cur, prv=prv, f_pl0=f_pl0, m2_new=m2_new,
                      p_eg=p_eg, p_el=p_el, p_si=p_si):
                    if cur is not None:
                        def minglcb(c):
                            sch.wait(v, "VE", "Ac", p_eg(c) + 1)
                            i = v.tensor_scalar(g1p[c % 2][:], g1p[c % 2][:],
                                                LIMIT, None, OP.min)
                            sch.inc(i, "Dv")
                            sch.wait(v, "VE", "Ac", p_el(c) + 1)
                            i = v.tensor_scalar(lcb2[c % 2][:], l1p[c % 2][:],
                                                LIMIT, -LIMIT, OP.min, OP.max)
                            sch.inc(i, "Dv")

                        def a0op(c):
                            sch.wait(v, "VE", "Ac", p_si(c) + 1)
                            sch.wait(v, "VE", "Pl", f_pl0 + c)
                            i = v.scalar_tensor_tensor(a0[:], lcb2[c % 2][:], 1.0,
                                                       p1[:], OP.add, OP.mult)
                            sch.inc(i, "Dv")
                            i = v.tensor_copy(dscr[:, 32:48], dscr[:, 0:16])
                            sch.inc(i, "Dv")

                        minglcb(0)
                        for c in range(1, ICH):
                            minglcb(c)
                            a0op(c - 1)
                        a0op(ICH - 1)
                    if prv is not None:
                        tp = prv % NG
                        for k in range(8):
                            ns, dh = divmod(k, 2)
                            c2 = tp * 4 + ns
                            sch.wait(v, "VE", "Pe", m2_new[k])
                            xsl = x_sb[:, c2 * 1024 + dh * 512:c2 * 1024 + (dh + 1) * 512]
                            i = v.tensor_tensor(xsl, xsl, pb[4 + k % 2][:, 0:512], OP.add)
                            sch.inc(i, "Dv")
                ENG["VE"](_)
                exp_dv = (32 if cur is not None else 0) + (8 if prv is not None else 0)
                assert sch.tok("Dv") == f_dv0 + exp_dv

                if prv is not None and prv >= NT - NG:
                    tq = prv - (NT - NG)

                    def _(v):
                        sch.drain_inc(v, "Dv")
                    ENG["VE"](_)
                    seal_t = sch.tok("Dv")

                    def _(s, tq=tq, s_i=s_i, seal_t=seal_t):
                        sch.wait(s, "SY", "Dv", seal_t)
                        i = s.dma_start(
                            out=xout[s_i][4 * tq * 128:(4 * tq + 4) * 128]
                                .rearrange("(c p) d -> p c d", p=128),
                            in_=x_sb[:, 4 * tq * 1024:(4 * tq + 4) * 1024]
                                .rearrange("p (c d) -> p c d", c=4))
                        sch.inc(i, "So", 16)
                        if s_i < nseq - 1:
                            sch.wait(s, "SY", "So", sch.cnt["So"])
                            i = s.dma_start(
                                out=x_sb[:, 4 * tq * 1024:(4 * tq + 4) * 1024]
                                    .rearrange("p (c d) -> p c d", c=4),
                                in_=xin[s_i + 1][4 * tq * 128:(4 * tq + 4) * 128]
                                    .rearrange("(c p) d -> p c d", p=128))
                            sch.inc(i, "Cx", 16)
                    ENG["SY"](_)
                    if s_i < nseq - 1:
                        cx_toks[tq] = sch.tok("Cx")
                        R["x"].set_w("Cx", cx_toks[tq])

                if prv == NT - 1 and s_i < nseq - 1:
                    def _(sc, cx_toks=cx_toks):
                        i = None
                        for tq in range(NG):
                            sch.wait(sc, "SE", "Cx", cx_toks[tq])
                            for ns in range(4):
                                c = tq * 4 + ns
                                xc = x_sb[:, c * 1024:(c + 1) * 1024]
                                sc.activation(sqscr[:], xc[:, 0:512], AF.Square,
                                              accum_out=sqs[:, c:c + 1])
                                i = sc.activation(sqscr[:], xc[:, 512:1024], AF.Square,
                                                  accum_out=msq[:, c:c + 1])
                        sch.inc(i, "Ac")
                    ENG["SE"](_)
                    R["x"].add_r("Ac", sch.tok("Ac"))

                    def _(v, cx_toks=cx_toks):
                        i = None
                        for tq in range(NG):
                            sch.wait(v, "VE", "Cx", cx_toks[tq])
                            for ns in range(4):
                                c = tq * 4 + ns
                                i = v.reduce_sum(sums[:, c:c + 1],
                                                 x_sb[:, c * 1024:(c + 1) * 1024],
                                                 axis=AX.X)
                        sch.inc(i, "Dv")
                    ENG["VE"](_)
                    R["x"].add_r("Dv", sch.tok("Dv"))

                if cur is not None:
                    def _(p, cur=cur, par=par, f_ac0=f_ac0, p_a0d=p_a0d):
                        sch.wait(p, "GE", "Ac", f_ac0 + 2)
                        sch.wait(p, "GE", "Pe", state["actR"][par])
                        for c in range(ICH):
                            sch.wait(p, "GE", "Dv", p_a0d(c))
                            i = p.tensor_tensor(actb[par][:, c * 512:(c + 1) * 512],
                                                a0[:], wbr[par][:], OP.mult)
                            sch.inc(i, "Pl")
                        i = p.tensor_copy(dscr[:, 48:64], dscr[:, 0:16])
                        sch.inc(i, "Pl")
                    ENG["GE"](_)
                    assert sch.tok("Pl") == f_pl0 + 9

                if cur is not None and t_c == NG - 1 and e_c < NE - 1:
                    def _(s, e_c=e_c, f_pe0=f_pe0):
                        sch.wait(s, "GE", "Pe", f_pe0 + 9)
                        i = s.dma_start(out=w1b[:, 0:8192], in_=d_w1[e_c + 1][:, 0:8192])
                        sch.inc(i, "Cw1", 16)
                        sch.wait(s, "GE", "Pe", f_pe0 + 17)
                        i = s.dma_start(out=w1b[:, 8192:16384],
                                        in_=d_w1[e_c + 1][:, 8192:16384])
                        sch.inc(i, "Cw1", 16)
                    ENG["GE"](_)
                    w1tok[e_c + 1] = (sch.tok("Cw1") - 16, sch.tok("Cw1"))
                if cur is not None and t_c == 0 and e_c >= 1:
                    def _(s, e_c=e_c, f_pe0=f_pe0):
                        sch.wait(s, "GE", "Pe", f_pe0 + 25)
                        i = s.dma_start(out=w2b[:], in_=d_w2[e_c])
                        sch.inc(i, "Cw2", 16)
                    ENG["GE"](_)
                    w2tok[e_c] = sch.tok("Cw2")
                if cur is not None:
                    state["t1war"] = [("Ac", p_eg(6)), ("Ac", p_eg(7)),
                                      ("Ac", p_el(6)), ("Ac", p_el(7))]
                    state["pb6war"] = ("Ac", f_ac0 + 1)
                    state["wbrR"][par] = f_pl0 + 9
                    prev = {"lcb": (p_lcb(6), p_lcb(7)), "a07": p_a0(7)}
                    if cur == NT - 1:
                        state["w1_last"] = f_pe0 + 17
                if prv is not None:
                    state["m2war"] = [xacc_base + 7, xacc_base + 8]
                    state["actR"][prv % 2] = m2_new[-1]
                    if prv == NT - 1:
                        state["w2_last"] = m2_new[-1]
                cut(f"stage_{n}")
            def _(v):
                sch.drain_inc(v, "Dv")
            ENG["VE"](_)
            ffn_t = sch.tok("Dv")
            cut("full")
          except _CutDone:
            pass
          if True:
            if DBG_STOP:
                def _(v):
                    i = None
                    for ix, tt in enumerate((sums, sqs, msq, nm, nv, vpe, lnv, rstd,
                                             rsq, vv, mt2, lnw, alphav, cc1, cc2)):
                        i = v.tensor_copy(q2[:, 0:2048].bitcast(F32)[:, ix * 16:(ix + 1) * 16], tt[:])
                    q2f = q2[:, 0:2048].bitcast(F32)
                    v.tensor_copy(q2f[:, 256:384], lgtm[:])
                    v.tensor_copy(q2f[:, 384:512], rwf[:])
                    v.tensor_copy(q2f[:, 512:528], rm1[:])
                    v.tensor_copy(q2f[:, 528:544], rm2[:])
                    v.tensor_copy(q2f[:, 544:560], rw1[:])
                    i = v.tensor_copy(q2f[:, 560:688], req1[:])
                    return i
                sec("VE", _, reads=("stats",), writes=("q2",))
            ffn_t = seal("VE")
            se_t = seal("SE")
            pl_t = seal("GE")

            def _(s):
                sch.wait(s, "SY", "Dv", ffn_t)
                sch.wait(s, "SY", "Ac", se_t)
                sch.wait(s, "SY", "Pl", pl_t)
                sch.wait(s, "SY", "Pe", sch.tok("Pe"))
                for cn in ("Cq", "Cz1", "Cz2", "Cw1", "Cw2"):
                    sch.wait(s, "SY", cn, sch.tok(cn))
                i = s.dma_start(out=d_dbg[:, :],
                                in_=arena[:, 0:57600].bitcast(F32)[:, 0:28800])
                sch.inc(i, "So", 16)
            if DBG_STOP and DBG_DUMP:
                ENG["SY"](_)

          if True:
            state["store"] = sch.tok("So")

        def _(g_):
            g_.wait_ge(sch.sem["So"], sch.cnt["So"])
        ENG["GE"](_)

    return nc


_NC_CACHE = None


def kernel(**inputs):
    global _NC_CACHE
    maps = _host_prep(inputs)
    if _NC_CACHE is None:
        _NC_CACHE = build()
    res = run_bass_kernel_spmd(_NC_CACHE, maps, core_ids=list(range(CORES)))
    out = np.empty((B, S, D), np.float32)
    for c in range(CORES):
        out[c * SPC:(c + 1) * SPC] = res.results[c]["xout"]
    return out


if __name__ == "__main__":
    import time
    t0 = time.time()
    nc = build()
    n = sum(1 for _ in nc.all_instructions())
    print("build s:", round(time.time() - t0, 1), "instrs:", n)


# version 12
# speedup vs baseline: 1.0949x; 1.0283x over previous
import contextlib
import numpy as np
import ml_dtypes
import concourse.bass as bass
import concourse.mybir as mybir
from concourse.bass_utils import run_bass_kernel_spmd
from concourse.alu_op_type import AluOpType as OP

F32 = mybir.dt.float32
BF16 = mybir.dt.bfloat16
AF = mybir.ActivationFunctionType
AX = mybir.AxisListType

B, S, D, H, DH, NE = 32, 2048, 1024, 8, 64, 8
CORES = 8
SPC = B // CORES
NCH, NG, NPAIR, FCH, ICH, DCH = 16, 4, 4, 16, 8, 8
EPS_LN = 1e-5
EPS_RMS = float(np.finfo(np.float32).eps)
GELU_A = 1.702
LIMIT = 7.0
bf16 = ml_dtypes.bfloat16

DBG_STOP = None
DBG_DUMP = True
DBG_NSEQ = None


class _CutDone(Exception):
    pass


ARENA_OFF = {}


def _host_prep(inp):
    f = np.float32
    wq = np.asarray(inp["L0_attn_wq"], f)
    wk = np.asarray(inp["L0_attn_wk"], f)
    wv = np.asarray(inp["L0_attn_wv"], f)
    wo = np.asarray(inp["L0_attn_wo"], f)
    wg = np.asarray(inp["L0_attn_w_gates"], f)
    rmsw = np.asarray(inp["L0_attn_rms_w"], f)
    nrmw = np.asarray(inp["L0_ffn_norm_w"], f)
    gw = np.asarray(inp["L0_ffn_gate_w"], f)
    gb = np.asarray(inp["L0_ffn_gate_b"], f)
    w1 = np.asarray(inp["L0_ffn_mlp1_w"], f)
    b1 = np.asarray(inp["L0_ffn_mlp1_b"], f)
    w2 = np.asarray(inp["L0_ffn_mlp2_w"], f)
    b2 = np.asarray(inp["L0_ffn_mlp2_b"], f)
    tok = np.asarray(inp["L0_attn_tokens"], f)

    def pairstack(w):
        a = np.stack([np.concatenate([w[2 * p], w[2 * p + 1]], axis=1) for p in range(NPAIR)])
        a = a.reshape(NPAIR, DCH, 128, 128).transpose(0, 2, 1, 3)
        return np.ascontiguousarray(a.reshape(NPAIR, 128, DCH * 128))

    wq_h, wk_h, wv_h = pairstack(wq), pairstack(wk), pairstack(wv)
    wo_h = np.ascontiguousarray(
        np.stack([wo[2 * p:2 * p + 2].reshape(128, 1024) for p in range(NPAIR)]))
    wg_h = np.ascontiguousarray(wg.reshape(DCH, 128, H).transpose(1, 0, 2).reshape(128, DCH * H))
    rms_h = np.ascontiguousarray(rmsw.reshape(DCH, 128).T)
    nrm_h = np.ascontiguousarray(nrmw.reshape(DCH, 128).T)
    gw_h = np.ascontiguousarray(gw.T.reshape(DCH, 128, NE).transpose(1, 0, 2).reshape(128, DCH * NE))
    gb_h = np.ascontiguousarray(gb.reshape(NE, 1))

    perm = np.empty(2048, np.int64)
    for fc in range(FCH):
        perm[fc * 128:(fc + 1) * 128] = 256 * (fc // 2) + 2 * np.arange(128) + (fc % 2)
    w1p = w1[:, perm, :]
    a = w1p.transpose(0, 2, 1).reshape(NE, DCH, 128, FCH, 128).transpose(0, 2, 3, 1, 4)
    w1_h = np.ascontiguousarray(a.reshape(NE, 128, FCH * DCH * 128))
    b1_h = np.ascontiguousarray(
        b1[:, perm].reshape(NE, FCH, 128).transpose(2, 0, 1).reshape(128, NE * FCH))
    a = w2.transpose(0, 2, 1).reshape(NE, ICH, 128, 1024).transpose(0, 2, 1, 3)
    w2_h = np.ascontiguousarray(a.reshape(NE, 128, ICH * 1024))
    b2_h = np.ascontiguousarray(b2)

    identb = np.eye(128, dtype=bf16)
    identf = np.eye(128, dtype=np.float32)
    p_ = np.arange(128)[:, None]
    q_ = np.arange(512)[None, :]
    maskt = np.concatenate([(128 * c + p_ <= q_) for c in range(4)], axis=1).astype(bf16)
    psel = np.zeros((8, 512), bf16)
    for p in range(NPAIR):
        psel[2 * p, p * 128:p * 128 + 64] = 1
        psel[2 * p + 1, p * 128 + 64:p * 128 + 128] = 1
    oneh = np.zeros((8, 1024), bf16)
    for e in range(NE):
        oneh[e, e * 128:(e + 1) * 128] = 1
    onescol = np.ones((128, 1), bf16)
    onesrow = np.ones((1, 128), bf16)

    shared = dict(wq=wq_h, wk=wk_h, wv=wv_h, wo=wo_h, wg=wg_h, rmsw=rms_h,
                  nrmw=nrm_h, gw=gw_h, gb=gb_h, w1=w1_h, b1=b1_h, w2=w2_h, b2=b2_h,
                  identb=identb, identf=identf, maskt=maskt, psel=psel, oneh=oneh,
                  onescol=onescol, onesrow=onesrow)
    maps = []
    for c in range(CORES):
        m = dict(shared)
        m["xin"] = np.ascontiguousarray(tok[c * SPC:(c + 1) * SPC])
        maps.append(m)
    return maps


class Sch:
    def __init__(self):
        self.sem, self.cnt, self.lastw = {}, {}, {}

    def add(self, n, h):
        self.sem[n] = h
        self.cnt[n] = 0

    def inc(self, instr, s, n=1):
        instr.then_inc(self.sem[s], n)
        self.cnt[s] += n
        return self.cnt[s]

    def tok(self, s):
        return self.cnt[s]

    def wait(self, eng, en, s, val):
        if not val or val <= 0:
            return
        k = (en, s)
        if self.lastw.get(k, 0) >= val:
            return
        eng.wait_ge(self.sem[s], val)
        self.lastw[k] = val

    def drain_inc(self, eng, s):
        eng.drain()
        i = eng.nop()
        i.then_inc(self.sem[s], 1)
        self.cnt[s] += 1
        return self.cnt[s]


class Res:
    def __init__(self):
        self.w = None
        self.r = {}

    def set_w(self, sem, tok, slack=0):
        self.w = (sem, tok, slack)
        self.r = {}

    def add_r(self, sem, tok):
        if self.r.get(sem, 0) < tok:
            self.r[sem] = tok


def build():
    nc = bass.Bass("TRN2", target_bir_lowering=False)
    dt = nc.dram_tensor
    xin = dt("xin", [SPC, S, D], F32, kind="ExternalInput").ap()
    xout = dt("xout", [SPC, S, D], F32, kind="ExternalOutput").ap()
    d_wq = dt("wq", [NPAIR, 128, 1024], F32, kind="ExternalInput").ap()
    d_wk = dt("wk", [NPAIR, 128, 1024], F32, kind="ExternalInput").ap()
    d_wv = dt("wv", [NPAIR, 128, 1024], F32, kind="ExternalInput").ap()
    d_wo = dt("wo", [NPAIR, 128, 1024], F32, kind="ExternalInput").ap()
    d_wg = dt("wg", [128, 64], F32, kind="ExternalInput").ap()
    d_rms = dt("rmsw", [128, 8], F32, kind="ExternalInput").ap()
    d_nrm = dt("nrmw", [128, 8], F32, kind="ExternalInput").ap()
    d_gw = dt("gw", [128, 64], F32, kind="ExternalInput").ap()
    d_gb = dt("gb", [8, 1], F32, kind="ExternalInput").ap()
    d_w1 = dt("w1", [NE, 128, 16384], F32, kind="ExternalInput").ap()
    d_b1 = dt("b1", [128, 128], F32, kind="ExternalInput").ap()
    d_w2 = dt("w2", [NE, 128, 8192], F32, kind="ExternalInput").ap()
    d_b2 = dt("b2", [8, 1024], F32, kind="ExternalInput").ap()
    d_idb = dt("identb", [128, 128], BF16, kind="ExternalInput").ap()
    d_idf = dt("identf", [128, 128], F32, kind="ExternalInput").ap()
    d_mask = dt("maskt", [128, 2048], BF16, kind="ExternalInput").ap()
    d_psel = dt("psel", [8, 512], BF16, kind="ExternalInput").ap()
    d_oneh = dt("oneh", [8, 1024], BF16, kind="ExternalInput").ap()
    d_onec = dt("onescol", [128, 1], BF16, kind="ExternalInput").ap()
    d_oner = dt("onesrow", [1, 128], BF16, kind="ExternalInput").ap()
    d_zscr = dt("zscr", [4096], F32, kind="Internal").ap()
    d_dbg = dt("dbg", [128, 28800], F32, kind="ExternalOutput").ap() if DBG_STOP else None

    with contextlib.ExitStack() as es:
        sbt = lambda n, sh, d_: es.enter_context(nc.sbuf_tensor(n, sh, d_))
        pst = lambda n, sh, d_: es.enter_context(nc.psum_tensor(n, sh, d_))

        x_sb = sbt("x_sb", [128, NCH * 1024], F32)
        identb = sbt("s_identb", [128, 128], BF16)
        identf = sbt("s_identf", [128, 128], F32)
        maskt = sbt("s_maskt", [128, 2048], BF16)
        psel = sbt("s_psel", [8, 512], BF16)
        oneh = sbt("s_oneh", [8, 1024], BF16)
        onescol = sbt("s_onescol", [128, 1], BF16)
        onesrow = sbt("s_onesrow", [1, 128], BF16)
        rms_sb = sbt("rms_sb", [128, 8], F32)
        nrm_sb = sbt("nrm_sb", [128, 8], F32)
        wg_sb = sbt("wg_sb", [128, 64], BF16)
        gw_sb = sbt("gw_sb", [128, 64], BF16)
        gb_sb = sbt("gb_sb", [8, 1], F32)
        b1_sb = sbt("b1_sb", [128, 128], F32)
        b2_sb = sbt("b2_sb", [8, 1024], BF16)
        b2f = sbt("b2f", [8, 1024], F32)
        sums = sbt("sums", [128, 16], F32)
        sqs = sbt("sqs", [128, 16], F32)
        nm = sbt("nm", [128, 16], F32)
        msq = sbt("msq", [128, 16], F32)
        nv = sbt("nv", [128, 16], F32)
        vpe = sbt("vpe", [128, 16], F32)
        lnv = sbt("lnv", [128, 16], F32)
        rstd = sbt("rstd", [128, 16], F32)
        rsq = sbt("rsq", [128, 16], F32)
        vv = sbt("vv", [128, 16], F32)
        mt2 = sbt("mt2", [128, 16], F32)
        lnw = sbt("lnw", [128, 16], F32)
        alphav = sbt("alphav", [128, 16], F32)
        cc1 = sbt("cc1", [128, 16], F32)
        cc2 = sbt("cc2", [128, 16], F32)
        mcol = sbt("mcol", [1, 16], F32)
        msc = sbt("msc", [1, 4], F32)
        nmb1 = sbt("nmb1", [1, 1], BF16)
        negM = sbt("negM", [128, 1], F32)
        lgtm = sbt("lgtm", [128, 128], F32)
        rm1 = sbt("rm1", [128, 16], F32)
        rm2 = sbt("rm2", [128, 16], F32)
        req1 = sbt("req1", [128, 128], F32)
        rl2 = sbt("rl2", [128, 128], F32)
        req2 = sbt("req2", [128, 128], F32)
        rd12 = sbt("rd12", [128, 16], F32)
        rw1 = sbt("rw1", [128, 16], F32)
        rw2 = sbt("rw2", [128, 16], F32)
        rwa = sbt("rwa", [128, 128], F32)
        rwf = sbt("rwf", [128, 128], F32)
        sqscr = sbt("sqscr", [128, 512], F32)
        dscr = sbt("dscr", [128, 64], F32)

        AR_COLS = 61440
        arena = sbt("arena", [128, AR_COLS], BF16)

        class Arena:
            def __init__(self):
                self.off = 0

            def take(self, cols):
                cols = (cols + 31) & ~31
                assert self.off + cols <= AR_COLS, f"arena overflow {self.off}+{cols}"
                o = self.off
                self.off += cols
                return o

            def b16(self, cols, parts=128, name=None):
                o = self.take(cols)
                if name:
                    ARENA_OFF[name] = (o, cols, 'b16', parts)
                return arena[0:parts, o:o + cols]

            def f32(self, cols, parts=128, name=None):
                o = self.take(2 * cols)
                if name:
                    ARENA_OFF[name] = (o, 2 * cols, 'f32', parts)
                return arena[0:parts, o:o + 2 * cols].bitcast(F32)

        A = Arena()
        tbf = A.b16(4096, name='tbf')
        tT = A.b16(16384, name='tT')
        mark = A.off
        qp = A.b16(2048, name='qp')
        kp = A.b16(2048, name='kp')
        vp = A.b16(2080, name='vp')
        q2 = A.b16(2048, name='q2')
        wqkv = A.b16(1024, name='wqkv')
        woall = A.b16(4096, name='woall')
        gTb = A.b16(2048, parts=8, name='gTb')
        outT = A.b16(8192, name='outT')
        Ebufs = [A.b16(512, name=f'E{_i}') for _i in range(6)]
        zst = A.b16(4096, parts=1, name='zst')
        zbuf = A.b16(2048, parts=8, name='zbuf')
        crec8 = A.f32(2048, parts=8, name='crec')
        cgb = A.b16(2048, parts=8, name='cgb')
        attn_end = A.off
        A.off = mark
        w1b = A.b16(16384, name='w1b')
        w2b = A.b16(8192, name='w2b')
        actb = [A.b16(4096, name='act0'), tbf]
        wbr = [A.b16(512, name='wbr0'), A.b16(512, name='wbr1')]
        g1p = [A.f32(512, name='g1a'), A.f32(512, name='g1b')]
        l1p = [A.f32(512, name='l1a'), A.f32(512, name='l1b')]
        lcb2 = [A.f32(512, name='lcba'), A.f32(512, name='lcbb')]
        p1 = A.f32(512, name='p1')
        a0 = A.f32(512, name='a0')
        lgt = A.f32(512, parts=8, name='lgt')
        wTb = A.b16(2048, parts=8, name='wTb')
        ffn_end = A.off
        k2b = tbf
        wqkvB = arena[0:128, ARENA_OFF['lgt'][0]:ARENA_OFF['lgt'][0] + 1024]
        wbuf = [wqkv, wqkvB]
        wbufn = ["wqkv", "wqkvB"]
        assert max(attn_end, ffn_end) <= AR_COLS, (attn_end, ffn_end)

        pb = [pst(f"pb{i}", [128, 512], F32) for i in range(7)]
        pbt = pst("pbt", [128, 1024], BF16)

        sch = Sch()
        for n in ["Cs", "Cg", "Cx", "Cq", "Cz1", "Cz2", "Cw1", "Cw2",
                  "Pe", "Ac", "Dv", "Pl", "So"]:
            sch.add(n, es.enter_context(nc.semaphore(n)))
        block = es.enter_context(nc.Block())
        ENG = {"PE": block.tensor, "VE": block.vector, "SE": block.scalar,
               "GE": block.gpsimd, "SY": block.sync}
        SEM = {"PE": "Pe", "VE": "Dv", "SE": "Ac", "GE": "Pl"}

        R = {}
        for name in (["wqkvB", "k2b"]
                     + ["x", "tbf", "tT", "qp", "kp", "vp", "q2", "wo", "gT", "outT",
                      "zst", "zbuf", "crec", "cgb", "w1", "w2", "act", "g1", "lc",
                      "p1", "a0", "lgt", "wTb", "lgtm", "rwf", "stats", "mh",
                      "negM", "b2", "rtr", "wqkv", "sqscr"]
                     + [f"pb{i}" for i in range(7)] + ["pbt"]
                     + [f"wbr{i}" for i in range(2)]):
            R[name] = Res()

        def sec(en, fn, reads=(), writes=(), war_extra=()):
            sem = SEM.get(en)

            def body(eng):
                for rn in reads:
                    rr = R[rn]
                    if rr.w and rr.w[0] != sem:
                        sch.wait(eng, en, rr.w[0], rr.w[1] + rr.w[2])
                for wn in writes:
                    rr = R[wn]
                    for s_, t_ in rr.r.items():
                        if s_ != sem:
                            sch.wait(eng, en, s_, t_)
                    if rr.w and rr.w[0] != sem:
                        sch.wait(eng, en, rr.w[0], rr.w[1])
                for s_, t_ in war_extra:
                    sch.wait(eng, en, s_, t_)
                out = fn(eng)
                if out is not None and sem is not None:
                    sch.inc(out, sem)
            ENG[en](body)
            tok = sch.tok(sem) if sem else None
            for rn in reads:
                if sem:
                    R[rn].add_r(sem, tok)
            for wn in writes:
                if sem:
                    R[wn].set_w(sem, tok, 1 if en in ("VE", "SE", "GE") else 0)
            return tok

        def seal(en):
            sem = SEM[en]

            def body(eng):
                sch.drain_inc(eng, sem)
            ENG[en](body)
            t = sch.tok(sem)
            for rr in R.values():
                if rr.w and rr.w[0] == sem and rr.w[2] and rr.w[1] < t:
                    rr.w = (sem, t, 0)
            return t

        def _(s):
            for dst, src in ((identf, d_idf), (rms_sb, d_rms), (nrm_sb, d_nrm),
                             (gb_sb, d_gb), (b1_sb, d_b1), (b2f, d_b2)):
                i = s.dma_start(out=dst[:], in_=src[:, :])
                sch.inc(i, "Cs", 16)
        ENG["SY"](_)
        lx0 = sch.tok("Cs")

        def _(g):
            for dst, src in ((identb, d_idb), (maskt, d_mask), (psel, d_psel),
                             (oneh, d_oneh), (onescol, d_onec), (onesrow, d_oner),
                             (wg_sb, d_wg), (gw_sb, d_gw)):
                i = g.dma_start(out=dst[:], in_=src[:, :])
                sch.inc(i, "Cg", 16)
        ENG["GE"](_)
        lw0 = sch.tok("Cg")

        def _(v):
            v.memset(dscr[:], 0.0)
            return v.tensor_scalar(b2_sb[:], b2f[:], GELU_A, None, OP.mult)
        sec("VE", _, writes=("b2",), war_extra=(("Cs", lx0),))
        for _en, _eng in (("PE", "tensor"), ("VE", "vector"), ("SE", "scalar"), ("GE", "gpsimd")):
            def _w(e, _en=_en):
                sch.wait(e, _en_cur[0], "Cs", lx0)
                sch.wait(e, _en_cur[0], "Cg", lw0)
            _en_cur = [_en]
            ENG[_en](_w)
        seal("VE")

        state = {"store": 0, "w1_last": 0, "w2_last": 0}

        def cut(label):
            if DBG_STOP == label:
                raise _CutDone()

        nseq = DBG_NSEQ if DBG_NSEQ is not None else SPC
        for s_i in range(nseq):
          try:
            if s_i == 0:
                def _(s, s_i=s_i):
                    sch.wait(s, "SY", "So", state["store"])
                    for s_, t_ in R["x"].r.items():
                        sch.wait(s, "SY", s_, t_)
                    i = s.dma_start(out=x_sb[:].rearrange("p (c d) -> p c d", c=NCH),
                                    in_=xin[s_i].rearrange("(c p) d -> p c d", p=128))
                    sch.inc(i, "Cx", 16)
                ENG["SY"](_)
                R["x"].set_w("Cx", sch.tok("Cx"))

            cut("load")
            def emit_ln(is_ln2, precomputed=False):
                if not precomputed:
                    def _(sc):
                        i = None
                        for c in range(NCH):
                            xc = x_sb[:, c * 1024:(c + 1) * 1024]
                            sc.activation(sqscr[:], xc[:, 0:512], AF.Square,
                                          accum_out=sqs[:, c:c + 1])
                            i = sc.activation(sqscr[:], xc[:, 512:1024], AF.Square,
                                              accum_out=msq[:, c:c + 1])
                        return i
                    sec("SE", _, reads=("x",), writes=("sqscr", "stats"))
                    sq_t = seal("SE")

                    def _(v):
                        i = None
                        for c in range(NCH):
                            i = v.reduce_sum(sums[:, c:c + 1],
                                             x_sb[:, c * 1024:(c + 1) * 1024], axis=AX.X)
                        return i
                    sec("VE", _, reads=("x",), writes=("stats",))
                    seal("VE")
                else:
                    sq_t = seal("SE")

                def _(v):
                    v.tensor_tensor(vv[:], sqs[:], msq[:], OP.add)
                    v.tensor_scalar(nm[:], sums[:], -1.0 / D, None, OP.mult)
                    v.drain()
                    v.tensor_scalar(msq[:], vv[:], 1.0 / D, None, OP.mult)
                    v.tensor_tensor(vpe[:], nm[:], nm[:], OP.mult)
                    v.drain()
                    v.tensor_tensor(nv[:], vpe[:], msq[:], OP.subtract)
                    v.drain()
                    return v.tensor_scalar(vpe[:], nv[:], -1.0, EPS_LN, OP.mult, OP.add)
                sec("VE", _, reads=("stats",), writes=("stats",), war_extra=(("Ac", sq_t),))
                seal("VE")

                def _(sc):
                    sc.activation(lnv[:], vpe[:], AF.Ln)
                    sc.drain()
                    sc.activation(rstd[:], lnv[:], AF.Exp, scale=-0.5)
                    return sc.activation(rsq[:], lnv[:], AF.Exp, scale=-1.0)
                sec("SE", _, reads=("stats",), writes=("stats",))
                seal("SE")

                def _(v):
                    v.tensor_scalar(vv[:], nv[:], -1.0, None, OP.mult)
                    v.drain()
                    v.tensor_tensor(mt2[:], vv[:], rsq[:], OP.mult)
                    v.drain()
                    return v.tensor_scalar(mt2[:], mt2[:], EPS_RMS, None, OP.add)
                sec("VE", _, reads=("stats",), writes=("stats",))
                seal("VE")

                def _(sc):
                    sc.activation(lnw[:], mt2[:], AF.Ln)
                    sc.drain()
                    return sc.activation(vv[:], lnw[:], AF.Exp, scale=-0.5)
                sec("SE", _, reads=("stats",), writes=("stats",))
                seal("SE")

                def _(v):
                    i = v.tensor_tensor(alphav[:], rstd[:], vv[:], OP.mult)
                    if is_ln2:
                        v.tensor_scalar(cc1[:], rstd[:], 1.0, None, OP.add)
                        i = v.tensor_tensor(cc2[:], nm[:], rstd[:], OP.mult)
                    return i
                sec("VE", _, reads=("stats",), writes=("stats",))
                seal("VE")

            def emit_apply_T(wvec_sb, rewrite, war_pe=0):
                for g in range(NG):
                    def _(v, g=g):
                        i = None
                        for k in range(4):
                            c = g * 4 + k
                            i = v.tensor_scalar(tbf[:, k * 1024:(k + 1) * 1024],
                                                x_sb[:, c * 1024:(c + 1) * 1024],
                                                nm[:, c:c + 1], alphav[:, c:c + 1],
                                                OP.add, OP.mult)
                        return i
                    sec("VE", _, reads=("stats", "x"), writes=("tbf",),
                        war_extra=((("Pe", war_pe),) if g == 0 and war_pe else ()))
                    seal("VE")

                    for dcp in range(4):
                        def _(t, g=g, dcp=dcp):
                            i = None
                            for half, dc in ((0, 2 * dcp), (512, 2 * dcp + 1)):
                                for k in range(4):
                                    i = t.transpose(
                                        pbt[:, half + k * 128:half + (k + 1) * 128],
                                        tbf[:, k * 1024 + dc * 128:k * 1024 + (dc + 1) * 128],
                                        identb[:])
                            return i
                        sec("PE", _, reads=("tbf",), writes=("pbt",))

                        def _(v, g=g, dcp=dcp):
                            i = None
                            for half, dc in ((0, 2 * dcp), (512, 2 * dcp + 1)):
                                i = v.tensor_scalar(
                                    tT[:, dc * 2048 + g * 512:dc * 2048 + (g + 1) * 512],
                                    pbt[:, half:half + 512], wvec_sb[:, dc:dc + 1],
                                    None, OP.mult)
                            return i
                        sec("VE", _, reads=("pbt",), writes=("tT",))

                    if rewrite:
                        def _(v, g=g):
                            i = None
                            for k in range(4):
                                c = g * 4 + k
                                i = v.tensor_scalar(x_sb[:, c * 1024:(c + 1) * 1024],
                                                    x_sb[:, c * 1024:(c + 1) * 1024],
                                                    cc1[:, c:c + 1], cc2[:, c:c + 1],
                                                    OP.mult, OP.add)
                            return i
                        sec("VE", _, reads=("stats",), writes=("x",))
                seal("VE")

            emit_ln(is_ln2=False, precomputed=(s_i > 0))
            cut("ln1")
            emit_apply_T(rms_sb, rewrite=False, war_pe=state["w2_last"])
            cut("applyT")

            g_pe0 = sch.tok("Pe")

            def _(t, g_pe0=g_pe0):
                rrt = R["tT"].w
                sch.wait(t, "PE", rrt[0], rrt[1] + rrt[2])
                for s_, t_ in R["pb6"].r.items():
                    sch.wait(t, "PE", s_, t_)
                for g in range(NG):
                    if g > 0:
                        sch.wait(t, "PE", "Ac", sch.tok("Ac") + g)
                    for dc in range(DCH):
                        i = t.matmul(pb[6][0:8, 0:512], wg_sb[:, dc * 8:(dc + 1) * 8],
                                     tT[:, dc * 2048 + g * 512:dc * 2048 + (g + 1) * 512],
                                     start=(dc == 0), stop=(dc == DCH - 1))
                    sch.inc(i, "Pe")
            ENG["PE"](_)

            def _(sc, g_pe0=g_pe0):
                for g in range(NG):
                    sch.wait(sc, "SE", "Pe", g_pe0 + g + 1)
                    i = sc.activation(gTb[:, g * 512:(g + 1) * 512],
                                      pb[6][0:8, 0:512], AF.Sigmoid)
                    sch.inc(i, "Ac")
                i = sc.copy(dscr[:, 16:32], dscr[:, 0:16])
                sch.inc(i, "Ac")
            ENG["SE"](_)
            R["gT"].set_w("Ac", sch.tok("Ac"), 0)
            R["pb6"].set_w("Pe", sch.tok("Pe"), 0)
            R["pb6"].add_r("Ac", sch.tok("Ac"))
            cut("gates")

            def _(g_, s_i=s_i):
                for s_, t_ in R["wo"].r.items():
                    sch.wait(g_, "GE", s_, t_)
                sch.wait(g_, "GE", "Pe", state["w1_last"])
                i = g_.dma_start(out=woall[:].rearrange("r (p c) -> r p c", p=NPAIR),
                                 in_=d_wo.rearrange("p r c -> r p c"))
                sch.inc(i, "Cq", 16)
                R["wo"].set_w("Cq", sch.cnt["Cq"])
            ENG["GE"](_)

            state["prev_av"] = 0
            for pair in range(NPAIR):
                state["attn_ep"] = sch.tok("Dv")
                def wdma(src, bufi, pair=pair):
                    def _(g_):
                        for s_, t_ in R[wbufn[bufi]].r.items():
                            sch.wait(g_, "GE", s_, t_)
                        if bufi == 1:
                            rl = R["lgt"]
                            if rl.w:
                                sch.wait(g_, "GE", rl.w[0], rl.w[1] + rl.w[2])
                            for s_, t_ in rl.r.items():
                                sch.wait(g_, "GE", s_, t_)
                        sch.wait(g_, "GE", "Pe", state["w1_last"])
                        i = g_.dma_start(out=wbuf[bufi][:], in_=src[pair])
                        sch.inc(i, "Cq", 16)
                        R[wbufn[bufi]].set_w("Cq", sch.cnt["Cq"])
                    ENG["GE"](_)

                if pair == 0:
                    wdma(d_wq, 0)
                    wdma(d_wk, 1)

                for which in (0, 1):
                    dstq = qp if which == 0 else kp
                    bres = "qp" if which == 0 else "kp"
                    wbi = (pair + which) % 2
                    qk_pe0 = sch.tok("Pe")
                    qk_ac0 = sch.tok("Ac")

                    def _(t, qk_pe0=qk_pe0, qk_ac0=qk_ac0, wbi=wbi):
                        rr = R[wbufn[wbi]].w
                        sch.wait(t, "PE", rr[0], rr[1])
                        rrt = R["tT"].w
                        sch.wait(t, "PE", rrt[0], rrt[1] + rrt[2])
                        for bn in ("pb5", "pb6"):
                            for s_, t_ in R[bn].r.items():
                                sch.wait(t, "PE", s_, t_)
                        for g in range(NG):
                            bk = pb[5 + (g % 2)]
                            if g >= 2:
                                sch.wait(t, "PE", "Ac", qk_ac0 + g - 1)
                            for dc in range(DCH):
                                i = t.matmul(bk[:, 0:512], wbuf[wbi][:, dc * 128:(dc + 1) * 128],
                                             tT[:, dc * 2048 + g * 512:dc * 2048 + (g + 1) * 512],
                                             start=(dc == 0), stop=(dc == DCH - 1))
                            sch.inc(i, "Pe")
                    ENG["PE"](_)

                    def _(sc, dstq=dstq, qk_pe0=qk_pe0, which=which):
                        for g in range(NG):
                            sch.wait(sc, "SE", "Pe", qk_pe0 + g + 1)
                            i = sc.copy(dstq[:, g * 512:(g + 1) * 512], pb[5 + (g % 2)][:, 0:512])
                            sch.inc(i, "Ac")
                        sqdst = q2 if which == 0 else k2b
                        rn = "q2" if which == 0 else "k2b"
                        for s_, t_ in R[rn].r.items():
                            sch.wait(sc, "SE", s_, t_)
                        i = sc.activation(sqdst[:, 0:2048], dstq[:], AF.Square)
                        sch.inc(i, "Ac")
                    ENG["SE"](_)
                    R["q2" if which == 0 else "k2b"].set_w("Ac", sch.tok("Ac"), 1)
                    if which == 1:
                        seal("SE")
                    R[wbufn[wbi]].add_r("Pe", sch.tok("Pe"))
                    if wbi == 1:
                        R["lgt"].add_r("Pe", sch.tok("Pe"))
                    R[bres].set_w("Ac", sch.tok("Ac"), 1)
                    R["pb5"].set_w("Pe", qk_pe0 + 3, 0)
                    R["pb6"].set_w("Pe", qk_pe0 + 4, 0)
                    R["pb5"].add_r("Ac", qk_ac0 + 3)
                    R["pb6"].add_r("Ac", qk_ac0 + 4)
                wdma(d_wv, pair % 2)

                def _(t):
                    for g4 in range(4):
                        i = t.matmul(pb[g4][0:1, 0:512], onescol[:],
                                     q2[:, g4 * 512:(g4 + 1) * 512], start=True, stop=True)
                        sch.inc(i, "Pe")
                    return None
                sec("PE", _, reads=("q2",), writes=("pb0", "pb1", "pb2", "pb3"))
                nq_t = sch.tok("Pe")
                for g4 in range(4):
                    R[f"pb{g4}"].set_w("Pe", nq_t, 0)

                def _(v):
                    sch.wait(v, "VE", "Pe", nq_t)
                    i = None
                    for g4 in range(4):
                        i = v.reduce_max(mcol[0:1, g4:g4 + 1], pb[g4][0:1, 0:512], axis=AX.X)
                    sch.drain_inc(v, "Dv")
                    return None
                sec("VE", _, reads=("pb0", "pb1", "pb2", "pb3"), writes=("mh",))
                qmax_t = sch.tok("Dv")
                R["mh"].set_w("Dv", qmax_t, 0)

                vp_pe0 = sch.tok("Pe")
                vp_dv0 = sch.tok("Dv")
                ev_tok = {}

                def _(t, vp_pe0=vp_pe0, wbi=pair % 2):
                    rr = R[wbufn[wbi]].w
                    sch.wait(t, "PE", rr[0], rr[1])
                    for bn in ("pb4", "pb5"):
                        for s_, t_ in R[bn].r.items():
                            sch.wait(t, "PE", s_, t_)
                    for c in range(12):
                        bk = pb[4 + c % 2]
                        if c >= 2:
                            sch.wait(t, "PE", "Dv", vp_dv0 + c - 1)
                        for dc in range(DCH):
                            i = t.matmul(bk[:, 0:128],
                                         tT[:, dc * 2048 + c * 128:dc * 2048 + (c + 1) * 128],
                                         wbuf[pair % 2][:, dc * 128:(dc + 1) * 128],
                                         start=(dc == 0), stop=(dc == DCH - 1))
                        sch.inc(i, "Pe")
                ENG["PE"](_)
                for c in range(12):
                    ev_tok[c] = vp_dv0 + c + 1

                def _(v, vp_pe0=vp_pe0):
                    for c in range(12):
                        bk = pb[4 + c % 2]
                        sch.wait(v, "VE", "Pe", vp_pe0 + c + 1)
                        v.tensor_copy(vp[:, c * 130:c * 130 + 64], bk[:, 0:64])
                        i = v.tensor_copy(vp[:, c * 130 + 65:c * 130 + 129], bk[:, 64:128])
                        sch.inc(i, "Dv")
                ENG["VE"](_)

                def _(t):
                    sch.wait(t, "PE", "Dv", qmax_t)
                    for g4 in range(4):
                        i = t.matmul(pb[g4][0:1, 0:512], onescol[:],
                                     k2b[:, g4 * 512:(g4 + 1) * 512], start=True, stop=True)
                        sch.inc(i, "Pe")
                    return None
                sec("PE", _, reads=("k2b",), writes=("pb0", "pb1", "pb2", "pb3"))
                nk_t = sch.tok("Pe")
                for g4 in range(4):
                    R[f"pb{g4}"].set_w("Pe", nk_t, 0)

                def _(v):
                    sch.wait(v, "VE", "Pe", nk_t)
                    for g4 in range(4):
                        v.reduce_max(mcol[0:1, 8 + g4:9 + g4], pb[g4][0:1, 0:512], axis=AX.X)
                    sch.drain_inc(v, "Dv")
                    v.reduce_max(msc[0:1, 0:1], mcol[0:1, 0:4], axis=AX.X)
                    v.reduce_max(msc[0:1, 1:2], mcol[0:1, 8:12], axis=AX.X)
                    sch.drain_inc(v, "Dv")
                    v.drain()
                    v.tensor_tensor(msc[0:1, 2:3], msc[0:1, 0:1], msc[0:1, 1:2], OP.mult)
                    sch.drain_inc(v, "Dv")
                    return None
                sec("VE", _, reads=("pb0", "pb1", "pb2", "pb3"), writes=("mh",))
                msc_t = sch.tok("Dv")
                R["mh"].set_w("Dv", msc_t, 0)

                def _(sc):
                    sch.wait(sc, "SE", "Dv", msc_t)
                    return sc.activation(msc[0:1, 3:4], msc[0:1, 2:3], AF.Sqrt)
                sec("SE", _, writes=("mh",))
                seal("SE")

                def _(v):
                    return v.tensor_scalar(nmb1[0:1, 0:1], msc[0:1, 3:4], -0.125, None, OP.mult)
                sec("VE", _, reads=("mh",), writes=("mh",))
                nmb1_t = sch.tok("Dv")

                ev_tok[12] = nmb1_t + 1
                ev_tok[13] = nmb1_t + 2

                def _(t, vp_pe0=vp_pe0):
                    for c in range(12, NCH):
                        bk = pb[4 + c % 2]
                        sch.wait(t, "PE", "Dv", ev_tok[c - 2])
                        for dc in range(DCH):
                            i = t.matmul(bk[:, 0:128],
                                         tT[:, dc * 2048 + c * 128:dc * 2048 + (c + 1) * 128],
                                         wbuf[pair % 2][:, dc * 128:(dc + 1) * 128],
                                         start=(dc == 0), stop=(dc == DCH - 1))
                        sch.inc(i, "Pe")
                ENG["PE"](_)
                vpB_pe0 = sch.tok("Pe") - 4

                def _(v, vpB_pe0=vpB_pe0):
                    for c in range(12, NCH):
                        bk = pb[4 + c % 2]
                        sch.wait(v, "VE", "Pe", vpB_pe0 + (c - 12) + 1)
                        v.tensor_copy(vp[:, c * 130:c * 130 + 64], bk[:, 0:64])
                        i = v.tensor_copy(vp[:, c * 130 + 65:c * 130 + 129], bk[:, 64:128])
                        sch.inc(i, "Dv")
                        if c in ev_tok:
                            assert ev_tok[c] == sch.tok("Dv"), (c, ev_tok[c])
                        ev_tok[c] = sch.tok("Dv")
                    v3 = vp[:].rearrange("p (cc j) -> p cc j", cc=NCH)
                    v.memset(v3[:, :, 64:65], 1.0)
                    v.memset(v3[:, :, 129:130], 1.0)
                    i = v.tensor_copy(dscr[:, 32:48], dscr[:, 0:16])
                    sch.inc(i, "Dv")
                ENG["VE"](_)
                R[wbufn[pair % 2]].add_r("Pe", sch.tok("Pe"))
                if pair % 2 == 1:
                    R["lgt"].add_r("Pe", sch.tok("Pe"))
                R["vp"].set_w("Dv", sch.tok("Dv"), 0)
                R["pb4"].set_w("Pe", sch.tok("Pe") - 1, 0)
                R["pb5"].set_w("Pe", sch.tok("Pe"), 0)
                R["pb4"].add_r("Dv", ev_tok[14])
                R["pb5"].add_r("Dv", ev_tok[15])
                if pair < NPAIR - 1:
                    wdma(d_wq, (pair + 1) % 2, pair=pair + 1)
                    wdma(d_wk, pair % 2, pair=pair + 1)

                def _(t):
                    sch.wait(t, "PE", "Dv", nmb1_t + 1)
                    return t.matmul(pb[6][:, 0:1], onesrow[0:1, :], nmb1[0:1, 0:1],
                                    start=True, stop=True)
                sec("PE", _, writes=("pb6",))

                def _(v):
                    return v.tensor_copy(negM[:], pb[6][:, 0:1])
                sec("VE", _, reads=("pb6",), writes=("negM",))
                negM_t = seal("VE")
                cut(f"mhat{pair}")

                for g in range(NG):
                    J = 4 * g + 4
                    d0 = 4 * g

                    def qw(j):
                        return max(0, (j - d0) * 128) if j >= d0 else 0

                    peops = []
                    for j in range(J):
                        peops.append(("s", j))
                        if j >= 3:
                            peops.append(("av", j - 3))
                    peops += [("av", J - 3), ("av", J - 2), ("av", J - 1)]
                    c_pe0 = sch.tok("Pe")
                    c_ac0 = sch.tok("Ac")
                    c_dv0 = sch.tok("Dv")
                    p_pe = {op: c_pe0 + k + 1 for k, op in enumerate(peops)}
                    p_ex = {j: c_ac0 + 2 * j + 1 for j in range(J)}
                    p_exd = {j: c_ac0 + 2 * j + 2 for j in range(J)}
                    p_mkd = {d0 + c: c_dv0 + 2 * (c + 1) for c in range(4)}

                    def _(t, g=g, J=J, d0=d0):
                        sch.wait(t, "PE", "Dv", state["attn_ep"])
                        sch.wait(t, "PE", "Dv", negM_t)
                        for kind, j in peops:
                            if kind == "s":
                                w = 512 - qw(j)
                                if j >= 2:
                                    sch.wait(t, "PE", "Ac", p_ex(j - 2) if False else p_ex[j - 2])
                                q0 = g * 512 + qw(j)
                                t.matmul(pb[j % 2][:, 0:w], kp[0:64, j * 128:(j + 1) * 128],
                                         qp[0:64, q0:(g + 1) * 512], start=True, stop=True)
                                i = t.matmul(pb[2 + j % 2][:, 0:w],
                                             kp[64:128, j * 128:(j + 1) * 128],
                                             qp[64:128, q0:(g + 1) * 512], start=True, stop=True)
                            else:
                                w = 512 - qw(j)
                                if j >= d0:
                                    sch.wait(t, "PE", "Dv", p_mkd[j])
                                else:
                                    sch.wait(t, "PE", "Ac", p_exd[j])
                                o0 = qw(j)
                                t.matmul(pb[4][0:65, o0:512], vp[:, j * 130:j * 130 + 65],
                                         Ebufs[j % 3][:, 0:w], start=(j == 0), stop=(j == J - 1))
                                i = t.matmul(pb[5][0:65, o0:512],
                                             vp[:, j * 130 + 65:j * 130 + 130],
                                             Ebufs[3 + j % 3][:, 0:w], start=(j == 0), stop=(j == J - 1))
                            sch.inc(i, "Pe")
                    ENG["PE"](_)
                    assert sch.tok("Pe") == c_pe0 + len(peops)

                    def _(sc, g=g, J=J):
                        sch.wait(sc, "SE", "Dv", negM_t)
                        for j in range(J):
                            w = 512 - qw(j)
                            sch.wait(sc, "SE", "Pe", p_pe[("s", j)])
                            if j >= 3:
                                sch.wait(sc, "SE", "Pe", p_pe[("av", j - 3)])
                            elif state["prev_av"]:
                                sch.wait(sc, "SE", "Pe", state["prev_av"])
                            sc.activation(Ebufs[j % 3][:, 0:w], pb[j % 2][:, 0:w], AF.Exp,
                                          bias=negM[:, 0:1], scale=0.125)
                            i = sc.activation(Ebufs[3 + j % 3][:, 0:w], pb[2 + j % 2][:, 0:w],
                                              AF.Exp, bias=negM[:, 0:1], scale=0.125)
                            sch.inc(i, "Ac")
                            i = sc.copy(dscr[:, 16:32], dscr[:, 0:16])
                            sch.inc(i, "Ac")
                    ENG["SE"](_)
                    assert sch.tok("Ac") == c_ac0 + 2 * J

                    def _(v, g=g, J=J, d0=d0, pair=pair):
                        for j in range(d0, J):
                            c = j - d0
                            w = 512 - 128 * c
                            sch.wait(v, "VE", "Ac", p_exd[j])
                            v.tensor_tensor(Ebufs[j % 3][:, 0:w], Ebufs[j % 3][:, 0:w],
                                            maskt[:, 0:w], OP.mult)
                            i = v.tensor_tensor(Ebufs[3 + j % 3][:, 0:w],
                                                Ebufs[3 + j % 3][:, 0:w],
                                                maskt[:, 0:w], OP.mult)
                            sch.inc(i, "Dv")
                            i = v.tensor_copy(dscr[:, 32:48], dscr[:, 0:16])
                            sch.inc(i, "Dv")
                        sch.wait(v, "VE", "Pe", p_pe[("av", J - 1)])
                        col = pair * 2048 + g * 512
                        zc = g * 1024
                        v.tensor_copy(outT[0:64, col:col + 512], pb[4][0:64, 0:512])
                        v.tensor_copy(outT[64:128, col:col + 512], pb[5][0:64, 0:512])
                        v.tensor_copy(zst[0:1, zc:zc + 512], pb[4][64:65, 0:512])
                        i = v.tensor_copy(zst[0:1, zc + 512:zc + 1024], pb[5][64:65, 0:512])
                        sch.inc(i, "Dv")
                        i = v.tensor_copy(dscr[:, 32:48], dscr[:, 0:16])
                        sch.inc(i, "Dv")
                    ENG["VE"](_)
                    assert sch.tok("Dv") == c_dv0 + 2 * 4 + 2
                    state["attn_ep"] = sch.tok("Dv")
                    state["prev_av"] = p_pe[("av", J - 1)]
                    cut(f"core{pair}_{g}")
                R["outT"].set_w("Dv", sch.tok("Dv"), 0)
                R["zst"].set_w("Dv", sch.tok("Dv"), 0)
                R["pb4"].set_w("Pe", state["prev_av"], 0)
                R["pb5"].set_w("Pe", state["prev_av"], 0)
                R["pb4"].add_r("Dv", state["attn_ep"])
                R["pb5"].add_r("Dv", state["attn_ep"])
                for jj in range(4):
                    R[f"pb{jj}"].set_w("Pe", state["prev_av"], 0)
                    R[f"pb{jj}"].add_r("Dv", state["attn_ep"])
                cut(f"preZ{pair}")
                R["negM"].add_r("Ac", sch.tok("Ac"))
                cut(f"preZ{pair}")

                def _(g_, pair=pair):
                    rr = R["zst"]
                    sch.wait(g_, "GE", rr.w[0], rr.w[1] + rr.w[2])
                    for s_, t_ in R["zbuf"].r.items():
                        sch.wait(g_, "GE", s_, t_)
                    sch.wait(g_, "GE", "Cz2", sch.cnt["Cz2"])
                    i = g_.dma_start(out=d_zscr.rearrange("(o w) -> o w", o=1), in_=zst[0:1, :])
                    sch.inc(i, "Cz1", 16)
                    g_.wait_ge(sch.sem["Cz1"], sch.cnt["Cz1"])
                    i = g_.dma_start(
                        out=zbuf[2 * pair:2 * pair + 2, :].rearrange("h (g w) -> h g w", g=4),
                        in_=d_zscr.rearrange("(g h w) -> h g w", g=4, h=2))
                    sch.inc(i, "Cz2", 16)
                    R["zst"].add_r("Cz1", sch.cnt["Cz1"])
                ENG["GE"](_)
                R["zbuf"].set_w("Cz2", sch.tok("Cz2"))
                cut(f"pair{pair}")

            def _(v):
                return v.reciprocal(crec8[:], zbuf[0:8, :])
            sec("VE", _, reads=("zbuf",), writes=("crec",))
            seal("VE")

            def _(v):
                return v.tensor_tensor(cgb[:], crec8[:], gTb[:], OP.mult)
            sec("VE", _, reads=("crec", "gT"), writes=("cgb",))
            cgb_t2 = seal("VE")

            state["scale_done"] = cgb_t2
            state["pred_done"] = cgb_t2
            for g in range(NG):
                s_pe0 = sch.tok("Pe")
                s_dv0 = sch.tok("Dv")

                def _(t, g=g, s_pe0=s_pe0, s_dv0=s_dv0):
                    sch.wait(t, "PE", "Dv", cgb_t2)
                    sch.wait(t, "PE", "Dv", state["attn_ep"])
                    for pair in range(NPAIR):
                        if pair >= 2:
                            sch.wait(t, "PE", "Dv", s_dv0 + pair - 1)
                        else:
                            sch.wait(t, "PE", "Dv", state["scale_done"])
                        i = t.matmul(pb[2 + pair % 2][:, 0:512],
                                     psel[:, pair * 128:(pair + 1) * 128],
                                     cgb[0:8, g * 512:(g + 1) * 512], start=True, stop=True)
                        sch.inc(i, "Pe")
                    for k in range(8):
                        ns, dh = divmod(k, 2)
                        sch.wait(t, "PE", "Dv", s_dv0 + 5)
                        if k >= 2:
                            sch.wait(t, "PE", "Dv", s_dv0 + 5 + k - 1)
                        else:
                            sch.wait(t, "PE", "Dv", state["pred_done"])
                        for pair in range(NPAIR):
                            i = t.matmul(pb[k % 2][:, 0:512],
                                         outT[:, pair * 2048 + g * 512 + ns * 128:
                                              pair * 2048 + g * 512 + (ns + 1) * 128],
                                         woall[:, pair * 1024 + dh * 512:
                                               pair * 1024 + (dh + 1) * 512],
                                         start=(pair == 0), stop=(pair == NPAIR - 1))
                        sch.inc(i, "Pe")
                ENG["PE"](_)

                def _(v, g=g, s_pe0=s_pe0):
                    for pair in range(NPAIR):
                        sch.wait(v, "VE", "Pe", s_pe0 + pair + 1)
                        col = pair * 2048 + g * 512
                        i = v.tensor_tensor(outT[:, col:col + 512], outT[:, col:col + 512],
                                            pb[2 + pair % 2][:, 0:512], OP.mult)
                        sch.inc(i, "Dv")
                    i = v.tensor_copy(dscr[:, 32:48], dscr[:, 0:16])
                    sch.inc(i, "Dv")
                    for k in range(8):
                        ns, dh = divmod(k, 2)
                        c = g * 4 + ns
                        sch.wait(v, "VE", "Pe", s_pe0 + 4 + k + 1)
                        xsl = x_sb[:, c * 1024 + dh * 512:c * 1024 + (dh + 1) * 512]
                        i = v.tensor_tensor(xsl, xsl, pb[k % 2][:, 0:512], OP.add)
                        sch.inc(i, "Dv")
                ENG["VE"](_)
                state["scale_done"] = s_dv0 + 5
                state["pred_done"] = s_dv0 + 13

                def _(sc, g=g, s_dv0=s_dv0):
                    i = None
                    for ns in range(4):
                        c = g * 4 + ns
                        sch.wait(sc, "SE", "Dv", s_dv0 + 7 + 2 * ns)
                        xc = x_sb[:, c * 1024:(c + 1) * 1024]
                        sc.activation(sqscr[:], xc[:, 0:512], AF.Square,
                                      accum_out=sqs[:, c:c + 1])
                        i = sc.activation(sqscr[:], xc[:, 512:1024], AF.Square,
                                          accum_out=msq[:, c:c + 1])
                    sch.inc(i, "Ac")
                ENG["SE"](_)
                R["x"].add_r("Ac", sch.tok("Ac"))

                def _(v, g=g):
                    i = None
                    for ns in range(4):
                        c = g * 4 + ns
                        i = v.reduce_sum(sums[:, c:c + 1],
                                         x_sb[:, c * 1024:(c + 1) * 1024], axis=AX.X)
                    sch.inc(i, "Dv")
                ENG["VE"](_)
                R["x"].add_r("Dv", sch.tok("Dv"))
            def _(v):
                sch.drain_inc(v, "Dv")
            ENG["VE"](_)
            R["x"].set_w("Dv", sch.tok("Dv"), 0)
            state["pred_dv"] = sch.tok("Dv")
            cut("pred")
            emit_ln(is_ln2=True, precomputed=True)
            emit_apply_T(nrm_sb, rewrite=True)
            hT_t = sch.tok("Dv")
            cut("ln2")

            for g in range(NG):
                def _(t, g=g):
                    i = None
                    for dc in range(DCH):
                        i = t.matmul(pb[6][0:8, 0:512], gw_sb[:, dc * 8:(dc + 1) * 8],
                                     tT[:, dc * 2048 + g * 512:dc * 2048 + (g + 1) * 512],
                                     start=(dc == 0), stop=(dc == DCH - 1))
                    return i
                sec("PE", _, reads=("tT",), writes=("pb6",))

                def _(v, g=g):
                    return v.tensor_scalar(lgt[:], pb[6][0:8, 0:512], gb_sb[0:8, 0:1],
                                           None, OP.add)
                sec("VE", _, reads=("pb6",), writes=("lgt",))
                seal("VE")

                def _(t, g=g):
                    i = None
                    for k in range(4):
                        i = t.transpose(pb[5][:, k * 8:(k + 1) * 8],
                                        lgt[0:8, k * 128:(k + 1) * 128], identf[0:8, 0:8])
                    return i
                sec("PE", _, reads=("lgt",), writes=("pb5",))

                def _(v, g=g):
                    return v.tensor_copy(lgtm[:, g * 32:g * 32 + 32], pb[5][:, 0:32])
                sec("VE", _, reads=("pb5",), writes=("lgtm",))
            seal("VE")

            def _(v):
                L3 = lgtm[:].rearrange("p (s e) -> p s e", e=8)
                v.tensor_reduce(rm1[:], L3, axis=AX.X, op=OP.max)
                sch.drain_inc(v, "Dv")
                v.tensor_tensor(req1[:].rearrange("p (s e) -> p s e", e=8), L3,
                                rm1[:].rearrange("p s -> p s ()").broadcast_to([128, 16, 8]),
                                OP.is_equal)
                v.drain()
                v.scalar_tensor_tensor(rl2[:], req1[:], -1e30, lgtm[:], OP.mult, OP.add)
                sch.drain_inc(v, "Dv")
                v.tensor_reduce(rm2[:], rl2[:].rearrange("p (s e) -> p s e", e=8),
                                axis=AX.X, op=OP.max)
                sch.drain_inc(v, "Dv")
                v.tensor_tensor(req2[:].rearrange("p (s e) -> p s e", e=8),
                                rl2[:].rearrange("p (s e) -> p s e", e=8),
                                rm2[:].rearrange("p s -> p s ()").broadcast_to([128, 16, 8]),
                                OP.is_equal)
                v.drain()
                return v.tensor_tensor(rd12[:], rm1[:], rm2[:], OP.subtract)
            sec("VE", _, reads=("lgtm",), writes=("rtr",))
            seal("VE")

            def _(sc):
                return sc.activation(rw1[:], rd12[:], AF.Sigmoid)
            sec("SE", _, reads=("rtr",), writes=("rtr",))
            seal("SE")

            def _(v):
                v.tensor_scalar(rw2[:], rw1[:], -1.0, 1.0, OP.mult, OP.add)
                v.drain()
                v.tensor_tensor(rwa[:].rearrange("p (s e) -> p s e", e=8),
                                req1[:].rearrange("p (s e) -> p s e", e=8),
                                rw1[:].rearrange("p s -> p s ()").broadcast_to([128, 16, 8]),
                                OP.mult)
                v.tensor_tensor(rwf[:].rearrange("p (s e) -> p s e", e=8),
                                req2[:].rearrange("p (s e) -> p s e", e=8),
                                rw2[:].rearrange("p s -> p s ()").broadcast_to([128, 16, 8]),
                                OP.mult)
                v.drain()
                return v.tensor_tensor(rwf[:], rwf[:], rwa[:], OP.add)
            sec("VE", _, reads=("rtr",), writes=("rwf",))
            seal("VE")

            for g in range(NG):
                def _(t, g=g):
                    i = None
                    for k in range(4):
                        sub = g * 4 + k
                        i = t.transpose(pb[5][0:8, k * 128:(k + 1) * 128],
                                        rwf[:, sub * 8:(sub + 1) * 8], identf[:])
                    return i
                sec("PE", _, reads=("rwf",), writes=("pb5",))

                def _(v, g=g):
                    return v.tensor_scalar(wTb[:, g * 512:(g + 1) * 512], pb[5][0:8, 0:512],
                                           1.0 / GELU_A, None, OP.mult)
                sec("VE", _, reads=("pb5",), writes=("wTb",))
            seal("VE")
            cut("router")

            wtb_t2 = sch.tok("Dv")
            NT = NE * NG
            state["t1war"] = [("Dv", wtb_t2)] * 4
            state["m2war"] = [wtb_t2, wtb_t2]
            state["pb6war"] = ("Dv", wtb_t2)
            state["wbrR"] = [0, 0]
            state["actR"] = [0, 0]
            prev = {}
            w1tok, w2tok = {}, {}
            cx_toks = {}

            def _(s):
                sch.wait(s, "GE", "Dv", state["pred_dv"])
                sch.wait(s, "GE", "Pe", state["w1_last"])
                i = s.dma_start(out=w1b[:, 0:8192], in_=d_w1[0][:, 0:8192])
                sch.inc(i, "Cw1", 16)
                i = s.dma_start(out=w1b[:, 8192:16384], in_=d_w1[0][:, 8192:16384])
                sch.inc(i, "Cw1", 16)
                sch.wait(s, "GE", "Pe", state["w2_last"])
                i = s.dma_start(out=w2b[:], in_=d_w2[0])
                sch.inc(i, "Cw2", 16)
            ENG["GE"](_)
            w1tok[0] = (sch.tok("Cw1") - 16, sch.tok("Cw1"))
            w2tok[0] = sch.tok("Cw2")

            for n in range(NT + 1):
                cur = n if n < NT else None
                prv = n - 1 if n >= 1 else None
                par = n % 2
                if cur is not None:
                    e_c, t_c = divmod(cur, NG)

                f_pe0 = sch.tok("Pe")
                f_ac0 = sch.tok("Ac")
                f_dv0 = sch.tok("Dv")
                f_pl0 = sch.tok("Pl")
                xacc_base = f_dv0 + (32 if cur is not None else 0)

                def p_eg(c, b=f_ac0):
                    return b + (2 if c == 0 else 3 * c + 1)

                def p_el(c, b=f_ac0):
                    return b + (3 if c == 0 else 3 * c + 2)

                def p_si(c, b=f_ac0):
                    return b + (3 * c + 6 if c <= 6 else 26)

                def p_lcb(c, b=f_dv0):
                    return b + (2 if c == 0 else 4 * c)

                def p_a0(c, b=f_dv0):
                    return b + (4 * (c + 1) + 1 if c <= 6 else 31)

                def p_a0d(c, b=f_dv0):
                    return b + (4 * (c + 1) + 2 if c <= 6 else 32)

                m2_new = []

                def _(t, cur=cur, prv=prv, xacc_base=xacc_base, f_pe0=f_pe0,
                      p_eg=p_eg, p_el=p_el, f_pl0=f_pl0, m2_new=m2_new):
                    if cur is not None:
                        e, t_i = divmod(cur, NG)
                        sch.wait(t, "PE", "Dv", wtb_t2)
                        sch.wait(t, "PE", state["pb6war"][0], state["pb6war"][1])
                        i = t.matmul(pb[6][:, 0:512], oneh[:, e * 128:(e + 1) * 128],
                                     wTb[0:8, t_i * 512:(t_i + 1) * 512],
                                     start=True, stop=True)
                        sch.inc(i, "Pe")
                        sch.wait(t, "PE", "Cw1", w1tok[e][0])
                        rrt = R["tT"].w
                        sch.wait(t, "PE", rrt[0], rrt[1] + rrt[2])
                        for c in range(ICH):
                            if c == 4:
                                sch.wait(t, "PE", "Cw1", w1tok[e][1])
                            for which in (0, 1):
                                fc = 2 * c + which
                                bi = (c % 2) if which == 0 else 2 + c % 2
                                bk = pb[bi]
                                if c >= 2:
                                    sch.wait(t, "PE", "Ac",
                                             p_eg(c - 2) if which == 0 else p_el(c - 2))
                                else:
                                    s_, t_ = state["t1war"][bi]
                                    sch.wait(t, "PE", s_, t_)
                                for dc in range(DCH):
                                    i = t.matmul(
                                        bk[:, 0:512],
                                        w1b[:, fc * 1024 + dc * 128:fc * 1024 + (dc + 1) * 128],
                                        tT[:, dc * 2048 + t_i * 512:dc * 2048 + (t_i + 1) * 512],
                                        start=(dc == 0), stop=(dc == DCH - 1))
                                sch.inc(i, "Pe")
                    if prv is not None:
                        ep, tp = divmod(prv, NG)
                        sch.wait(t, "PE", "Cw2", w2tok[ep])
                        sch.wait(t, "PE", "Pl", f_pl0)
                        for k in range(8):
                            ns, dh = divmod(k, 2)
                            c2 = tp * 4 + ns
                            bk = pb[4 + k % 2]
                            if k >= 2:
                                sch.wait(t, "PE", "Dv", xacc_base + (k - 2) + 1)
                            else:
                                sch.wait(t, "PE", "Dv", state["m2war"][k % 2])
                            first = True
                            if ep == 0:
                                t.matmul(bk[:, 0:512], wTb[0:8, c2 * 128:(c2 + 1) * 128],
                                         b2_sb[0:8, dh * 512:(dh + 1) * 512],
                                         start=True, stop=False)
                                first = False
                            for ic in range(ICH):
                                i = t.matmul(
                                    bk[:, 0:512],
                                    actb[prv % 2][:, ic * 512 + ns * 128:ic * 512 + (ns + 1) * 128],
                                    w2b[:, ic * 1024 + dh * 512:ic * 1024 + (dh + 1) * 512],
                                    start=(first and ic == 0), stop=(ic == ICH - 1))
                                first = False
                            sch.inc(i, "Pe")
                            m2_new.append(sch.tok("Pe"))
                ENG["PE"](_)
                exp_pe = (1 + 16 if cur is not None else 0) + (8 if prv is not None else 0)
                assert sch.tok("Pe") == f_pe0 + exp_pe

                if cur is not None:
                    def _(sc, cur=cur, par=par, f_pe0=f_pe0, f_ac0=f_ac0,
                          p_lcb=p_lcb, p_a0=p_a0, prev=prev):
                        e, t_i = divmod(cur, NG)
                        sch.wait(sc, "SE", "Pe", f_pe0 + 1)
                        sch.wait(sc, "SE", "Pl", state["wbrR"][par])
                        i = sc.copy(wbr[par][:], pb[6][:, 0:512])
                        sch.inc(i, "Ac")

                        def eg(c):
                            bgc = b1_sb[:, e * 16 + 2 * c:e * 16 + 2 * c + 1]
                            sch.wait(sc, "SE", "Pe", f_pe0 + 2 + 2 * c)
                            if c >= 2:
                                sch.wait(sc, "SE", "Dv", p_lcb(c - 2))
                            elif prev:
                                sch.wait(sc, "SE", "Dv", prev["lcb"][c])
                            i = sc.activation(g1p[c % 2][:], pb[c % 2][:, 0:512],
                                              AF.Identity, bias=bgc)
                            sch.inc(i, "Ac")

                        def el(c):
                            blc = b1_sb[:, e * 16 + 2 * c + 1:e * 16 + 2 * c + 2]
                            sch.wait(sc, "SE", "Pe", f_pe0 + 3 + 2 * c)
                            if c >= 2:
                                sch.wait(sc, "SE", "Dv", p_lcb(c - 2))
                            elif prev:
                                sch.wait(sc, "SE", "Dv", prev["lcb"][c])
                            i = sc.activation(l1p[c % 2][:], pb[2 + c % 2][:, 0:512],
                                              AF.Identity, bias=blc)
                            sch.inc(i, "Ac")

                        def si(c):
                            sch.wait(sc, "SE", "Dv", p_lcb(c))
                            if c >= 1:
                                sch.wait(sc, "SE", "Dv", p_a0(c - 1))
                            elif prev:
                                sch.wait(sc, "SE", "Dv", prev["a07"])
                            i = sc.activation(p1[:], g1p[c % 2][:], AF.Silu, scale=GELU_A)
                            sch.inc(i, "Ac")

                        eg(0); el(0)
                        for c in range(1, ICH):
                            eg(c); el(c); si(c - 1)
                        i = sc.copy(dscr[:, 16:32], dscr[:, 0:16])
                        sch.inc(i, "Ac")
                        si(ICH - 1)
                        i = sc.copy(dscr[:, 16:32], dscr[:, 0:16])
                        sch.inc(i, "Ac")
                    ENG["SE"](_)
                    assert sch.tok("Ac") == f_ac0 + 27

                def _(v, cur=cur, prv=prv, f_pl0=f_pl0, m2_new=m2_new,
                      p_eg=p_eg, p_el=p_el, p_si=p_si):
                    if cur is not None:
                        def minglcb(c):
                            sch.wait(v, "VE", "Ac", p_eg(c) + 1)
                            i = v.tensor_scalar(g1p[c % 2][:], g1p[c % 2][:],
                                                LIMIT, None, OP.min)
                            sch.inc(i, "Dv")
                            sch.wait(v, "VE", "Ac", p_el(c) + 1)
                            i = v.tensor_scalar(lcb2[c % 2][:], l1p[c % 2][:],
                                                LIMIT, -LIMIT, OP.min, OP.max)
                            sch.inc(i, "Dv")

                        def a0op(c):
                            sch.wait(v, "VE", "Ac", p_si(c) + 1)
                            sch.wait(v, "VE", "Pl", f_pl0 + c)
                            i = v.scalar_tensor_tensor(a0[:], lcb2[c % 2][:], 1.0,
                                                       p1[:], OP.add, OP.mult)
                            sch.inc(i, "Dv")
                            i = v.tensor_copy(dscr[:, 32:48], dscr[:, 0:16])
                            sch.inc(i, "Dv")

                        minglcb(0)
                        for c in range(1, ICH):
                            minglcb(c)
                            a0op(c - 1)
                        a0op(ICH - 1)
                    if prv is not None:
                        tp = prv % NG
                        for k in range(8):
                            ns, dh = divmod(k, 2)
                            c2 = tp * 4 + ns
                            sch.wait(v, "VE", "Pe", m2_new[k])
                            xsl = x_sb[:, c2 * 1024 + dh * 512:c2 * 1024 + (dh + 1) * 512]
                            i = v.tensor_tensor(xsl, xsl, pb[4 + k % 2][:, 0:512], OP.add)
                            sch.inc(i, "Dv")
                ENG["VE"](_)
                exp_dv = (32 if cur is not None else 0) + (8 if prv is not None else 0)
                assert sch.tok("Dv") == f_dv0 + exp_dv

                if prv is not None and prv >= NT - NG:
                    tq = prv - (NT - NG)

                    def _(v):
                        sch.drain_inc(v, "Dv")
                    ENG["VE"](_)
                    seal_t = sch.tok("Dv")

                    def _(s, tq=tq, s_i=s_i, seal_t=seal_t):
                        sch.wait(s, "SY", "Dv", seal_t)
                        i = s.dma_start(
                            out=xout[s_i][4 * tq * 128:(4 * tq + 4) * 128]
                                .rearrange("(c p) d -> p c d", p=128),
                            in_=x_sb[:, 4 * tq * 1024:(4 * tq + 4) * 1024]
                                .rearrange("p (c d) -> p c d", c=4))
                        sch.inc(i, "So", 16)
                        if s_i < nseq - 1:
                            sch.wait(s, "SY", "So", sch.cnt["So"])
                            i = s.dma_start(
                                out=x_sb[:, 4 * tq * 1024:(4 * tq + 4) * 1024]
                                    .rearrange("p (c d) -> p c d", c=4),
                                in_=xin[s_i + 1][4 * tq * 128:(4 * tq + 4) * 128]
                                    .rearrange("(c p) d -> p c d", p=128))
                            sch.inc(i, "Cx", 16)
                    ENG["SY"](_)
                    if s_i < nseq - 1:
                        cx_toks[tq] = sch.tok("Cx")
                        R["x"].set_w("Cx", cx_toks[tq])

                if prv == NT - 1 and s_i < nseq - 1:
                    def _(sc, cx_toks=cx_toks):
                        i = None
                        for tq in range(NG):
                            sch.wait(sc, "SE", "Cx", cx_toks[tq])
                            for ns in range(4):
                                c = tq * 4 + ns
                                xc = x_sb[:, c * 1024:(c + 1) * 1024]
                                sc.activation(sqscr[:], xc[:, 0:512], AF.Square,
                                              accum_out=sqs[:, c:c + 1])
                                i = sc.activation(sqscr[:], xc[:, 512:1024], AF.Square,
                                                  accum_out=msq[:, c:c + 1])
                        sch.inc(i, "Ac")
                    ENG["SE"](_)
                    R["x"].add_r("Ac", sch.tok("Ac"))

                    def _(v, cx_toks=cx_toks):
                        i = None
                        for tq in range(NG):
                            sch.wait(v, "VE", "Cx", cx_toks[tq])
                            for ns in range(4):
                                c = tq * 4 + ns
                                i = v.reduce_sum(sums[:, c:c + 1],
                                                 x_sb[:, c * 1024:(c + 1) * 1024],
                                                 axis=AX.X)
                        sch.inc(i, "Dv")
                    ENG["VE"](_)
                    R["x"].add_r("Dv", sch.tok("Dv"))

                if cur is not None:
                    def _(p, cur=cur, par=par, f_ac0=f_ac0, p_a0d=p_a0d):
                        sch.wait(p, "GE", "Ac", f_ac0 + 2)
                        sch.wait(p, "GE", "Pe", state["actR"][par])
                        for c in range(ICH):
                            sch.wait(p, "GE", "Dv", p_a0d(c))
                            i = p.tensor_tensor(actb[par][:, c * 512:(c + 1) * 512],
                                                a0[:], wbr[par][:], OP.mult)
                            sch.inc(i, "Pl")
                        i = p.tensor_copy(dscr[:, 48:64], dscr[:, 0:16])
                        sch.inc(i, "Pl")
                    ENG["GE"](_)
                    assert sch.tok("Pl") == f_pl0 + 9

                if cur is not None and t_c == NG - 1 and e_c < NE - 1:
                    def _(s, e_c=e_c, f_pe0=f_pe0):
                        sch.wait(s, "GE", "Pe", f_pe0 + 9)
                        i = s.dma_start(out=w1b[:, 0:8192], in_=d_w1[e_c + 1][:, 0:8192])
                        sch.inc(i, "Cw1", 16)
                        sch.wait(s, "GE", "Pe", f_pe0 + 17)
                        i = s.dma_start(out=w1b[:, 8192:16384],
                                        in_=d_w1[e_c + 1][:, 8192:16384])
                        sch.inc(i, "Cw1", 16)
                    ENG["GE"](_)
                    w1tok[e_c + 1] = (sch.tok("Cw1") - 16, sch.tok("Cw1"))
                if cur is not None and t_c == 0 and e_c >= 1:
                    def _(s, e_c=e_c, f_pe0=f_pe0):
                        sch.wait(s, "GE", "Pe", f_pe0 + 25)
                        i = s.dma_start(out=w2b[:], in_=d_w2[e_c])
                        sch.inc(i, "Cw2", 16)
                    ENG["GE"](_)
                    w2tok[e_c] = sch.tok("Cw2")
                if cur is not None:
                    state["t1war"] = [("Ac", p_eg(6)), ("Ac", p_eg(7)),
                                      ("Ac", p_el(6)), ("Ac", p_el(7))]
                    state["pb6war"] = ("Ac", f_ac0 + 1)
                    state["wbrR"][par] = f_pl0 + 9
                    prev = {"lcb": (p_lcb(6), p_lcb(7)), "a07": p_a0(7)}
                    if cur == NT - 1:
                        state["w1_last"] = f_pe0 + 17
                if prv is not None:
                    state["m2war"] = [xacc_base + 7, xacc_base + 8]
                    state["actR"][prv % 2] = m2_new[-1]
                    if prv == NT - 1:
                        state["w2_last"] = m2_new[-1]
                cut(f"stage_{n}")
            def _(v):
                sch.drain_inc(v, "Dv")
            ENG["VE"](_)
            ffn_t = sch.tok("Dv")
            cut("full")
          except _CutDone:
            pass
          if True:
            if DBG_STOP:
                def _(v):
                    i = None
                    for ix, tt in enumerate((sums, sqs, msq, nm, nv, vpe, lnv, rstd,
                                             rsq, vv, mt2, lnw, alphav, cc1, cc2)):
                        i = v.tensor_copy(q2[:, 0:2048].bitcast(F32)[:, ix * 16:(ix + 1) * 16], tt[:])
                    q2f = q2[:, 0:2048].bitcast(F32)
                    v.tensor_copy(q2f[:, 256:384], lgtm[:])
                    v.tensor_copy(q2f[:, 384:512], rwf[:])
                    v.tensor_copy(q2f[:, 512:528], rm1[:])
                    v.tensor_copy(q2f[:, 528:544], rm2[:])
                    v.tensor_copy(q2f[:, 544:560], rw1[:])
                    i = v.tensor_copy(q2f[:, 560:688], req1[:])
                    return i
                sec("VE", _, reads=("stats",), writes=("q2",))
            ffn_t = seal("VE")
            se_t = seal("SE")
            pl_t = seal("GE")

            def _(s):
                sch.wait(s, "SY", "Dv", ffn_t)
                sch.wait(s, "SY", "Ac", se_t)
                sch.wait(s, "SY", "Pl", pl_t)
                sch.wait(s, "SY", "Pe", sch.tok("Pe"))
                for cn in ("Cq", "Cz1", "Cz2", "Cw1", "Cw2"):
                    sch.wait(s, "SY", cn, sch.tok(cn))
                i = s.dma_start(out=d_dbg[:, :],
                                in_=arena[:, 0:57600].bitcast(F32)[:, 0:28800])
                sch.inc(i, "So", 16)
            if DBG_STOP and DBG_DUMP:
                ENG["SY"](_)

          if True:
            state["store"] = sch.tok("So")

        def _(g_):
            g_.wait_ge(sch.sem["So"], sch.cnt["So"])
        ENG["GE"](_)

    return nc


_NC_CACHE = None


def kernel(**inputs):
    global _NC_CACHE
    maps = _host_prep(inputs)
    if _NC_CACHE is None:
        _NC_CACHE = build()
    res = run_bass_kernel_spmd(_NC_CACHE, maps, core_ids=list(range(CORES)))
    out = np.empty((B, S, D), np.float32)
    for c in range(CORES):
        out[c * SPC:(c + 1) * SPC] = res.results[c]["xout"]
    return out


if __name__ == "__main__":
    import time
    t0 = time.time()
    nc = build()
    n = sum(1 for _ in nc.all_instructions())
    print("build s:", round(time.time() - t0, 1), "instrs:", n)


# version 13
# speedup vs baseline: 1.1086x; 1.0125x over previous
import contextlib
import numpy as np
import ml_dtypes
import concourse.bass as bass
import concourse.mybir as mybir
from concourse.bass_utils import run_bass_kernel_spmd
from concourse.alu_op_type import AluOpType as OP

F32 = mybir.dt.float32
BF16 = mybir.dt.bfloat16
AF = mybir.ActivationFunctionType
AX = mybir.AxisListType

B, S, D, H, DH, NE = 32, 2048, 1024, 8, 64, 8
CORES = 8
SPC = B // CORES
NCH, NG, NPAIR, FCH, ICH, DCH = 16, 4, 4, 16, 8, 8
EPS_LN = 1e-5
EPS_RMS = float(np.finfo(np.float32).eps)
GELU_A = 1.702
LIMIT = 7.0
bf16 = ml_dtypes.bfloat16

DBG_STOP = None
DBG_DUMP = True
DBG_NSEQ = None


class _CutDone(Exception):
    pass


ARENA_OFF = {}


def _host_prep(inp):
    f = np.float32
    wq = np.asarray(inp["L0_attn_wq"], f)
    wk = np.asarray(inp["L0_attn_wk"], f)
    wv = np.asarray(inp["L0_attn_wv"], f)
    wo = np.asarray(inp["L0_attn_wo"], f)
    wg = np.asarray(inp["L0_attn_w_gates"], f)
    rmsw = np.asarray(inp["L0_attn_rms_w"], f)
    nrmw = np.asarray(inp["L0_ffn_norm_w"], f)
    gw = np.asarray(inp["L0_ffn_gate_w"], f)
    gb = np.asarray(inp["L0_ffn_gate_b"], f)
    w1 = np.asarray(inp["L0_ffn_mlp1_w"], f)
    b1 = np.asarray(inp["L0_ffn_mlp1_b"], f)
    w2 = np.asarray(inp["L0_ffn_mlp2_w"], f)
    b2 = np.asarray(inp["L0_ffn_mlp2_b"], f)
    tok = np.asarray(inp["L0_attn_tokens"], f)

    def pairstack(w):
        a = np.stack([np.concatenate([w[2 * p], w[2 * p + 1]], axis=1) for p in range(NPAIR)])
        a = a.reshape(NPAIR, DCH, 128, 128).transpose(0, 2, 1, 3)
        return np.ascontiguousarray(a.reshape(NPAIR, 128, DCH * 128))

    wq_h, wk_h, wv_h = pairstack(wq), pairstack(wk), pairstack(wv)
    wo_h = np.ascontiguousarray(
        np.stack([wo[2 * p:2 * p + 2].reshape(128, 1024) for p in range(NPAIR)]))
    wg_h = np.ascontiguousarray(wg.reshape(DCH, 128, H).transpose(1, 0, 2).reshape(128, DCH * H))
    rms_h = np.ascontiguousarray(rmsw.reshape(DCH, 128).T)
    nrm_h = np.ascontiguousarray(nrmw.reshape(DCH, 128).T)
    gw_h = np.ascontiguousarray(gw.T.reshape(DCH, 128, NE).transpose(1, 0, 2).reshape(128, DCH * NE))
    gb_h = np.ascontiguousarray(gb.reshape(NE, 1))

    perm = np.empty(2048, np.int64)
    for fc in range(FCH):
        perm[fc * 128:(fc + 1) * 128] = 256 * (fc // 2) + 2 * np.arange(128) + (fc % 2)
    w1p = w1[:, perm, :]
    a = w1p.transpose(0, 2, 1).reshape(NE, DCH, 128, FCH, 128).transpose(0, 2, 3, 1, 4)
    w1_h = np.ascontiguousarray(a.reshape(NE, 128, FCH * DCH * 128))
    b1_h = np.ascontiguousarray(
        b1[:, perm].reshape(NE, FCH, 128).transpose(2, 0, 1).reshape(128, NE * FCH))
    a = w2.transpose(0, 2, 1).reshape(NE, ICH, 128, 1024).transpose(0, 2, 1, 3)
    w2_h = np.ascontiguousarray(a.reshape(NE, 128, ICH * 1024))
    b2_h = np.ascontiguousarray(b2)

    identb = np.eye(128, dtype=bf16)
    identf = np.eye(128, dtype=np.float32)
    p_ = np.arange(128)[:, None]
    q_ = np.arange(512)[None, :]
    maskt = np.concatenate([(128 * c + p_ <= q_) for c in range(4)], axis=1).astype(bf16)
    psel = np.zeros((8, 512), bf16)
    for p in range(NPAIR):
        psel[2 * p, p * 128:p * 128 + 64] = 1
        psel[2 * p + 1, p * 128 + 64:p * 128 + 128] = 1
    oneh = np.zeros((8, 1024), bf16)
    for e in range(NE):
        oneh[e, e * 128:(e + 1) * 128] = 1
    onescol = np.ones((128, 1), bf16)
    onesrow = np.ones((1, 128), bf16)

    shared = dict(wq=wq_h, wk=wk_h, wv=wv_h, wo=wo_h, wg=wg_h, rmsw=rms_h,
                  nrmw=nrm_h, gw=gw_h, gb=gb_h, w1=w1_h, b1=b1_h, w2=w2_h, b2=b2_h,
                  identb=identb, identf=identf, maskt=maskt, psel=psel, oneh=oneh,
                  onescol=onescol, onesrow=onesrow)
    maps = []
    for c in range(CORES):
        m = dict(shared)
        m["xin"] = np.ascontiguousarray(tok[c * SPC:(c + 1) * SPC])
        maps.append(m)
    return maps


class Sch:
    def __init__(self):
        self.sem, self.cnt, self.lastw = {}, {}, {}

    def add(self, n, h):
        self.sem[n] = h
        self.cnt[n] = 0

    def inc(self, instr, s, n=1):
        instr.then_inc(self.sem[s], n)
        self.cnt[s] += n
        return self.cnt[s]

    def tok(self, s):
        return self.cnt[s]

    def wait(self, eng, en, s, val):
        if not val or val <= 0:
            return
        k = (en, s)
        if self.lastw.get(k, 0) >= val:
            return
        eng.wait_ge(self.sem[s], val)
        self.lastw[k] = val

    def drain_inc(self, eng, s):
        eng.drain()
        i = eng.nop()
        i.then_inc(self.sem[s], 1)
        self.cnt[s] += 1
        return self.cnt[s]


class Res:
    def __init__(self):
        self.w = None
        self.r = {}

    def set_w(self, sem, tok, slack=0):
        self.w = (sem, tok, slack)
        self.r = {}

    def add_r(self, sem, tok):
        if self.r.get(sem, 0) < tok:
            self.r[sem] = tok


def build():
    nc = bass.Bass("TRN2", target_bir_lowering=False)
    dt = nc.dram_tensor
    xin = dt("xin", [SPC, S, D], F32, kind="ExternalInput").ap()
    xout = dt("xout", [SPC, S, D], F32, kind="ExternalOutput").ap()
    d_wq = dt("wq", [NPAIR, 128, 1024], F32, kind="ExternalInput").ap()
    d_wk = dt("wk", [NPAIR, 128, 1024], F32, kind="ExternalInput").ap()
    d_wv = dt("wv", [NPAIR, 128, 1024], F32, kind="ExternalInput").ap()
    d_wo = dt("wo", [NPAIR, 128, 1024], F32, kind="ExternalInput").ap()
    d_wg = dt("wg", [128, 64], F32, kind="ExternalInput").ap()
    d_rms = dt("rmsw", [128, 8], F32, kind="ExternalInput").ap()
    d_nrm = dt("nrmw", [128, 8], F32, kind="ExternalInput").ap()
    d_gw = dt("gw", [128, 64], F32, kind="ExternalInput").ap()
    d_gb = dt("gb", [8, 1], F32, kind="ExternalInput").ap()
    d_w1 = dt("w1", [NE, 128, 16384], F32, kind="ExternalInput").ap()
    d_b1 = dt("b1", [128, 128], F32, kind="ExternalInput").ap()
    d_w2 = dt("w2", [NE, 128, 8192], F32, kind="ExternalInput").ap()
    d_b2 = dt("b2", [8, 1024], F32, kind="ExternalInput").ap()
    d_idb = dt("identb", [128, 128], BF16, kind="ExternalInput").ap()
    d_idf = dt("identf", [128, 128], F32, kind="ExternalInput").ap()
    d_mask = dt("maskt", [128, 2048], BF16, kind="ExternalInput").ap()
    d_psel = dt("psel", [8, 512], BF16, kind="ExternalInput").ap()
    d_oneh = dt("oneh", [8, 1024], BF16, kind="ExternalInput").ap()
    d_onec = dt("onescol", [128, 1], BF16, kind="ExternalInput").ap()
    d_oner = dt("onesrow", [1, 128], BF16, kind="ExternalInput").ap()
    d_zscr = dt("zscr", [4096], F32, kind="Internal").ap()
    d_dbg = dt("dbg", [128, 28800], F32, kind="ExternalOutput").ap() if DBG_STOP else None

    with contextlib.ExitStack() as es:
        sbt = lambda n, sh, d_: es.enter_context(nc.sbuf_tensor(n, sh, d_))
        pst = lambda n, sh, d_: es.enter_context(nc.psum_tensor(n, sh, d_))

        x_sb = sbt("x_sb", [128, NCH * 1024], F32)
        identb = sbt("s_identb", [128, 128], BF16)
        identf = sbt("s_identf", [128, 128], F32)
        maskt = sbt("s_maskt", [128, 2048], BF16)
        psel = sbt("s_psel", [8, 512], BF16)
        oneh = sbt("s_oneh", [8, 1024], BF16)
        onescol = sbt("s_onescol", [128, 1], BF16)
        onesrow = sbt("s_onesrow", [1, 128], BF16)
        rms_sb = sbt("rms_sb", [128, 8], F32)
        nrm_sb = sbt("nrm_sb", [128, 8], F32)
        wg_sb = sbt("wg_sb", [128, 64], BF16)
        gw_sb = sbt("gw_sb", [128, 64], BF16)
        gb_sb = sbt("gb_sb", [8, 1], F32)
        b1_sb = sbt("b1_sb", [128, 128], F32)
        b2_sb = sbt("b2_sb", [8, 1024], BF16)
        b2f = sbt("b2f", [8, 1024], F32)
        sums = sbt("sums", [128, 16], F32)
        sqs = sbt("sqs", [128, 16], F32)
        nm = sbt("nm", [128, 16], F32)
        msq = sbt("msq", [128, 16], F32)
        nv = sbt("nv", [128, 16], F32)
        vpe = sbt("vpe", [128, 16], F32)
        lnv = sbt("lnv", [128, 16], F32)
        rstd = sbt("rstd", [128, 16], F32)
        rsq = sbt("rsq", [128, 16], F32)
        vv = sbt("vv", [128, 16], F32)
        mt2 = sbt("mt2", [128, 16], F32)
        lnw = sbt("lnw", [128, 16], F32)
        alphav = sbt("alphav", [128, 16], F32)
        cc1 = sbt("cc1", [128, 16], F32)
        cc2 = sbt("cc2", [128, 16], F32)
        mcol = sbt("mcol", [1, 16], F32)
        msc = sbt("msc", [1, 4], F32)
        nmb1 = sbt("nmb1", [1, 1], BF16)
        negM = sbt("negM", [128, 1], F32)
        lgtm = sbt("lgtm", [128, 128], F32)
        rm1 = sbt("rm1", [128, 16], F32)
        rm2 = sbt("rm2", [128, 16], F32)
        req1 = sbt("req1", [128, 128], F32)
        rl2 = sbt("rl2", [128, 128], F32)
        req2 = sbt("req2", [128, 128], F32)
        rd12 = sbt("rd12", [128, 16], F32)
        rw1 = sbt("rw1", [128, 16], F32)
        rw2 = sbt("rw2", [128, 16], F32)
        rwa = sbt("rwa", [128, 128], F32)
        rwf = sbt("rwf", [128, 128], F32)
        sqscr = sbt("sqscr", [128, 512], F32)
        dscr = sbt("dscr", [128, 64], F32)

        AR_COLS = 61440
        arena = sbt("arena", [128, AR_COLS], BF16)

        class Arena:
            def __init__(self):
                self.off = 0

            def take(self, cols):
                cols = (cols + 31) & ~31
                assert self.off + cols <= AR_COLS, f"arena overflow {self.off}+{cols}"
                o = self.off
                self.off += cols
                return o

            def b16(self, cols, parts=128, name=None):
                o = self.take(cols)
                if name:
                    ARENA_OFF[name] = (o, cols, 'b16', parts)
                return arena[0:parts, o:o + cols]

            def f32(self, cols, parts=128, name=None):
                o = self.take(2 * cols)
                if name:
                    ARENA_OFF[name] = (o, 2 * cols, 'f32', parts)
                return arena[0:parts, o:o + 2 * cols].bitcast(F32)

        A = Arena()
        tbf = A.b16(4096, name='tbf')
        tT = A.b16(16384, name='tT')
        mark = A.off
        qp = A.b16(2048, name='qp')
        kp = A.b16(2048, name='kp')
        vp = A.b16(2080, name='vp')
        q2 = A.b16(2048, name='q2')
        wqkv = A.b16(1024, name='wqkv')
        woall = A.b16(4096, name='woall')
        gTb = A.b16(2048, parts=8, name='gTb')
        outT = A.b16(8192, name='outT')
        Ebufs = [A.b16(512, name=f'E{_i}') for _i in range(6)]
        zst = A.b16(4096, parts=1, name='zst')
        zbuf = A.b16(2048, parts=8, name='zbuf')
        crec8 = A.f32(2048, parts=8, name='crec')
        cgb = A.b16(2048, parts=8, name='cgb')
        attn_end = A.off
        A.off = mark
        w1b = A.b16(16384, name='w1b')
        w2b = A.b16(8192, name='w2b')
        actb = [A.b16(4096, name='act0'), tbf]
        wbr = [A.b16(512, name='wbr0'), A.b16(512, name='wbr1')]
        g1p = [A.f32(512, name='g1a'), A.f32(512, name='g1b')]
        l1p = [A.f32(512, name='l1a'), A.f32(512, name='l1b')]
        lcb2 = [A.f32(512, name='lcba'), A.f32(512, name='lcbb')]
        p1 = A.f32(512, name='p1')
        a0 = A.f32(512, name='a0')
        lgt = A.f32(512, parts=8, name='lgt')
        wTb = A.b16(2048, parts=8, name='wTb')
        ffn_end = A.off
        k2b = tbf
        wqkvB = arena[0:128, ARENA_OFF['lgt'][0]:ARENA_OFF['lgt'][0] + 1024]
        wbuf = [wqkv, wqkvB]
        wbufn = ["wqkv", "wqkvB"]
        assert max(attn_end, ffn_end) <= AR_COLS, (attn_end, ffn_end)

        pb = [pst(f"pb{i}", [128, 512], F32) for i in range(7)]
        pbt = pst("pbt", [128, 1024], BF16)

        sch = Sch()
        for n in ["Cs", "Cg", "Cx", "Cq", "Cz1", "Cz2", "Cw1", "Cw2",
                  "Pe", "Ac", "Dv", "Pl", "So"]:
            sch.add(n, es.enter_context(nc.semaphore(n)))
        block = es.enter_context(nc.Block())
        ENG = {"PE": block.tensor, "VE": block.vector, "SE": block.scalar,
               "GE": block.gpsimd, "SY": block.sync}
        SEM = {"PE": "Pe", "VE": "Dv", "SE": "Ac", "GE": "Pl"}

        R = {}
        for name in (["wqkvB", "k2b"]
                     + ["x", "tbf", "tT", "qp", "kp", "vp", "q2", "wo", "gT", "outT",
                      "zst", "zbuf", "crec", "cgb", "w1", "w2", "act", "g1", "lc",
                      "p1", "a0", "lgt", "wTb", "lgtm", "rwf", "stats", "mh",
                      "negM", "b2", "rtr", "wqkv", "sqscr"]
                     + [f"pb{i}" for i in range(7)] + ["pbt"]
                     + [f"wbr{i}" for i in range(2)]):
            R[name] = Res()

        def sec(en, fn, reads=(), writes=(), war_extra=()):
            sem = SEM.get(en)

            def body(eng):
                for rn in reads:
                    rr = R[rn]
                    if rr.w and rr.w[0] != sem:
                        sch.wait(eng, en, rr.w[0], rr.w[1] + rr.w[2])
                for wn in writes:
                    rr = R[wn]
                    for s_, t_ in rr.r.items():
                        if s_ != sem:
                            sch.wait(eng, en, s_, t_)
                    if rr.w and rr.w[0] != sem:
                        sch.wait(eng, en, rr.w[0], rr.w[1])
                for s_, t_ in war_extra:
                    sch.wait(eng, en, s_, t_)
                out = fn(eng)
                if out is not None and sem is not None:
                    sch.inc(out, sem)
            ENG[en](body)
            tok = sch.tok(sem) if sem else None
            for rn in reads:
                if sem:
                    R[rn].add_r(sem, tok)
            for wn in writes:
                if sem:
                    R[wn].set_w(sem, tok, 1 if en in ("VE", "SE", "GE") else 0)
            return tok

        def seal(en):
            sem = SEM[en]

            def body(eng):
                sch.drain_inc(eng, sem)
            ENG[en](body)
            t = sch.tok(sem)
            for rr in R.values():
                if rr.w and rr.w[0] == sem and rr.w[2] and rr.w[1] < t:
                    rr.w = (sem, t, 0)
            return t

        def _(s):
            for dst, src in ((identf, d_idf), (rms_sb, d_rms), (nrm_sb, d_nrm),
                             (gb_sb, d_gb), (b1_sb, d_b1), (b2f, d_b2)):
                i = s.dma_start(out=dst[:], in_=src[:, :])
                sch.inc(i, "Cs", 16)
        ENG["SY"](_)
        lx0 = sch.tok("Cs")

        def _(g):
            for dst, src in ((identb, d_idb), (maskt, d_mask), (psel, d_psel),
                             (oneh, d_oneh), (onescol, d_onec), (onesrow, d_oner),
                             (wg_sb, d_wg), (gw_sb, d_gw)):
                i = g.dma_start(out=dst[:], in_=src[:, :])
                sch.inc(i, "Cg", 16)
        ENG["GE"](_)
        lw0 = sch.tok("Cg")

        def _(v):
            v.memset(dscr[:], 0.0)
            return v.tensor_scalar(b2_sb[:], b2f[:], GELU_A, None, OP.mult)
        sec("VE", _, writes=("b2",), war_extra=(("Cs", lx0),))
        for _en, _eng in (("PE", "tensor"), ("VE", "vector"), ("SE", "scalar"), ("GE", "gpsimd")):
            def _w(e, _en=_en):
                sch.wait(e, _en_cur[0], "Cs", lx0)
                sch.wait(e, _en_cur[0], "Cg", lw0)
            _en_cur = [_en]
            ENG[_en](_w)
        seal("VE")

        state = {"store": 0, "w1_last": 0, "w2_last": 0}

        def cut(label):
            if DBG_STOP == label:
                raise _CutDone()

        nseq = DBG_NSEQ if DBG_NSEQ is not None else SPC
        for s_i in range(nseq):
          try:
            if s_i == 0:
                def _(s, s_i=s_i):
                    sch.wait(s, "SY", "So", state["store"])
                    for s_, t_ in R["x"].r.items():
                        sch.wait(s, "SY", s_, t_)
                    i = s.dma_start(out=x_sb[:].rearrange("p (c d) -> p c d", c=NCH),
                                    in_=xin[s_i].rearrange("(c p) d -> p c d", p=128))
                    sch.inc(i, "Cx", 16)
                ENG["SY"](_)
                R["x"].set_w("Cx", sch.tok("Cx"))

            cut("load")
            def emit_ln(is_ln2, precomputed=False):
                if not precomputed:
                    def _(sc):
                        i = None
                        for c in range(NCH):
                            xc = x_sb[:, c * 1024:(c + 1) * 1024]
                            sc.activation(sqscr[:], xc[:, 0:512], AF.Square,
                                          accum_out=sqs[:, c:c + 1])
                            i = sc.activation(sqscr[:], xc[:, 512:1024], AF.Square,
                                              accum_out=msq[:, c:c + 1])
                        return i
                    sec("SE", _, reads=("x",), writes=("sqscr", "stats"))
                    sq_t = seal("SE")

                    def _(v):
                        i = None
                        for c in range(NCH):
                            i = v.reduce_sum(sums[:, c:c + 1],
                                             x_sb[:, c * 1024:(c + 1) * 1024], axis=AX.X)
                        return i
                    sec("VE", _, reads=("x",), writes=("stats",))
                    seal("VE")
                else:
                    sq_t = seal("SE")

                def _(v):
                    v.tensor_tensor(vv[:], sqs[:], msq[:], OP.add)
                    v.tensor_scalar(nm[:], sums[:], -1.0 / D, None, OP.mult)
                    v.drain()
                    v.tensor_scalar(msq[:], vv[:], 1.0 / D, None, OP.mult)
                    v.tensor_tensor(vpe[:], nm[:], nm[:], OP.mult)
                    v.drain()
                    v.tensor_tensor(nv[:], vpe[:], msq[:], OP.subtract)
                    v.drain()
                    return v.tensor_scalar(vpe[:], nv[:], -1.0, EPS_LN, OP.mult, OP.add)
                sec("VE", _, reads=("stats",), writes=("stats",), war_extra=(("Ac", sq_t),))
                seal("VE")

                def _(sc):
                    sc.activation(lnv[:], vpe[:], AF.Ln)
                    sc.drain()
                    sc.activation(rstd[:], lnv[:], AF.Exp, scale=-0.5)
                    return sc.activation(rsq[:], lnv[:], AF.Exp, scale=-1.0)
                sec("SE", _, reads=("stats",), writes=("stats",))
                seal("SE")

                def _(v):
                    v.tensor_scalar(vv[:], nv[:], -1.0, None, OP.mult)
                    v.drain()
                    v.tensor_tensor(mt2[:], vv[:], rsq[:], OP.mult)
                    v.drain()
                    return v.tensor_scalar(mt2[:], mt2[:], EPS_RMS, None, OP.add)
                sec("VE", _, reads=("stats",), writes=("stats",))
                seal("VE")

                def _(sc):
                    sc.activation(lnw[:], mt2[:], AF.Ln)
                    sc.drain()
                    return sc.activation(vv[:], lnw[:], AF.Exp, scale=-0.5)
                sec("SE", _, reads=("stats",), writes=("stats",))
                seal("SE")

                def _(v):
                    i = v.tensor_tensor(alphav[:], rstd[:], vv[:], OP.mult)
                    if is_ln2:
                        v.tensor_scalar(cc1[:], rstd[:], 1.0, None, OP.add)
                        i = v.tensor_tensor(cc2[:], nm[:], rstd[:], OP.mult)
                    return i
                sec("VE", _, reads=("stats",), writes=("stats",))
                seal("VE")

            def emit_apply_T(wvec_sb, rewrite, war_pe=0):
                for g in range(NG):
                    def _(v, g=g):
                        i = None
                        for k in range(4):
                            c = g * 4 + k
                            i = v.tensor_scalar(tbf[:, k * 1024:(k + 1) * 1024],
                                                x_sb[:, c * 1024:(c + 1) * 1024],
                                                nm[:, c:c + 1], alphav[:, c:c + 1],
                                                OP.add, OP.mult)
                        return i
                    sec("VE", _, reads=("stats", "x"), writes=("tbf",),
                        war_extra=((("Pe", war_pe),) if g == 0 and war_pe else ()))
                    seal("VE")

                    for dcp in range(4):
                        def _(t, g=g, dcp=dcp):
                            i = None
                            for half, dc in ((0, 2 * dcp), (512, 2 * dcp + 1)):
                                for k in range(4):
                                    i = t.transpose(
                                        pbt[:, half + k * 128:half + (k + 1) * 128],
                                        tbf[:, k * 1024 + dc * 128:k * 1024 + (dc + 1) * 128],
                                        identb[:])
                            return i
                        sec("PE", _, reads=("tbf",), writes=("pbt",))

                        def _(v, g=g, dcp=dcp):
                            i = None
                            for half, dc in ((0, 2 * dcp), (512, 2 * dcp + 1)):
                                i = v.tensor_scalar(
                                    tT[:, dc * 2048 + g * 512:dc * 2048 + (g + 1) * 512],
                                    pbt[:, half:half + 512], wvec_sb[:, dc:dc + 1],
                                    None, OP.mult)
                            return i
                        sec("VE", _, reads=("pbt",), writes=("tT",))

                    if rewrite:
                        def _(v, g=g):
                            i = None
                            for k in range(4):
                                c = g * 4 + k
                                i = v.tensor_scalar(x_sb[:, c * 1024:(c + 1) * 1024],
                                                    x_sb[:, c * 1024:(c + 1) * 1024],
                                                    cc1[:, c:c + 1], cc2[:, c:c + 1],
                                                    OP.mult, OP.add)
                            return i
                        sec("VE", _, reads=("stats",), writes=("x",))
                seal("VE")

            emit_ln(is_ln2=False, precomputed=(s_i > 0))
            cut("ln1")
            emit_apply_T(rms_sb, rewrite=False, war_pe=state["w2_last"])
            cut("applyT")

            g_pe0 = sch.tok("Pe")

            def _(t, g_pe0=g_pe0):
                rrt = R["tT"].w
                sch.wait(t, "PE", rrt[0], rrt[1] + rrt[2])
                for s_, t_ in R["pb6"].r.items():
                    sch.wait(t, "PE", s_, t_)
                for g in range(NG):
                    if g > 0:
                        sch.wait(t, "PE", "Ac", sch.tok("Ac") + g)
                    for dc in range(DCH):
                        i = t.matmul(pb[6][0:8, 0:512], wg_sb[:, dc * 8:(dc + 1) * 8],
                                     tT[:, dc * 2048 + g * 512:dc * 2048 + (g + 1) * 512],
                                     start=(dc == 0), stop=(dc == DCH - 1))
                    sch.inc(i, "Pe")
            ENG["PE"](_)

            def _(sc, g_pe0=g_pe0):
                for g in range(NG):
                    sch.wait(sc, "SE", "Pe", g_pe0 + g + 1)
                    i = sc.activation(gTb[:, g * 512:(g + 1) * 512],
                                      pb[6][0:8, 0:512], AF.Sigmoid)
                    sch.inc(i, "Ac")
                i = sc.copy(dscr[:, 16:32], dscr[:, 0:16])
                sch.inc(i, "Ac")
            ENG["SE"](_)
            R["gT"].set_w("Ac", sch.tok("Ac"), 0)
            R["pb6"].set_w("Pe", sch.tok("Pe"), 0)
            R["pb6"].add_r("Ac", sch.tok("Ac"))
            cut("gates")

            def _(g_, s_i=s_i):
                for s_, t_ in R["wo"].r.items():
                    sch.wait(g_, "GE", s_, t_)
                sch.wait(g_, "GE", "Pe", state["w1_last"])
                i = g_.dma_start(out=woall[:].rearrange("r (p c) -> r p c", p=NPAIR),
                                 in_=d_wo.rearrange("p r c -> r p c"))
                sch.inc(i, "Cq", 16)
                R["wo"].set_w("Cq", sch.cnt["Cq"])
            ENG["GE"](_)

            state["prev_av"] = 0
            for pair in range(NPAIR):
                state["attn_ep"] = sch.tok("Dv")
                def wdma(src, bufi, pair=pair):
                    def _(g_):
                        for s_, t_ in R[wbufn[bufi]].r.items():
                            sch.wait(g_, "GE", s_, t_)
                        if bufi == 1:
                            rl = R["lgt"]
                            if rl.w:
                                sch.wait(g_, "GE", rl.w[0], rl.w[1] + rl.w[2])
                            for s_, t_ in rl.r.items():
                                sch.wait(g_, "GE", s_, t_)
                        sch.wait(g_, "GE", "Pe", state["w1_last"])
                        i = g_.dma_start(out=wbuf[bufi][:], in_=src[pair])
                        sch.inc(i, "Cq", 16)
                        R[wbufn[bufi]].set_w("Cq", sch.cnt["Cq"])
                    ENG["GE"](_)

                if pair == 0:
                    wdma(d_wq, 0)
                    wdma(d_wk, 1)

                for which in (0, 1):
                    dstq = qp if which == 0 else kp
                    bres = "qp" if which == 0 else "kp"
                    wbi = (pair + which) % 2
                    qk_pe0 = sch.tok("Pe")
                    qk_ac0 = sch.tok("Ac")

                    def _(t, qk_pe0=qk_pe0, qk_ac0=qk_ac0, wbi=wbi):
                        rr = R[wbufn[wbi]].w
                        sch.wait(t, "PE", rr[0], rr[1])
                        rrt = R["tT"].w
                        sch.wait(t, "PE", rrt[0], rrt[1] + rrt[2])
                        for bn in ("pb5", "pb6"):
                            for s_, t_ in R[bn].r.items():
                                sch.wait(t, "PE", s_, t_)
                        for g in range(NG):
                            bk = pb[5 + (g % 2)]
                            if g >= 2:
                                sch.wait(t, "PE", "Ac", qk_ac0 + g - 1)
                            for dc in range(DCH):
                                i = t.matmul(bk[:, 0:512], wbuf[wbi][:, dc * 128:(dc + 1) * 128],
                                             tT[:, dc * 2048 + g * 512:dc * 2048 + (g + 1) * 512],
                                             start=(dc == 0), stop=(dc == DCH - 1))
                            sch.inc(i, "Pe")
                    ENG["PE"](_)

                    def _(sc, dstq=dstq, qk_pe0=qk_pe0, which=which):
                        for g in range(NG):
                            sch.wait(sc, "SE", "Pe", qk_pe0 + g + 1)
                            i = sc.copy(dstq[:, g * 512:(g + 1) * 512], pb[5 + (g % 2)][:, 0:512])
                            sch.inc(i, "Ac")
                        sqdst = q2 if which == 0 else k2b
                        rn = "q2" if which == 0 else "k2b"
                        for s_, t_ in R[rn].r.items():
                            sch.wait(sc, "SE", s_, t_)
                        i = sc.activation(sqdst[:, 0:2048], dstq[:], AF.Square)
                        sch.inc(i, "Ac")
                    ENG["SE"](_)
                    R["q2" if which == 0 else "k2b"].set_w("Ac", sch.tok("Ac"), 1)
                    if which == 1:
                        seal("SE")
                    R[wbufn[wbi]].add_r("Pe", sch.tok("Pe"))
                    if wbi == 1:
                        R["lgt"].add_r("Pe", sch.tok("Pe"))
                    R[bres].set_w("Ac", sch.tok("Ac"), 1)
                    R["pb5"].set_w("Pe", qk_pe0 + 3, 0)
                    R["pb6"].set_w("Pe", qk_pe0 + 4, 0)
                    R["pb5"].add_r("Ac", qk_ac0 + 3)
                    R["pb6"].add_r("Ac", qk_ac0 + 4)
                wdma(d_wv, pair % 2)

                def _(t):
                    for g4 in range(4):
                        i = t.matmul(pb[g4][0:1, 0:512], onescol[:],
                                     q2[:, g4 * 512:(g4 + 1) * 512], start=True, stop=True)
                        sch.inc(i, "Pe")
                    return None
                sec("PE", _, reads=("q2",), writes=("pb0", "pb1", "pb2", "pb3"))
                nq_t = sch.tok("Pe")
                for g4 in range(4):
                    R[f"pb{g4}"].set_w("Pe", nq_t, 0)

                def _(v):
                    sch.wait(v, "VE", "Pe", nq_t)
                    i = None
                    for g4 in range(4):
                        i = v.reduce_max(mcol[0:1, g4:g4 + 1], pb[g4][0:1, 0:512], axis=AX.X)
                    sch.drain_inc(v, "Dv")
                    return None
                sec("VE", _, reads=("pb0", "pb1", "pb2", "pb3"), writes=("mh",))
                qmax_t = sch.tok("Dv")
                R["mh"].set_w("Dv", qmax_t, 0)

                vp_pe0 = sch.tok("Pe")
                vp_dv0 = sch.tok("Dv")
                ev_tok = {}

                def _(t, vp_pe0=vp_pe0, wbi=pair % 2):
                    rr = R[wbufn[wbi]].w
                    sch.wait(t, "PE", rr[0], rr[1])
                    for bn in ("pb4", "pb5"):
                        for s_, t_ in R[bn].r.items():
                            sch.wait(t, "PE", s_, t_)
                    for c in range(12):
                        bk = pb[4 + c % 2]
                        if c >= 2:
                            sch.wait(t, "PE", "Dv", vp_dv0 + c - 1)
                        for dc in range(DCH):
                            i = t.matmul(bk[:, 0:128],
                                         tT[:, dc * 2048 + c * 128:dc * 2048 + (c + 1) * 128],
                                         wbuf[pair % 2][:, dc * 128:(dc + 1) * 128],
                                         start=(dc == 0), stop=(dc == DCH - 1))
                        sch.inc(i, "Pe")
                ENG["PE"](_)
                for c in range(12):
                    ev_tok[c] = vp_dv0 + c + 1

                def _(v, vp_pe0=vp_pe0):
                    for c in range(12):
                        bk = pb[4 + c % 2]
                        sch.wait(v, "VE", "Pe", vp_pe0 + c + 1)
                        v.tensor_copy(vp[:, c * 130:c * 130 + 64], bk[:, 0:64])
                        i = v.tensor_copy(vp[:, c * 130 + 65:c * 130 + 129], bk[:, 64:128])
                        sch.inc(i, "Dv")
                ENG["VE"](_)

                def _(t):
                    sch.wait(t, "PE", "Dv", qmax_t)
                    for g4 in range(4):
                        i = t.matmul(pb[g4][0:1, 0:512], onescol[:],
                                     k2b[:, g4 * 512:(g4 + 1) * 512], start=True, stop=True)
                        sch.inc(i, "Pe")
                    return None
                sec("PE", _, reads=("k2b",), writes=("pb0", "pb1", "pb2", "pb3"))
                nk_t = sch.tok("Pe")
                for g4 in range(4):
                    R[f"pb{g4}"].set_w("Pe", nk_t, 0)

                def _(v):
                    sch.wait(v, "VE", "Pe", nk_t)
                    for g4 in range(4):
                        v.reduce_max(mcol[0:1, 8 + g4:9 + g4], pb[g4][0:1, 0:512], axis=AX.X)
                    sch.drain_inc(v, "Dv")
                    v.reduce_max(msc[0:1, 0:1], mcol[0:1, 0:4], axis=AX.X)
                    v.reduce_max(msc[0:1, 1:2], mcol[0:1, 8:12], axis=AX.X)
                    sch.drain_inc(v, "Dv")
                    v.drain()
                    v.tensor_tensor(msc[0:1, 2:3], msc[0:1, 0:1], msc[0:1, 1:2], OP.mult)
                    sch.drain_inc(v, "Dv")
                    return None
                sec("VE", _, reads=("pb0", "pb1", "pb2", "pb3"), writes=("mh",))
                msc_t = sch.tok("Dv")
                R["mh"].set_w("Dv", msc_t, 0)

                def _(sc):
                    sch.wait(sc, "SE", "Dv", msc_t)
                    return sc.activation(msc[0:1, 3:4], msc[0:1, 2:3], AF.Sqrt)
                sec("SE", _, writes=("mh",))
                seal("SE")

                def _(v):
                    return v.tensor_scalar(nmb1[0:1, 0:1], msc[0:1, 3:4], -0.125, None, OP.mult)
                sec("VE", _, reads=("mh",), writes=("mh",))
                nmb1_t = sch.tok("Dv")

                ev_tok[12] = nmb1_t + 1
                ev_tok[13] = nmb1_t + 2

                def _(t, vp_pe0=vp_pe0):
                    for c in range(12, NCH):
                        bk = pb[4 + c % 2]
                        sch.wait(t, "PE", "Dv", ev_tok[c - 2])
                        for dc in range(DCH):
                            i = t.matmul(bk[:, 0:128],
                                         tT[:, dc * 2048 + c * 128:dc * 2048 + (c + 1) * 128],
                                         wbuf[pair % 2][:, dc * 128:(dc + 1) * 128],
                                         start=(dc == 0), stop=(dc == DCH - 1))
                        sch.inc(i, "Pe")
                ENG["PE"](_)
                vpB_pe0 = sch.tok("Pe") - 4

                def _(v, vpB_pe0=vpB_pe0):
                    for c in range(12, NCH):
                        bk = pb[4 + c % 2]
                        sch.wait(v, "VE", "Pe", vpB_pe0 + (c - 12) + 1)
                        v.tensor_copy(vp[:, c * 130:c * 130 + 64], bk[:, 0:64])
                        i = v.tensor_copy(vp[:, c * 130 + 65:c * 130 + 129], bk[:, 64:128])
                        sch.inc(i, "Dv")
                        if c in ev_tok:
                            assert ev_tok[c] == sch.tok("Dv"), (c, ev_tok[c])
                        ev_tok[c] = sch.tok("Dv")
                    v3 = vp[:].rearrange("p (cc j) -> p cc j", cc=NCH)
                    v.memset(v3[:, :, 64:65], 1.0)
                    v.memset(v3[:, :, 129:130], 1.0)
                    i = v.tensor_copy(dscr[:, 32:48], dscr[:, 0:16])
                    sch.inc(i, "Dv")
                ENG["VE"](_)
                R[wbufn[pair % 2]].add_r("Pe", sch.tok("Pe"))
                if pair % 2 == 1:
                    R["lgt"].add_r("Pe", sch.tok("Pe"))
                R["vp"].set_w("Dv", sch.tok("Dv"), 0)
                R["pb4"].set_w("Pe", sch.tok("Pe") - 1, 0)
                R["pb5"].set_w("Pe", sch.tok("Pe"), 0)
                R["pb4"].add_r("Dv", ev_tok[14])
                R["pb5"].add_r("Dv", ev_tok[15])
                if pair < NPAIR - 1:
                    wdma(d_wq, (pair + 1) % 2, pair=pair + 1)
                    wdma(d_wk, pair % 2, pair=pair + 1)

                def _(t):
                    sch.wait(t, "PE", "Dv", nmb1_t + 1)
                    return t.matmul(pb[6][:, 0:1], onesrow[0:1, :], nmb1[0:1, 0:1],
                                    start=True, stop=True)
                sec("PE", _, writes=("pb6",))

                def _(v):
                    return v.tensor_copy(negM[:], pb[6][:, 0:1])
                sec("VE", _, reads=("pb6",), writes=("negM",))
                negM_t = seal("VE")
                cut(f"mhat{pair}")

                for g in range(NG):
                    J = 4 * g + 4
                    d0 = 4 * g

                    def qw(j):
                        return max(0, (j - d0) * 128) if j >= d0 else 0

                    peops = []
                    for j in range(J):
                        peops.append(("s", j))
                        if j >= 3:
                            peops.append(("av", j - 3))
                    peops += [("av", J - 3), ("av", J - 2), ("av", J - 1)]
                    c_pe0 = sch.tok("Pe")
                    c_ac0 = sch.tok("Ac")
                    c_dv0 = sch.tok("Dv")
                    p_pe = {op: c_pe0 + k + 1 for k, op in enumerate(peops)}
                    p_ex = {j: c_ac0 + 2 * j + 1 for j in range(J)}
                    p_exd = {j: c_ac0 + 2 * j + 2 for j in range(J)}
                    p_mkd = {d0 + c: c_dv0 + 2 * (c + 1) for c in range(4)}

                    def _(t, g=g, J=J, d0=d0):
                        sch.wait(t, "PE", "Dv", state["attn_ep"])
                        sch.wait(t, "PE", "Dv", negM_t)
                        for kind, j in peops:
                            if kind == "s":
                                w = 512 - qw(j)
                                if j >= 2:
                                    sch.wait(t, "PE", "Ac", p_ex(j - 2) if False else p_ex[j - 2])
                                q0 = g * 512 + qw(j)
                                t.matmul(pb[j % 2][:, 0:w], kp[0:64, j * 128:(j + 1) * 128],
                                         qp[0:64, q0:(g + 1) * 512], start=True, stop=True)
                                i = t.matmul(pb[2 + j % 2][:, 0:w],
                                             kp[64:128, j * 128:(j + 1) * 128],
                                             qp[64:128, q0:(g + 1) * 512], start=True, stop=True)
                            else:
                                w = 512 - qw(j)
                                if j >= d0:
                                    sch.wait(t, "PE", "Dv", p_mkd[j])
                                else:
                                    sch.wait(t, "PE", "Ac", p_exd[j])
                                o0 = qw(j)
                                t.matmul(pb[4][0:65, o0:512], vp[:, j * 130:j * 130 + 65],
                                         Ebufs[j % 3][:, 0:w], start=(j == 0), stop=(j == J - 1))
                                i = t.matmul(pb[5][0:65, o0:512],
                                             vp[:, j * 130 + 65:j * 130 + 130],
                                             Ebufs[3 + j % 3][:, 0:w], start=(j == 0), stop=(j == J - 1))
                            sch.inc(i, "Pe")
                    ENG["PE"](_)
                    assert sch.tok("Pe") == c_pe0 + len(peops)

                    def _(sc, g=g, J=J):
                        sch.wait(sc, "SE", "Dv", negM_t)
                        for j in range(J):
                            w = 512 - qw(j)
                            sch.wait(sc, "SE", "Pe", p_pe[("s", j)])
                            if j >= 3:
                                sch.wait(sc, "SE", "Pe", p_pe[("av", j - 3)])
                            elif state["prev_av"]:
                                sch.wait(sc, "SE", "Pe", state["prev_av"])
                            sc.activation(Ebufs[j % 3][:, 0:w], pb[j % 2][:, 0:w], AF.Exp,
                                          bias=negM[:, 0:1], scale=0.125)
                            i = sc.activation(Ebufs[3 + j % 3][:, 0:w], pb[2 + j % 2][:, 0:w],
                                              AF.Exp, bias=negM[:, 0:1], scale=0.125)
                            sch.inc(i, "Ac")
                            i = sc.copy(dscr[:, 16:32], dscr[:, 0:16])
                            sch.inc(i, "Ac")
                    ENG["SE"](_)
                    assert sch.tok("Ac") == c_ac0 + 2 * J

                    def _(v, g=g, J=J, d0=d0, pair=pair):
                        for j in range(d0, J):
                            c = j - d0
                            w = 512 - 128 * c
                            sch.wait(v, "VE", "Ac", p_exd[j])
                            v.tensor_tensor(Ebufs[j % 3][:, 0:w], Ebufs[j % 3][:, 0:w],
                                            maskt[:, 0:w], OP.mult)
                            i = v.tensor_tensor(Ebufs[3 + j % 3][:, 0:w],
                                                Ebufs[3 + j % 3][:, 0:w],
                                                maskt[:, 0:w], OP.mult)
                            sch.inc(i, "Dv")
                            i = v.tensor_copy(dscr[:, 32:48], dscr[:, 0:16])
                            sch.inc(i, "Dv")
                        sch.wait(v, "VE", "Pe", p_pe[("av", J - 1)])
                        col = pair * 2048 + g * 512
                        zc = g * 1024
                        v.tensor_copy(outT[0:64, col:col + 512], pb[4][0:64, 0:512])
                        v.tensor_copy(outT[64:128, col:col + 512], pb[5][0:64, 0:512])
                        v.tensor_copy(zst[0:1, zc:zc + 512], pb[4][64:65, 0:512])
                        i = v.tensor_copy(zst[0:1, zc + 512:zc + 1024], pb[5][64:65, 0:512])
                        sch.inc(i, "Dv")
                        i = v.tensor_copy(dscr[:, 32:48], dscr[:, 0:16])
                        sch.inc(i, "Dv")
                    ENG["VE"](_)
                    assert sch.tok("Dv") == c_dv0 + 2 * 4 + 2
                    state["attn_ep"] = sch.tok("Dv")
                    state["prev_av"] = p_pe[("av", J - 1)]
                    cut(f"core{pair}_{g}")
                R["outT"].set_w("Dv", sch.tok("Dv"), 0)
                R["zst"].set_w("Dv", sch.tok("Dv"), 0)
                R["pb4"].set_w("Pe", state["prev_av"], 0)
                R["pb5"].set_w("Pe", state["prev_av"], 0)
                R["pb4"].add_r("Dv", state["attn_ep"])
                R["pb5"].add_r("Dv", state["attn_ep"])
                for jj in range(4):
                    R[f"pb{jj}"].set_w("Pe", state["prev_av"], 0)
                    R[f"pb{jj}"].add_r("Dv", state["attn_ep"])
                cut(f"preZ{pair}")
                R["negM"].add_r("Ac", sch.tok("Ac"))
                cut(f"preZ{pair}")

                def _(g_, pair=pair):
                    rr = R["zst"]
                    sch.wait(g_, "GE", rr.w[0], rr.w[1] + rr.w[2])
                    for s_, t_ in R["zbuf"].r.items():
                        sch.wait(g_, "GE", s_, t_)
                    sch.wait(g_, "GE", "Cz2", sch.cnt["Cz2"])
                    i = g_.dma_start(out=d_zscr.rearrange("(o w) -> o w", o=1), in_=zst[0:1, :])
                    sch.inc(i, "Cz1", 16)
                    g_.wait_ge(sch.sem["Cz1"], sch.cnt["Cz1"])
                    i = g_.dma_start(
                        out=zbuf[2 * pair:2 * pair + 2, :].rearrange("h (g w) -> h g w", g=4),
                        in_=d_zscr.rearrange("(g h w) -> h g w", g=4, h=2))
                    sch.inc(i, "Cz2", 16)
                    R["zst"].add_r("Cz1", sch.cnt["Cz1"])
                ENG["GE"](_)
                R["zbuf"].set_w("Cz2", sch.tok("Cz2"))
                cut(f"pair{pair}")

            def _(v):
                return v.reciprocal(crec8[:], zbuf[0:8, :])
            sec("VE", _, reads=("zbuf",), writes=("crec",))
            seal("VE")

            def _(v):
                return v.tensor_tensor(cgb[:], crec8[:], gTb[:], OP.mult)
            sec("VE", _, reads=("crec", "gT"), writes=("cgb",))
            cgb_t2 = seal("VE")

            state["scale_done"] = cgb_t2
            state["pred_done"] = cgb_t2
            for g in range(NG):
                s_pe0 = sch.tok("Pe")
                s_dv0 = sch.tok("Dv")

                def _(t, g=g, s_pe0=s_pe0, s_dv0=s_dv0):
                    sch.wait(t, "PE", "Dv", cgb_t2)
                    sch.wait(t, "PE", "Dv", state["attn_ep"])
                    for pair in range(NPAIR):
                        if pair >= 2:
                            sch.wait(t, "PE", "Dv", s_dv0 + pair - 1)
                        else:
                            sch.wait(t, "PE", "Dv", state["scale_done"])
                        i = t.matmul(pb[2 + pair % 2][:, 0:512],
                                     psel[:, pair * 128:(pair + 1) * 128],
                                     cgb[0:8, g * 512:(g + 1) * 512], start=True, stop=True)
                        sch.inc(i, "Pe")
                    for k in range(8):
                        ns, dh = divmod(k, 2)
                        sch.wait(t, "PE", "Dv", s_dv0 + 5)
                        if k >= 2:
                            sch.wait(t, "PE", "Dv", s_dv0 + 5 + k - 1)
                        else:
                            sch.wait(t, "PE", "Dv", state["pred_done"])
                        for pair in range(NPAIR):
                            i = t.matmul(pb[k % 2][:, 0:512],
                                         outT[:, pair * 2048 + g * 512 + ns * 128:
                                              pair * 2048 + g * 512 + (ns + 1) * 128],
                                         woall[:, pair * 1024 + dh * 512:
                                               pair * 1024 + (dh + 1) * 512],
                                         start=(pair == 0), stop=(pair == NPAIR - 1))
                        sch.inc(i, "Pe")
                ENG["PE"](_)

                def _(v, g=g, s_pe0=s_pe0):
                    for pair in range(NPAIR):
                        sch.wait(v, "VE", "Pe", s_pe0 + pair + 1)
                        col = pair * 2048 + g * 512
                        i = v.tensor_tensor(outT[:, col:col + 512], outT[:, col:col + 512],
                                            pb[2 + pair % 2][:, 0:512], OP.mult)
                        sch.inc(i, "Dv")
                    i = v.tensor_copy(dscr[:, 32:48], dscr[:, 0:16])
                    sch.inc(i, "Dv")
                    for k in range(8):
                        ns, dh = divmod(k, 2)
                        c = g * 4 + ns
                        sch.wait(v, "VE", "Pe", s_pe0 + 4 + k + 1)
                        xsl = x_sb[:, c * 1024 + dh * 512:c * 1024 + (dh + 1) * 512]
                        i = v.tensor_tensor(xsl, xsl, pb[k % 2][:, 0:512], OP.add)
                        sch.inc(i, "Dv")
                ENG["VE"](_)
                state["scale_done"] = s_dv0 + 5
                state["pred_done"] = s_dv0 + 13

                def _(sc, g=g, s_dv0=s_dv0):
                    i = None
                    for ns in range(4):
                        c = g * 4 + ns
                        sch.wait(sc, "SE", "Dv", s_dv0 + 7 + 2 * ns)
                        xc = x_sb[:, c * 1024:(c + 1) * 1024]
                        sc.activation(sqscr[:], xc[:, 0:512], AF.Square,
                                      accum_out=sqs[:, c:c + 1])
                        i = sc.activation(sqscr[:], xc[:, 512:1024], AF.Square,
                                          accum_out=msq[:, c:c + 1])
                    sch.inc(i, "Ac")
                ENG["SE"](_)
                R["x"].add_r("Ac", sch.tok("Ac"))

                def _(v, g=g):
                    i = None
                    for ns in range(4):
                        c = g * 4 + ns
                        i = v.reduce_sum(sums[:, c:c + 1],
                                         x_sb[:, c * 1024:(c + 1) * 1024], axis=AX.X)
                    sch.inc(i, "Dv")
                ENG["VE"](_)
                R["x"].add_r("Dv", sch.tok("Dv"))
            def _(v):
                sch.drain_inc(v, "Dv")
            ENG["VE"](_)
            R["x"].set_w("Dv", sch.tok("Dv"), 0)
            state["pred_dv"] = sch.tok("Dv")
            cut("pred")
            emit_ln(is_ln2=True, precomputed=True)
            emit_apply_T(nrm_sb, rewrite=True)
            hT_t = sch.tok("Dv")
            cut("ln2")

            for g in range(NG):
                def _(t, g=g):
                    i = None
                    for dc in range(DCH):
                        i = t.matmul(pb[6][0:8, 0:512], gw_sb[:, dc * 8:(dc + 1) * 8],
                                     tT[:, dc * 2048 + g * 512:dc * 2048 + (g + 1) * 512],
                                     start=(dc == 0), stop=(dc == DCH - 1))
                    return i
                sec("PE", _, reads=("tT",), writes=("pb6",))

                def _(v, g=g):
                    return v.tensor_scalar(lgt[:], pb[6][0:8, 0:512], gb_sb[0:8, 0:1],
                                           None, OP.add)
                sec("VE", _, reads=("pb6",), writes=("lgt",))
                seal("VE")

                def _(t, g=g):
                    i = None
                    for k in range(4):
                        i = t.transpose(pb[5][:, k * 8:(k + 1) * 8],
                                        lgt[0:8, k * 128:(k + 1) * 128], identf[0:8, 0:8])
                    return i
                sec("PE", _, reads=("lgt",), writes=("pb5",))

                def _(v, g=g):
                    return v.tensor_copy(lgtm[:, g * 32:g * 32 + 32], pb[5][:, 0:32])
                sec("VE", _, reads=("pb5",), writes=("lgtm",))
            seal("VE")

            def _(v):
                L3 = lgtm[:].rearrange("p (s e) -> p s e", e=8)
                v.tensor_reduce(rm1[:], L3, axis=AX.X, op=OP.max)
                sch.drain_inc(v, "Dv")
                v.tensor_tensor(req1[:].rearrange("p (s e) -> p s e", e=8), L3,
                                rm1[:].rearrange("p s -> p s ()").broadcast_to([128, 16, 8]),
                                OP.is_equal)
                v.drain()
                v.scalar_tensor_tensor(rl2[:], req1[:], -1e30, lgtm[:], OP.mult, OP.add)
                sch.drain_inc(v, "Dv")
                v.tensor_reduce(rm2[:], rl2[:].rearrange("p (s e) -> p s e", e=8),
                                axis=AX.X, op=OP.max)
                sch.drain_inc(v, "Dv")
                v.tensor_tensor(req2[:].rearrange("p (s e) -> p s e", e=8),
                                rl2[:].rearrange("p (s e) -> p s e", e=8),
                                rm2[:].rearrange("p s -> p s ()").broadcast_to([128, 16, 8]),
                                OP.is_equal)
                v.drain()
                return v.tensor_tensor(rd12[:], rm1[:], rm2[:], OP.subtract)
            sec("VE", _, reads=("lgtm",), writes=("rtr",))
            seal("VE")

            def _(sc):
                return sc.activation(rw1[:], rd12[:], AF.Sigmoid)
            sec("SE", _, reads=("rtr",), writes=("rtr",))
            seal("SE")

            def _(v):
                v.tensor_scalar(rw2[:], rw1[:], -1.0, 1.0, OP.mult, OP.add)
                v.drain()
                v.tensor_tensor(rwa[:].rearrange("p (s e) -> p s e", e=8),
                                req1[:].rearrange("p (s e) -> p s e", e=8),
                                rw1[:].rearrange("p s -> p s ()").broadcast_to([128, 16, 8]),
                                OP.mult)
                v.tensor_tensor(rwf[:].rearrange("p (s e) -> p s e", e=8),
                                req2[:].rearrange("p (s e) -> p s e", e=8),
                                rw2[:].rearrange("p s -> p s ()").broadcast_to([128, 16, 8]),
                                OP.mult)
                v.drain()
                return v.tensor_tensor(rwf[:], rwf[:], rwa[:], OP.add)
            sec("VE", _, reads=("rtr",), writes=("rwf",))
            seal("VE")

            for g in range(NG):
                def _(t, g=g):
                    i = None
                    for k in range(4):
                        sub = g * 4 + k
                        i = t.transpose(pb[5][0:8, k * 128:(k + 1) * 128],
                                        rwf[:, sub * 8:(sub + 1) * 8], identf[:])
                    return i
                sec("PE", _, reads=("rwf",), writes=("pb5",))

                def _(v, g=g):
                    return v.tensor_scalar(wTb[:, g * 512:(g + 1) * 512], pb[5][0:8, 0:512],
                                           1.0 / GELU_A, None, OP.mult)
                sec("VE", _, reads=("pb5",), writes=("wTb",))
            seal("VE")
            cut("router")

            wtb_t2 = sch.tok("Dv")
            NT = NE * NG
            state["t1war"] = [("Dv", wtb_t2)] * 4
            state["m2war"] = [wtb_t2, wtb_t2]
            state["pb6war"] = ("Dv", wtb_t2)
            state["wbrR"] = [0, 0]
            state["actR"] = [0, 0]
            prev = {}
            w1tok, w2tok = {}, {}
            cx_toks = {}

            def _(s):
                sch.wait(s, "GE", "Dv", state["pred_dv"])
                sch.wait(s, "GE", "Pe", state["w1_last"])
                i = s.dma_start(out=w1b[:, 0:8192], in_=d_w1[0][:, 0:8192])
                sch.inc(i, "Cw1", 16)
                i = s.dma_start(out=w1b[:, 8192:16384], in_=d_w1[0][:, 8192:16384])
                sch.inc(i, "Cw1", 16)
                sch.wait(s, "GE", "Pe", state["w2_last"])
                i = s.dma_start(out=w2b[:], in_=d_w2[0])
                sch.inc(i, "Cw2", 16)
            ENG["GE"](_)
            w1tok[0] = (sch.tok("Cw1") - 16, sch.tok("Cw1"))
            w2tok[0] = sch.tok("Cw2")

            for n in range(NT + 1):
                cur = n if n < NT else None
                prv = n - 1 if n >= 1 else None
                par = n % 2
                if cur is not None:
                    e_c, t_c = divmod(cur, NG)

                f_pe0 = sch.tok("Pe")
                f_ac0 = sch.tok("Ac")
                f_dv0 = sch.tok("Dv")
                f_pl0 = sch.tok("Pl")
                xacc_base = f_dv0 + (32 if cur is not None else 0)

                def p_eg(c, b=f_ac0):
                    return b + (2 if c == 0 else 3 * c + 1)

                def p_el(c, b=f_ac0):
                    return b + (3 if c == 0 else 3 * c + 2)

                def p_si(c, b=f_ac0):
                    return b + (3 * c + 6 if c <= 6 else 26)

                def p_lcb(c, b=f_dv0):
                    return b + (2 if c == 0 else 4 * c)

                def p_a0(c, b=f_dv0):
                    return b + (4 * (c + 1) + 1 if c <= 6 else 31)

                def p_a0d(c, b=f_dv0):
                    return b + (4 * (c + 1) + 2 if c <= 6 else 32)

                m2_new = []

                def _(t, cur=cur, prv=prv, xacc_base=xacc_base, f_pe0=f_pe0,
                      p_eg=p_eg, p_el=p_el, f_pl0=f_pl0, m2_new=m2_new):
                    if cur is not None:
                        e, t_i = divmod(cur, NG)
                        sch.wait(t, "PE", "Dv", wtb_t2)
                        sch.wait(t, "PE", state["pb6war"][0], state["pb6war"][1])
                        i = t.matmul(pb[6][:, 0:512], oneh[:, e * 128:(e + 1) * 128],
                                     wTb[0:8, t_i * 512:(t_i + 1) * 512],
                                     start=True, stop=True)
                        sch.inc(i, "Pe")
                        sch.wait(t, "PE", "Cw1", w1tok[e][0])
                        rrt = R["tT"].w
                        sch.wait(t, "PE", rrt[0], rrt[1] + rrt[2])
                        for c in range(ICH):
                            if c == 4:
                                sch.wait(t, "PE", "Cw1", w1tok[e][1])
                            for which in (0, 1):
                                fc = 2 * c + which
                                bi = (c % 2) if which == 0 else 2 + c % 2
                                bk = pb[bi]
                                if c >= 2:
                                    sch.wait(t, "PE", "Ac",
                                             p_eg(c - 2) if which == 0 else p_el(c - 2))
                                else:
                                    s_, t_ = state["t1war"][bi]
                                    sch.wait(t, "PE", s_, t_)
                                for dc in range(DCH):
                                    i = t.matmul(
                                        bk[:, 0:512],
                                        w1b[:, fc * 1024 + dc * 128:fc * 1024 + (dc + 1) * 128],
                                        tT[:, dc * 2048 + t_i * 512:dc * 2048 + (t_i + 1) * 512],
                                        start=(dc == 0), stop=(dc == DCH - 1))
                                sch.inc(i, "Pe")
                    if prv is not None:
                        ep, tp = divmod(prv, NG)
                        sch.wait(t, "PE", "Cw2", w2tok[ep])
                        for k in range(8):
                            ns, dh = divmod(k, 2)
                            c2 = tp * 4 + ns
                            bk = pb[4 + k % 2]
                            if k >= 2:
                                sch.wait(t, "PE", "Dv", xacc_base + (k - 2) + 1)
                            else:
                                sch.wait(t, "PE", "Dv", state["m2war"][k % 2])
                            first = True
                            if ep == 0:
                                t.matmul(bk[:, 0:512], wTb[0:8, c2 * 128:(c2 + 1) * 128],
                                         b2_sb[0:8, dh * 512:(dh + 1) * 512],
                                         start=True, stop=False)
                                first = False
                            for ic in range(ICH):
                                if k == 0:
                                    sch.wait(t, "PE", "Pl", min(f_pl0, f_pl0 - 7 + ic))
                                i = t.matmul(
                                    bk[:, 0:512],
                                    actb[prv % 2][:, ic * 512 + ns * 128:ic * 512 + (ns + 1) * 128],
                                    w2b[:, ic * 1024 + dh * 512:ic * 1024 + (dh + 1) * 512],
                                    start=(first and ic == 0), stop=(ic == ICH - 1))
                                first = False
                            sch.inc(i, "Pe")
                            m2_new.append(sch.tok("Pe"))
                ENG["PE"](_)
                exp_pe = (1 + 16 if cur is not None else 0) + (8 if prv is not None else 0)
                assert sch.tok("Pe") == f_pe0 + exp_pe

                if cur is not None:
                    def _(sc, cur=cur, par=par, f_pe0=f_pe0, f_ac0=f_ac0,
                          p_lcb=p_lcb, p_a0=p_a0, prev=prev):
                        e, t_i = divmod(cur, NG)
                        sch.wait(sc, "SE", "Pe", f_pe0 + 1)
                        sch.wait(sc, "SE", "Pl", state["wbrR"][par])
                        i = sc.copy(wbr[par][:], pb[6][:, 0:512])
                        sch.inc(i, "Ac")

                        def eg(c):
                            bgc = b1_sb[:, e * 16 + 2 * c:e * 16 + 2 * c + 1]
                            sch.wait(sc, "SE", "Pe", f_pe0 + 2 + 2 * c)
                            if c >= 2:
                                sch.wait(sc, "SE", "Dv", p_lcb(c - 2))
                            elif prev:
                                sch.wait(sc, "SE", "Dv", prev["lcb"][c])
                            i = sc.activation(g1p[c % 2][:], pb[c % 2][:, 0:512],
                                              AF.Identity, bias=bgc)
                            sch.inc(i, "Ac")

                        def el(c):
                            blc = b1_sb[:, e * 16 + 2 * c + 1:e * 16 + 2 * c + 2]
                            sch.wait(sc, "SE", "Pe", f_pe0 + 3 + 2 * c)
                            if c >= 2:
                                sch.wait(sc, "SE", "Dv", p_lcb(c - 2))
                            elif prev:
                                sch.wait(sc, "SE", "Dv", prev["lcb"][c])
                            i = sc.activation(l1p[c % 2][:], pb[2 + c % 2][:, 0:512],
                                              AF.Identity, bias=blc)
                            sch.inc(i, "Ac")

                        def si(c):
                            sch.wait(sc, "SE", "Dv", p_lcb(c))
                            if c >= 1:
                                sch.wait(sc, "SE", "Dv", p_a0(c - 1))
                            elif prev:
                                sch.wait(sc, "SE", "Dv", prev["a07"])
                            i = sc.activation(p1[:], g1p[c % 2][:], AF.Silu, scale=GELU_A)
                            sch.inc(i, "Ac")

                        eg(0); el(0)
                        for c in range(1, ICH):
                            eg(c); el(c); si(c - 1)
                        i = sc.copy(dscr[:, 16:32], dscr[:, 0:16])
                        sch.inc(i, "Ac")
                        si(ICH - 1)
                        i = sc.copy(dscr[:, 16:32], dscr[:, 0:16])
                        sch.inc(i, "Ac")
                    ENG["SE"](_)
                    assert sch.tok("Ac") == f_ac0 + 27

                def _(v, cur=cur, prv=prv, f_pl0=f_pl0, m2_new=m2_new,
                      p_eg=p_eg, p_el=p_el, p_si=p_si):
                    if cur is not None:
                        def minglcb(c):
                            sch.wait(v, "VE", "Ac", p_eg(c) + 1)
                            i = v.tensor_scalar(g1p[c % 2][:], g1p[c % 2][:],
                                                LIMIT, None, OP.min)
                            sch.inc(i, "Dv")
                            sch.wait(v, "VE", "Ac", p_el(c) + 1)
                            i = v.tensor_scalar(lcb2[c % 2][:], l1p[c % 2][:],
                                                LIMIT, -LIMIT, OP.min, OP.max)
                            sch.inc(i, "Dv")

                        def a0op(c):
                            sch.wait(v, "VE", "Ac", p_si(c) + 1)
                            sch.wait(v, "VE", "Pl", f_pl0 + c)
                            i = v.scalar_tensor_tensor(a0[:], lcb2[c % 2][:], 1.0,
                                                       p1[:], OP.add, OP.mult)
                            sch.inc(i, "Dv")
                            i = v.tensor_copy(dscr[:, 32:48], dscr[:, 0:16])
                            sch.inc(i, "Dv")

                        minglcb(0)
                        for c in range(1, ICH):
                            minglcb(c)
                            a0op(c - 1)
                        a0op(ICH - 1)
                    if prv is not None:
                        tp = prv % NG
                        for k in range(8):
                            ns, dh = divmod(k, 2)
                            c2 = tp * 4 + ns
                            sch.wait(v, "VE", "Pe", m2_new[k])
                            xsl = x_sb[:, c2 * 1024 + dh * 512:c2 * 1024 + (dh + 1) * 512]
                            i = v.tensor_tensor(xsl, xsl, pb[4 + k % 2][:, 0:512], OP.add)
                            sch.inc(i, "Dv")
                ENG["VE"](_)
                exp_dv = (32 if cur is not None else 0) + (8 if prv is not None else 0)
                assert sch.tok("Dv") == f_dv0 + exp_dv

                if prv is not None and prv >= NT - NG:
                    tq = prv - (NT - NG)

                    def _(v):
                        sch.drain_inc(v, "Dv")
                    ENG["VE"](_)
                    seal_t = sch.tok("Dv")

                    def _(s, tq=tq, s_i=s_i, seal_t=seal_t):
                        sch.wait(s, "SY", "Dv", seal_t)
                        i = s.dma_start(
                            out=xout[s_i][4 * tq * 128:(4 * tq + 4) * 128]
                                .rearrange("(c p) d -> p c d", p=128),
                            in_=x_sb[:, 4 * tq * 1024:(4 * tq + 4) * 1024]
                                .rearrange("p (c d) -> p c d", c=4))
                        sch.inc(i, "So", 16)
                        if s_i < nseq - 1:
                            sch.wait(s, "SY", "So", sch.cnt["So"])
                            i = s.dma_start(
                                out=x_sb[:, 4 * tq * 1024:(4 * tq + 4) * 1024]
                                    .rearrange("p (c d) -> p c d", c=4),
                                in_=xin[s_i + 1][4 * tq * 128:(4 * tq + 4) * 128]
                                    .rearrange("(c p) d -> p c d", p=128))
                            sch.inc(i, "Cx", 16)
                    ENG["SY"](_)
                    if s_i < nseq - 1:
                        cx_toks[tq] = sch.tok("Cx")
                        R["x"].set_w("Cx", cx_toks[tq])

                if prv == NT - 1 and s_i < nseq - 1:
                    def _(sc, cx_toks=cx_toks):
                        i = None
                        for tq in range(NG):
                            sch.wait(sc, "SE", "Cx", cx_toks[tq])
                            for ns in range(4):
                                c = tq * 4 + ns
                                xc = x_sb[:, c * 1024:(c + 1) * 1024]
                                sc.activation(sqscr[:], xc[:, 0:512], AF.Square,
                                              accum_out=sqs[:, c:c + 1])
                                i = sc.activation(sqscr[:], xc[:, 512:1024], AF.Square,
                                                  accum_out=msq[:, c:c + 1])
                        sch.inc(i, "Ac")
                    ENG["SE"](_)
                    R["x"].add_r("Ac", sch.tok("Ac"))

                    def _(v, cx_toks=cx_toks):
                        i = None
                        for tq in range(NG):
                            sch.wait(v, "VE", "Cx", cx_toks[tq])
                            for ns in range(4):
                                c = tq * 4 + ns
                                i = v.reduce_sum(sums[:, c:c + 1],
                                                 x_sb[:, c * 1024:(c + 1) * 1024],
                                                 axis=AX.X)
                        sch.inc(i, "Dv")
                    ENG["VE"](_)
                    R["x"].add_r("Dv", sch.tok("Dv"))

                if cur is not None:
                    def _(p, cur=cur, par=par, f_ac0=f_ac0, p_a0d=p_a0d):
                        sch.wait(p, "GE", "Ac", f_ac0 + 2)
                        sch.wait(p, "GE", "Pe", state["actR"][par])
                        for c in range(ICH):
                            sch.wait(p, "GE", "Dv", p_a0d(c))
                            i = p.tensor_tensor(actb[par][:, c * 512:(c + 1) * 512],
                                                a0[:], wbr[par][:], OP.mult)
                            sch.inc(i, "Pl")
                        i = p.tensor_copy(dscr[:, 48:64], dscr[:, 0:16])
                        sch.inc(i, "Pl")
                    ENG["GE"](_)
                    assert sch.tok("Pl") == f_pl0 + 9

                if cur is not None and t_c == NG - 1 and e_c < NE - 1:
                    def _(s, e_c=e_c, f_pe0=f_pe0):
                        sch.wait(s, "GE", "Pe", f_pe0 + 9)
                        i = s.dma_start(out=w1b[:, 0:8192], in_=d_w1[e_c + 1][:, 0:8192])
                        sch.inc(i, "Cw1", 16)
                        sch.wait(s, "GE", "Pe", f_pe0 + 17)
                        i = s.dma_start(out=w1b[:, 8192:16384],
                                        in_=d_w1[e_c + 1][:, 8192:16384])
                        sch.inc(i, "Cw1", 16)
                    ENG["GE"](_)
                    w1tok[e_c + 1] = (sch.tok("Cw1") - 16, sch.tok("Cw1"))
                if cur is not None and t_c == 0 and e_c >= 1:
                    def _(s, e_c=e_c, f_pe0=f_pe0):
                        sch.wait(s, "GE", "Pe", f_pe0 + 25)
                        i = s.dma_start(out=w2b[:], in_=d_w2[e_c])
                        sch.inc(i, "Cw2", 16)
                    ENG["GE"](_)
                    w2tok[e_c] = sch.tok("Cw2")
                if cur is not None:
                    state["t1war"] = [("Ac", p_eg(6)), ("Ac", p_eg(7)),
                                      ("Ac", p_el(6)), ("Ac", p_el(7))]
                    state["pb6war"] = ("Ac", f_ac0 + 1)
                    state["wbrR"][par] = f_pl0 + 9
                    prev = {"lcb": (p_lcb(6), p_lcb(7)), "a07": p_a0(7)}
                    if cur == NT - 1:
                        state["w1_last"] = f_pe0 + 17
                if prv is not None:
                    state["m2war"] = [xacc_base + 7, xacc_base + 8]
                    state["actR"][prv % 2] = m2_new[-1]
                    if prv == NT - 1:
                        state["w2_last"] = m2_new[-1]
                cut(f"stage_{n}")
            def _(v):
                sch.drain_inc(v, "Dv")
            ENG["VE"](_)
            ffn_t = sch.tok("Dv")
            cut("full")
          except _CutDone:
            pass
          if True:
            if DBG_STOP:
                def _(v):
                    i = None
                    for ix, tt in enumerate((sums, sqs, msq, nm, nv, vpe, lnv, rstd,
                                             rsq, vv, mt2, lnw, alphav, cc1, cc2)):
                        i = v.tensor_copy(q2[:, 0:2048].bitcast(F32)[:, ix * 16:(ix + 1) * 16], tt[:])
                    q2f = q2[:, 0:2048].bitcast(F32)
                    v.tensor_copy(q2f[:, 256:384], lgtm[:])
                    v.tensor_copy(q2f[:, 384:512], rwf[:])
                    v.tensor_copy(q2f[:, 512:528], rm1[:])
                    v.tensor_copy(q2f[:, 528:544], rm2[:])
                    v.tensor_copy(q2f[:, 544:560], rw1[:])
                    i = v.tensor_copy(q2f[:, 560:688], req1[:])
                    return i
                sec("VE", _, reads=("stats",), writes=("q2",))
            ffn_t = seal("VE")
            se_t = seal("SE")
            pl_t = seal("GE")

            def _(s):
                sch.wait(s, "SY", "Dv", ffn_t)
                sch.wait(s, "SY", "Ac", se_t)
                sch.wait(s, "SY", "Pl", pl_t)
                sch.wait(s, "SY", "Pe", sch.tok("Pe"))
                for cn in ("Cq", "Cz1", "Cz2", "Cw1", "Cw2"):
                    sch.wait(s, "SY", cn, sch.tok(cn))
                i = s.dma_start(out=d_dbg[:, :],
                                in_=arena[:, 0:57600].bitcast(F32)[:, 0:28800])
                sch.inc(i, "So", 16)
            if DBG_STOP and DBG_DUMP:
                ENG["SY"](_)

          if True:
            state["store"] = sch.tok("So")

        def _(g_):
            g_.wait_ge(sch.sem["So"], sch.cnt["So"])
        ENG["GE"](_)

    return nc


_NC_CACHE = None


def kernel(**inputs):
    global _NC_CACHE
    maps = _host_prep(inputs)
    if _NC_CACHE is None:
        _NC_CACHE = build()
    res = run_bass_kernel_spmd(_NC_CACHE, maps, core_ids=list(range(CORES)))
    out = np.empty((B, S, D), np.float32)
    for c in range(CORES):
        out[c * SPC:(c + 1) * SPC] = res.results[c]["xout"]
    return out


if __name__ == "__main__":
    import time
    t0 = time.time()
    nc = build()
    n = sum(1 for _ in nc.all_instructions())
    print("build s:", round(time.time() - t0, 1), "instrs:", n)
